# version 04
# baseline (speedup 1.0000x reference)
import numpy as np
import ml_dtypes
import concourse.bass as bass
import concourse.mybir as mybir
from concourse.bass_utils import run_bass_kernel_spmd
from concourse.aot_env import allow_ici_all_to_all

F32 = mybir.dt.float32
BF16 = mybir.dt.bfloat16
F8 = mybir.dt.float8e4
AF = mybir.ActivationFunctionType
ALU = mybir.AluOpType
BF = ml_dtypes.bfloat16

D = 2048
DH = 128
CH = 16
GT = 128
NCORE = 8
EPS_A = 1e-6
EPS_M = 1e-5

ENGS = ("sync", "gpsimd", "vector", "scalar", "tensor")


COMPUTE_ENGS = ("vector", "scalar", "tensor")


class Sched:

    def __init__(self):
        self.ops = {e: [] for e in ENGS}
        self.cnt = {e: 0 for e in ENGS}
        self.step = {"sync": 16, "gpsimd": 16, "vector": 1, "scalar": 1,
                     "tensor": 1}

    def op(self, eng, fn, inc=False, waits=()):
        self.ops[eng].append((fn, bool(inc), tuple(waits)))
        if eng in COMPUTE_ENGS or inc:
            self.cnt[eng] += self.step[eng]
        return self.cnt[eng]

    def now(self, eng):
        return self.cnt[eng]

    def barrier(self, eng, others):
        w = tuple((o, self.cnt[o]) for o in others if self.cnt[o] > 0)
        if w:
            self.ops[eng].append((None, False, w))

    def emit(self, block, sems):
        waited = {e: set() for e in ENGS}
        for e in list(self.ops):
            for fn, inc, waits in self.ops[e]:
                for we, wv in waits:
                    if we != e and wv > 0 and we in waited:
                        waited[we].add(wv)

        class _TProxy:
            def __init__(self, h):
                self._h = h

            def __getattr__(self, k):
                return getattr(self._h, k)

            def matmul(self, *a, **kw):
                kw.setdefault("skip_group_check", True)
                return self._h.matmul(*a, **kw)

        def mk(eng):
            selfsync = eng in COMPUTE_ENGS
            miles = sorted(waited[eng]) if not selfsync else []

            def body(h):
                n = 0
                enforced = 0
                mi = 0
                for fn, inc, waits in self.ops[eng]:
                    for we, wv in waits:
                        if wv > 0:
                            h.wait_ge(sems[we], wv)
                    if fn is not None:
                        if eng == "tensor":
                            h2 = _TProxy(h)
                        else:
                            h2 = h
                        if selfsync:
                            if n > 0:
                                h.wait_ge(sems[eng], n)
                        elif inc:
                            tgt = 0
                            while mi < len(miles) and miles[mi] <= n:
                                tgt = miles[mi]
                                mi += 1
                            if tgt > enforced:
                                h.wait_ge(sems[eng], tgt)
                                enforced = tgt
                        ins = fn(h2)
                        if selfsync:
                            n += 1
                            ins.then_inc(sems[eng], 1)
                        elif inc:
                            ins.then_inc(sems[eng], self.step[eng])
                            n += self.step[eng]
            return body
        block.sync(mk("sync"))
        block.gpsimd(mk("gpsimd"))
        block.vector(mk("vector"))
        block.scalar(mk("scalar"))
        block.tensor(mk("tensor"))


def build_program(T):
    NG = T // GT
    Tc = T // NCORE
    NLG = Tc // GT
    NCH = GT // CH
    TS = min(256, Tc)
    NSL = Tc // TS
    NDT = min(512, Tc)
    ND = Tc // NDT
    assert T % (GT * NCORE) == 0

    nc = bass.Bass("TRN2", target_bir_lowering=False)
    s = Sched()

    def din(name, shape, dt):
        return nc.dram_tensor(name, shape, dt, kind="ExternalInput").ap()

    xT = din("xT", [128, 16, T], F32)
    wf = din("wf", [128, 16 * 8 * 128], F32)
    wo = din("wo", [16, 128, 16 * 128], F32)
    wg2 = din("wg2", [128, 256], F32)
    lnw_c = din("lnw_c", [128, 2], F32)
    lnb_c = din("lnb_c", [128, 2], F32)
    nw_c = din("nw_c", [128, 2], F32)
    gwt = din("gwt", [48, 128, 16 * 128], F32)
    uwt = din("uwt", [48, 128, 16 * 128], F32)
    dwt = din("dwt", [16, 128, 48 * 128], F32)
    gsc = din("gsc", [128, 48], F32)
    usc = din("usc", [128, 48], F32)
    dsc = din("dsc", [128, 16], F32)
    idn = din("idn", [128, 128], BF16)
    mkKL = din("mkKL", [64, 128], F32)
    mkKH = din("mkKH", [64, 128], F32)
    mkGL = din("mkGL", [64, 128], F32)
    mkGH = din("mkGH", [64, 128], F32)
    onesc = din("onesc", [128, 1], F32)
    onesf = din("onesf", [128, 128], F32)
    onesrow = din("onesrow", [1, 128], F32)
    xres = din("xres", [128, 16, Tc], F32)

    out = nc.dram_tensor("out", [Tc, D], F32, kind="ExternalOutput").ap()
    o_stage = nc.dram_tensor("o_stage", [NCORE * NLG * 2 * 128, GT], BF16)
    o_gath = nc.dram_tensor("o_gath", [NCORE * NLG * 2 * 128, GT], BF16)
    x1_dram = nc.dram_tensor("x1_dram", [128, 16, Tc], BF16)
    o_st5 = o_stage.ap().rearrange("(a l hh p) c -> a l hh p c",
                                   a=NCORE, l=NLG, hh=2, p=128)
    o_gt5 = o_gath.ap().rearrange("(a l hh p) c -> a l hh p c",
                                  a=NCORE, l=NLG, hh=2, p=128)

    import contextlib
    es = contextlib.ExitStack()
    with es:
        def sb(name, shape, dt):
            return es.enter_context(nc.sbuf_tensor(name, shape, dt))

        pbank = [es.enter_context(nc.psum_tensor("pb%d" % i, [128, 512], F32))
                 for i in range(8)]
        pqA, pqB, pA, pB, pS, pSC, pQM, pSM = pbank
        pSMb = pSM.ap().bitcast(BF16)

        sems = {e: es.enter_context(nc.semaphore("sem_" + e)) for e in ENGS}
        sems["cc"] = es.enter_context(nc.semaphore("sem_cc"))
        block = es.enter_context(nc.Block())

        c_id = sb("c_id", [128, 128], BF16)
        c_idf = sb("c_idf", [128, 128], F32)
        c_mkKL = sb("c_mkKL", [64, 128], F32)
        c_mkKH = sb("c_mkKH", [64, 128], F32)
        c_mkGL = sb("c_mkGL", [64, 128], F32)
        c_mkGH = sb("c_mkGH", [64, 128], F32)
        c_ones = sb("c_ones", [128, 128], F32)
        c_onesc = sb("c_onesc", [128, 1], F32)
        c_onesb = sb("c_onesb", [128, 1], BF16)
        c_onesrow = sb("c_onesrow", [1, 128], F32)
        c_lnw = sb("c_lnw", [128, 2], F32)
        c_lnb = sb("c_lnb", [128, 2], F32)
        c_nw = sb("c_nw", [128, 2], F32)
        c_w2 = sb("c_w2", [128, 2], F32)
        c_wb = sb("c_wb", [128, 2], F32)
        c_negw = sb("c_negw", [128, 2], F32)
        c_w2B = sb("c_w2B", [128, 32], F32)
        c_epsA = sb("c_epsA", [128, 1], F32)
        c_epsM = sb("c_epsM", [128, 1], F32)

        for dst, src in [(c_id, idn), (c_mkKL, mkKL), (c_mkKH, mkKH),
                         (c_mkGL, mkGL), (c_mkGH, mkGH), (c_ones, onesf),
                         (c_onesc, onesc), (c_onesrow, onesrow),
                         (c_lnw, lnw_c), (c_lnb, lnb_c), (c_nw, nw_c)]:
            s.op("sync", lambda h, d=dst, sr=src: h.dma_start(out=d[:], in_=sr),
                 inc=True)
        ld0 = s.now("sync")

        scan_done = {}
        qkv_evac_done = {}
        qkv_pe_done = {}
        for g in range(-4, 0):
            scan_done[g] = 0
            qkv_evac_done[g] = 0
            qkv_pe_done[g] = 0

        esA = contextlib.ExitStack()
        with esA:
            def sba(name, shape, dt):
                return esA.enter_context(nc.sbuf_tensor(name, shape, dt))

            wfb = sba("wfb", [128, 16 * 8 * 128], BF16)
            wg2b = sba("wg2b", [128, 256], BF16)
            csWb = sba("csWb", [1, 8 * 128], BF16)
            W = 2
            xb = sba("xb", [128, W * 2048], BF16)
            sqb = sba("sqb", [128, W * 2048], BF16)
            qs_f = sba("qs_f", [128, W * 256], F32)
            qs_b = sba("qs_b", [128, W * 256], BF16)
            kn_f = sba("kn_f", [128, W * 256], F32)
            kn_b = sba("kn_b", [128, W * 256], BF16)
            v_f = sba("v_f", [128, W * 256], F32)
            g1_b = sba("g1_b", [128, W * 128], BF16)
            eT = sba("eT", [128, W * 4], F32)
            rowbuf = sba("rowbuf", [128, W * 128], F32)
            s_op = sba("s_op", [128, 256], BF16)
            vnstkL = sba("vnstkL", [64, 256], BF16)
            vnstkH = sba("vnstkH", [64, 256], BF16)
            aktL = sba("aktL", [64, 256], BF16)
            aktH = sba("aktH", [64, 256], BF16)
            agtL = sba("agtL", [64, 256], BF16)
            agtH = sba("agtH", [64, 256], BF16)
            vnstg = sba("vnstg", [128, 64], BF16)
            wbm = sba("wbm", [128, 256], F32)
            ketL = sba("ketL", [64, 256], BF16)
            ketH = sba("ketH", [64, 256], BF16)
            kescr = sba("kescr", [128, 128], BF16)
            scrA = sba("scrA", [128, 128], F32)
            scr1 = sba("scr1", [128, 32], F32)
            scr2 = sba("scr2", [128, 32], F32)
            rstdc = sba("rstdc", [128, 32], F32)
            khh = sba("khh", [128, 32], F32)
            strow = sba("strow", [1, 6 * 128], F32)
            strowb = sba("strowb", [1, 128], BF16)
            rstdB = sba("rstdB", [128, 128], F32)
            qscr1 = sba("qscr1", [128, 128], F32)
            qscr2 = sba("qscr2", [128, 128], F32)
            oscr = sba("oscr", [128, 256], F32)
            oth = sba("oth", [128, 128], F32)
            osig = sba("osig", [128, 128], F32)
            oo = sba("oo", [128, 128], F32)
            oo2 = sba("oo2", [128, 128], F32)
            obuf = sba("obuf", [128, 256], BF16)

            s.op("gpsimd", lambda h: h.dma_start(out=wfb[:], in_=wf), inc=True)
            s.op("gpsimd", lambda h: h.dma_start(out=wg2b[:], in_=wg2), inc=True)
            ld1 = s.now("gpsimd")

            s.op("vector", lambda h: h.tensor_tensor(
                out=c_w2[:], in0=c_lnw[:], in1=c_lnw[:], op=ALU.mult),
                waits=[("sync", ld0)])
            s.op("vector", lambda h: h.tensor_tensor(
                out=c_wb[:], in0=c_lnw[:], in1=c_lnb[:], op=ALU.mult))
            s.op("vector", lambda h: h.tensor_scalar(
                out=c_negw[:], in0=c_lnw[:], scalar1=-1.0, scalar2=None,
                op0=ALU.mult))
            s.op("vector", lambda h: h.tensor_copy(out=c_idf[:], in_=c_id[:]))
            s.op("vector", lambda h: h.tensor_copy(out=c_onesb[:], in_=c_onesc[:]))
            for hh in range(2):
                s.op("vector", lambda h, hh=hh: h.tensor_scalar(
                    out=c_w2B[:, hh * 16:hh * 16 + 16], in0=c_ones[:, 0:16],
                    scalar1=c_w2[:, hh:hh + 1], scalar2=None, op0=ALU.mult))
            s.op("vector", lambda h: h.memset(s_op[:], 0.0))
            s.op("vector", lambda h: h.memset(rowbuf[:], 0.0))
            s.op("vector", lambda h: h.memset(c_epsA[:], EPS_A))
            s.op("vector", lambda h: h.memset(c_epsM[:], EPS_M), inc=True)
            pre0 = s.now("vector")

            for t in range(8):
                for d in range(16):
                    wv = [("gpsimd", ld1), ("vector", pre0)] \
                        if (t == 0 and d == 0) else ()
                    s.op("tensor", lambda h, t=t, d=d: h.matmul(
                        pQM[0:1, 0:128], c_onesb[:],
                        wfb[:, (d * 8 + t) * 128:(d * 8 + t) * 128 + 128],
                        start=(d == 0), stop=(d == 15)), waits=wv)
                pe_cs = s.op("tensor", lambda h: h.nop(), inc=True)
                s.op("vector", lambda h, t=t: h.tensor_scalar(
                    out=csWb[0:1, t * 128:t * 128 + 128], in0=pQM[0:1, 0:128],
                    scalar1=-1.0, scalar2=None, op0=ALU.mult),
                    inc=True, waits=[("tensor", pe_cs)])
                s.barrier("tensor", ["vector"])

            def qkv_group(g):
                sg = g % W
                xo = sg * 2048
                ao = sg * 256
                wv = [("tensor", qkv_pe_done[g - W])] if g >= W else ()
                s.op("gpsimd", lambda h, g=g, xo=xo: h.dma_start(
                    out=xb[:, xo:xo + 2048].rearrange("p (d c) -> p d c", d=16),
                    in_=xT[:, :, g * GT:(g + 1) * GT]), inc=True, waits=wv)
                ldx = s.now("gpsimd")
                s.op("scalar", lambda h, xo=xo: h.activation(
                    out=sqb[:, xo:xo + 2048], in_=xb[:, xo:xo + 2048],
                    func=AF.Square), inc=True, waits=[("gpsimd", ldx)])
                sqd = s.now("scalar")

                for t in range(8):
                    pq = pqA if t < 4 else pqB
                    for d in range(16):
                        wv = [("gpsimd", ldx), ("vector", qkv_evac_done[g - 1]),
                              ("vector", scan_done[g - W])] \
                            if (t == 0 and d == 0) else ()
                        s.op("tensor", lambda h, t=t, d=d, pq=pq, xo=xo: h.matmul(
                            pq[:, (t % 4) * 128:(t % 4) * 128 + 128],
                            wfb[:, (d * 8 + t) * 128:(d * 8 + t) * 128 + 128],
                            xb[:, xo + d * 128: xo + d * 128 + 128],
                            start=(t % 4 == 0 and d == 0), stop=(d == 15)),
                            waits=wv)
                for d in range(16):
                    s.op("tensor", lambda h, d=d, xo=xo: h.matmul(
                        pQM[0:1, 0:128], c_onesb[:],
                        sqb[:, xo + d * 128: xo + d * 128 + 128],
                        start=(d == 0), stop=(d == 15)),
                        waits=[("scalar", sqd)] if d == 0 else ())
                pe_mm = s.op("tensor", lambda h: h.nop(), inc=True)

                ext = pqB[:, 384:512]
                s.op("vector", lambda h, ext=ext: h.tensor_copy(
                    out=strow[0:1, 640:768], in_=ext[0:1, 0:128]),
                    waits=[("tensor", pe_mm)])
                s.op("vector", lambda h: h.tensor_tensor(
                    out=strow[0:1, 128:256], in0=strow[0:1, 640:768],
                    in1=strow[0:1, 640:768], op=ALU.mult))
                s.op("vector", lambda h: h.scalar_tensor_tensor(
                    out=strow[0:1, 128:256], in0=strow[0:1, 128:256],
                    scalar=-1.0 / D, in1=pQM[0:1, 0:128], op0=ALU.mult,
                    op1=ALU.add))
                s.op("vector", lambda h, ext=ext: h.tensor_scalar(
                    out=strowb[0:1, :], in0=ext[0:1, 0:128], scalar1=1.0 / D,
                    scalar2=None, op0=ALU.mult), inc=True)
                dve_r = s.now("vector")
                s.op("scalar", lambda h: h.activation(
                    out=strow[0:1, 256:384], in_=strow[0:1, 128:256], func=AF.Ln,
                    scale=1.0 / D, bias=c_epsM[0:1, :]), waits=[("vector", dve_r)])
                s.op("scalar", lambda h: h.activation(
                    out=strow[0:1, 0:128], in_=strow[0:1, 256:384], func=AF.Exp,
                    scale=-0.5), inc=True)
                act_r = s.now("scalar")

                for t in range(8):
                    pq = pqA if t < 4 else pqB
                    s.op("tensor", lambda h, t=t, pq=pq: h.matmul(
                        pq[:, (t % 4) * 128:(t % 4) * 128 + 128],
                        csWb[0:1, t * 128:t * 128 + 128], strowb[0:1, :],
                        start=False, stop=True),
                        waits=[("scalar", act_r)] if t == 0 else ())
                pe_fix = s.op("tensor", lambda h: h.matmul(
                    pQM[:, 128:256], c_onesrow[0:1, :], strow[0:1, 0:128],
                    start=True, stop=True), inc=True)
                qkv_pe_done[g] = pe_fix
                s.op("vector", lambda h: h.tensor_copy(
                    out=rstdB[:], in_=pQM[:, 128:256]), inc=True,
                    waits=[("tensor", pe_fix), ("vector", scan_done[g - W])])
                s.barrier("tensor", ["vector"])

                for hh in range(2):
                    tq = pqA[:, hh * 128:hh * 128 + 128]
                    qf = qs_f[:, ao + hh * 128: ao + hh * 128 + 128]
                    q_ev = s.op("vector", lambda h, tq=tq, qf=qf: h.tensor_tensor(
                        out=qf, in0=tq, in1=rstdB[:], op=ALU.mult), inc=True)
                    a_t = s.op("scalar", lambda h, qf=qf: h.activation(
                        out=oth[:], in_=qf, func=AF.Tanh, scale=0.5),
                        inc=True, waits=[("vector", q_ev)])
                    s.op("vector", lambda h, qf=qf: h.tensor_scalar(
                        out=qscr1[:], in0=qf, scalar1=0.5, scalar2=None,
                        op0=ALU.mult), waits=[("scalar", a_t)])
                    s.op("vector", lambda h: h.tensor_tensor(
                        out=qscr2[:], in0=qscr1[:], in1=oth[:], op=ALU.mult))
                    s.op("vector", lambda h, qf=qf: h.tensor_tensor(
                        out=qf, in0=qscr1[:], in1=qscr2[:], op=ALU.add))
                    s.op("vector", lambda h, qf=qf, hh=hh, ao=ao: h.tensor_copy(
                        out=qs_b[:, ao + hh * 128: ao + hh * 128 + 128], in_=qf),
                        inc=True)
                    s.barrier("scalar", ["vector"])
                for hh in range(2):
                    tk = pqA[:, (2 + hh) * 128:(2 + hh) * 128 + 128]
                    kf = kn_f[:, ao + hh * 128: ao + hh * 128 + 128]
                    k_ev = s.op("vector", lambda h, tk=tk, kf=kf: h.tensor_tensor(
                        out=kf, in0=tk, in1=rstdB[:], op=ALU.mult), inc=True)
                    k_sq = s.op("scalar", lambda h, kf=kf: h.activation(
                        out=qscr1[:], in_=kf, func=AF.Square),
                        inc=True, waits=[("vector", k_ev)])
                    k_mm = s.op("tensor", lambda h: h.matmul(
                        pQM[0:1, 0:128], c_onesc[:], qscr1[:],
                        start=True, stop=True), inc=True, waits=[("scalar", k_sq)])
                    s.op("scalar", lambda h: h.activation(
                        out=strow[0:1, 384:512], in_=pQM[0:1, 0:128], func=AF.Ln,
                        scale=1.0, bias=c_epsA[0:1, :]), waits=[("tensor", k_mm)])
                    k_rn = s.op("scalar", lambda h: h.activation(
                        out=strow[0:1, 384:512], in_=strow[0:1, 384:512],
                        func=AF.Exp, scale=-0.5), inc=True)
                    k_bc = s.op("tensor", lambda h: h.matmul(
                        pQM[:, 128:256], c_onesrow[0:1, :], strow[0:1, 384:512],
                        start=True, stop=True), inc=True, waits=[("scalar", k_rn)])
                    s.op("vector", lambda h, kf=kf: h.tensor_tensor(
                        out=kf, in0=kf, in1=pQM[:, 128:256], op=ALU.mult),
                        waits=[("tensor", k_bc)])
                    s.op("vector", lambda h, kf=kf, hh=hh, ao=ao: h.tensor_copy(
                        out=kn_b[:, ao + hh * 128: ao + hh * 128 + 128], in_=kf),
                        inc=True)
                    s.barrier("tensor", ["vector"])
                    s.barrier("scalar", ["vector"])
                for hh in range(2):
                    tv = pqB[:, hh * 128:hh * 128 + 128]
                    s.op("vector", lambda h, tv=tv, hh=hh, ao=ao: h.tensor_tensor(
                        out=v_f[:, ao + hh * 128: ao + hh * 128 + 128],
                        in0=tv, in1=rstdB[:], op=ALU.mult))
                s.op("vector", lambda h: h.tensor_tensor(
                    out=qscr2[:], in0=pqB[:, 256:384], in1=rstdB[:], op=ALU.mult))
                s.op("vector", lambda h, sg=sg: h.tensor_copy(
                    out=g1_b[:, sg * 128:sg * 128 + 128], in_=qscr2[:]))
                s.op("vector", lambda h, sg=sg, ext=ext: h.tensor_tensor(
                    out=rowbuf[32:33, sg * 128:sg * 128 + 128], in0=ext[32:33, 0:128],
                    in1=rstdB[32:33, 0:128], op=ALU.mult))
                v_done = s.op("vector", lambda h, sg=sg, ext=ext: h.tensor_tensor(
                    out=rowbuf[64:65, sg * 128:sg * 128 + 128],
                    in0=ext[64:65, 0:128], in1=rstdB[64:65, 0:128], op=ALU.mult),
                    inc=True)
                for i, base in enumerate((32, 64)):
                    kw = [("vector", v_done)] if i == 0 else ()
                    ii = (i == 1)
                    s.op("scalar", lambda h, base=base, sg=sg: h.activation(
                        out=rowbuf[base:base + 1, sg * 128:sg * 128 + 128],
                        in_=rowbuf[base:base + 1, sg * 128:sg * 128 + 128],
                        func=AF.Tanh, scale=0.5), inc=ii, waits=kw)
                e_th = s.now("scalar")
                for i, base in enumerate((32, 64)):
                    kw = [("scalar", e_th)] if i == 0 else ()
                    s.op("vector", lambda h, base=base, sg=sg: h.tensor_scalar(
                        out=rowbuf[base:base + 1, sg * 128:sg * 128 + 128],
                        in0=rowbuf[base:base + 1, sg * 128:sg * 128 + 128],
                        scalar1=1.0, scalar2=-1.0, op0=ALU.add, op1=ALU.mult),
                        inc=(i == 1), waits=kw)
                e_neg = s.now("vector")
                i = 0
                for hhb, base in enumerate((32, 64)):
                    for half in range(2):
                        kw = [("vector", e_neg)] if i == 0 else ()
                        s.op("tensor", lambda h, base=base, sg=sg, i=i, half=half:
                             h.matmul(
                                 pQM[0:64, 256 + i:257 + i],
                                 rowbuf[base:base + 1,
                                        sg * 128 + half * 64:
                                        sg * 128 + half * 64 + 64],
                                 c_ones[base:base + 1, 0:1],
                                 start=True, stop=True),
                             inc=(i == 3), waits=kw)
                        i += 1
                e_tr = s.now("tensor")
                s.op("vector", lambda h, sg=sg: h.tensor_copy(
                    out=eT[0:64, sg * 4: sg * 4 + 4], in_=pQM[0:64, 256:260]),
                    inc=True, waits=[("tensor", e_tr)])
                s.barrier("tensor", ["vector"])
                qkv_evac_done[g] = s.now("vector")

            def scan_group(g):
                sg = g % W
                ao = sg * 256
                qev = qkv_evac_done[g]
                guard = qkv_evac_done.get(g + 1, qev)

                if g >= 1:
                    for hh in range(2):
                        s.op("vector", lambda h, hh=hh: h.tensor_copy(
                            out=s_op[:, hh * 128:hh * 128 + 128],
                            in_=pS[:, hh * 128:hh * 128 + 128]),
                            inc=(hh == 1),
                            waits=[("vector", qev)] if hh == 0 else ())

                for hh in range(2):
                    kb = kn_b[:, ao + hh * 128: ao + hh * 128 + 128]
                    qb = qs_b[:, ao + hh * 128: ao + hh * 128 + 128]
                    for half, (aktX, agtX, mKX, mGX) in enumerate(
                            ((aktL, agtL, c_mkKL, c_mkGL),
                             (aktH, agtH, c_mkKH, c_mkGH))):
                        ec = eT[0:64, sg * 4 + hh * 2 + half:
                                sg * 4 + hh * 2 + half + 1]
                        mm1 = s.op("tensor", lambda h, kb=kb, half=half: h.matmul(
                            pSC[0:64, 0:128], kb[:, half * 64:half * 64 + 64],
                            kb, start=True, stop=True), inc=True,
                            waits=[("vector", qev)] if (hh == 0 and half == 0)
                            else ())
                        dv1 = s.op("vector", lambda h, hh=hh, ec=ec, aktX=aktX,
                                   mKX=mKX: h.scalar_tensor_tensor(
                                       out=aktX[:, hh * 128:hh * 128 + 128],
                                       in0=pSC[0:64, 0:128], scalar=ec,
                                       in1=mKX[:], op0=ALU.mult, op1=ALU.mult),
                                   inc=True, waits=[("tensor", mm1)])
                        mm2 = s.op("tensor", lambda h, kb=kb, qb=qb, half=half:
                                   h.matmul(
                                       pSC[0:64, 0:128],
                                       kb[:, half * 64:half * 64 + 64], qb,
                                       start=True, stop=True), inc=True,
                                   waits=[("vector", dv1)])
                        s.op("vector", lambda h, hh=hh, ec=ec, agtX=agtX,
                             mGX=mGX: h.scalar_tensor_tensor(
                                 out=agtX[:, hh * 128:hh * 128 + 128],
                                 in0=pSC[0:64, 0:128], scalar=ec, in1=mGX[:],
                                 op0=ALU.mult, op1=ALU.mult),
                             inc=True, waits=[("tensor", mm2)])
                        s.barrier("tensor", ["vector"])

                for hh in range(2):
                    s.op("vector", lambda h, hh=hh, ao=ao: h.tensor_tensor(
                        out=oscr[:, 0:128],
                        in0=v_f[:, ao + hh * 128:ao + hh * 128 + 128],
                        in1=kn_f[:, ao + hh * 128:ao + hh * 128 + 128],
                        op=ALU.subtract))
                    s.op("vector", lambda h, hh=hh: h.tensor_scalar(
                        out=wbm[:, hh * 128:hh * 128 + 128], in0=oscr[:, 0:128],
                        scalar1=c_negw[:, hh:hh + 1], scalar2=c_wb[:, hh:hh + 1],
                        op0=ALU.mult, op1=ALU.add), inc=True)
                    base = 32 + hh * 32
                    ebc = s.op("tensor", lambda h, sg=sg, base=base: h.matmul(
                        pQM[:, 128:256], c_ones[base:base + 1, 0:128],
                        rowbuf[base:base + 1, sg * 128:sg * 128 + 128],
                        start=True, stop=True), inc=True,
                        waits=[("vector", guard)] if hh == 0 else ())
                    kes = s.op("vector", lambda h, hh=hh, ao=ao: h.tensor_tensor(
                        out=kescr[:],
                        in0=kn_f[:, ao + hh * 128:ao + hh * 128 + 128],
                        in1=pQM[:, 128:256], op=ALU.mult), inc=True,
                        waits=[("tensor", ebc)])
                    ktr = 0
                    for half in range(2):
                        ktr = s.op("tensor", lambda h, half=half: h.transpose(
                            pSMb[0:64, half * 128:half * 128 + 128],
                            kescr[:, half * 64:half * 64 + 64], c_id[:]),
                            inc=True,
                            waits=[("vector", kes)] if half == 0 else ())
                    for half, ketX in enumerate((ketL, ketH)):
                        s.op("vector", lambda h, hh=hh, half=half, ketX=ketX:
                             h.tensor_copy(
                                 out=ketX[:, hh * 128:hh * 128 + 128],
                                 in_=pSMb[0:64, half * 128:half * 128 + 128]),
                             inc=(half == 1),
                             waits=[("tensor", ktr)] if half == 0 else ())
                    s.barrier("tensor", ["vector"])
                prep_done = s.now("vector")

                for hh in range(2):
                    s.op("tensor", lambda h, hh=hh, ao=ao: h.matmul(
                        pA[:, hh * 128:hh * 128 + 128],
                        s_op[:, hh * 128:hh * 128 + 128],
                        kn_b[:, ao + hh * 128:ao + hh * 128 + 128],
                        start=(hh == 0), stop=True),
                        waits=[("vector", prep_done)] if hh == 0 else ())
                    s.op("tensor", lambda h, hh=hh, ao=ao: h.matmul(
                        pB[:, hh * 128:hh * 128 + 128],
                        s_op[:, hh * 128:hh * 128 + 128],
                        qs_b[:, ao + hh * 128:ao + hh * 128 + 128],
                        start=(hh == 0), stop=True))
                s.op("tensor", lambda h: h.nop(), inc=True)

                last_stk = s.now("vector")
                for j in range(NCH):
                    m = j // 2
                    c0 = j * CH
                    if j > 0:
                        KL = min(16 * j, 64)
                        KH = 16 * j - KL
                        for hh in range(2):
                            s.op("tensor", lambda h, hh=hh, KL=KL, c0=c0: h.matmul(
                                pA[:, hh * 128 + c0: hh * 128 + c0 + CH],
                                vnstkL[0:KL, hh * 128:hh * 128 + 128],
                                aktL[0:KL, hh * 128 + c0:hh * 128 + c0 + CH],
                                start=False, stop=True),
                                waits=[("vector", last_stk)] if hh == 0 else ())
                            if KH > 0:
                                s.op("tensor", lambda h, hh=hh, KH=KH, c0=c0:
                                     h.matmul(
                                         pA[:, hh * 128 + c0:
                                            hh * 128 + c0 + CH],
                                         vnstkH[0:KH, hh * 128:hh * 128 + 128],
                                         aktH[0:KH,
                                              hh * 128 + c0:hh * 128 + c0 + CH],
                                         start=False, stop=True))
                    corr_pe = s.op("tensor", lambda h: h.nop(), inc=True)

                    s.op("vector", lambda h, c0=c0: h.tensor_copy(
                        out=scrA[:, 0:16], in_=pA[:, c0:c0 + CH]),
                        waits=[("tensor", corr_pe)])
                    s.op("vector", lambda h, c0=c0: h.tensor_copy(
                        out=scrA[:, 16:32], in_=pA[:, 128 + c0:128 + c0 + CH]))
                    kh_ev = s.op("vector", lambda h: h.tensor_tensor(
                        out=scrA[:, 32:64], in0=scrA[:, 0:32], in1=scrA[:, 0:32],
                        op=ALU.mult), inc=True)
                    st1 = s.op("tensor", lambda h: h.matmul(
                        pSM[:, 128:192], c_ones[:], scrA[:, 0:64],
                        start=True, stop=True), inc=True, waits=[("vector", kh_ev)])
                    s.op("vector", lambda h: h.tensor_copy(
                        out=scr2[:], in_=pSM[:, 128:160]), waits=[("tensor", st1)])
                    s.op("vector", lambda h: h.tensor_tensor(
                        out=scr1[:], in0=scr2[:], in1=scr2[:], op=ALU.mult))
                    vpre = s.op("vector", lambda h: h.scalar_tensor_tensor(
                        out=scr1[:], in0=scr1[:], scalar=-1.0 / DH,
                        in1=pSM[:, 160:192], op0=ALU.mult, op1=ALU.add), inc=True)
                    s.op("scalar", lambda h: h.activation(
                        out=scr2[:], in_=scr1[:], func=AF.Ln, scale=1.0 / DH,
                        bias=c_epsA[:]), waits=[("vector", vpre)])
                    a_rs = s.op("scalar", lambda h: h.activation(
                        out=rstdc[:], in_=scr2[:], func=AF.Exp, scale=-0.5),
                        inc=True)
                    s.op("vector", lambda h: h.scalar_tensor_tensor(
                        out=khh[:], in0=pSM[:, 128:160], scalar=-1.0 / DH,
                        in1=scrA[:, 0:32], op0=ALU.mult, op1=ALU.add),
                        waits=[("scalar", a_rs)])
                    s.op("vector", lambda h: h.tensor_tensor(
                        out=khh[:], in0=khh[:], in1=rstdc[:], op=ALU.mult))
                    s.op("vector", lambda h: h.tensor_tensor(
                        out=scrA[:, 64:96], in0=khh[:], in1=c_w2B[:], op=ALU.mult))
                    s.op("vector", lambda h, c0=c0: h.tensor_tensor(
                        out=scrA[:, 64:80], in0=scrA[:, 64:80],
                        in1=wbm[:, c0:c0 + CH], op=ALU.add))
                    s.op("vector", lambda h, c0=c0: h.tensor_tensor(
                        out=scrA[:, 80:96], in0=scrA[:, 80:96],
                        in1=wbm[:, 128 + c0:128 + c0 + CH], op=ALU.add))
                    gw_d = s.op("vector", lambda h: h.tensor_tensor(
                        out=scrA[:, 96:128], in0=scrA[:, 64:96], in1=khh[:],
                        op=ALU.mult), inc=True)
                    st2 = s.op("tensor", lambda h: h.matmul(
                        pSM[:, 192:256], c_ones[:], scrA[:, 64:128],
                        start=True, stop=True), inc=True, waits=[("vector", gw_d)])
                    s.op("vector", lambda h: h.scalar_tensor_tensor(
                        out=scr1[:], in0=khh[:], scalar=1.0 / DH,
                        in1=pSM[:, 224:256], op0=ALU.mult, op1=ALU.mult),
                        waits=[("tensor", st2)])
                    s.op("vector", lambda h: h.tensor_tensor(
                        out=scr1[:], in0=scrA[:, 64:96], in1=scr1[:],
                        op=ALU.subtract))
                    s.op("vector", lambda h: h.scalar_tensor_tensor(
                        out=scr1[:], in0=pSM[:, 192:224], scalar=-1.0 / DH,
                        in1=scr1[:], op0=ALU.mult, op1=ALU.add))
                    s.op("vector", lambda h: h.tensor_tensor(
                        out=scr2[:], in0=scr1[:], in1=rstdc[:], op=ALU.mult))
                    po = (j % 2) * 16
                    s.op("vector", lambda h, po=po: h.tensor_copy(
                        out=vnstg[:, po:po + 16], in_=scr2[:, 0:16]))
                    vn_d = s.op("vector", lambda h, po=po: h.tensor_copy(
                        out=vnstg[:, 32 + po:32 + po + 16], in_=scr2[:, 16:32]),
                        inc=True)

                    vstk = vnstkL if m < 2 else vnstkH
                    r0 = 32 * (m % 2)
                    nr = 16 if j % 2 == 0 else 32
                    src0 = 0 if j % 2 == 0 else 0
                    trs = 0
                    for hh in range(2):
                        trs = s.op("tensor", lambda h, hh=hh, r0=r0, nr=nr:
                                   h.transpose(
                                       pSMb[r0:r0 + nr,
                                            hh * 128:hh * 128 + 128],
                                       vnstg[:, hh * 32:hh * 32 + nr], c_id[:]),
                                   inc=True,
                                   waits=[("vector", vn_d)] if hh == 0 else ())
                    for hh in range(2):
                        s.op("vector", lambda h, hh=hh, r0=r0, nr=nr, vstk=vstk:
                             h.tensor_copy(
                                 out=vstk[r0:r0 + nr, hh * 128:hh * 128 + 128],
                                 in_=pSMb[r0:r0 + nr, hh * 128:hh * 128 + 128]),
                             inc=(hh == 1),
                             waits=[("tensor", trs)] if hh == 0 else ())
                    last_stk = s.now("vector")
                    s.barrier("tensor", ["vector"])

                for hh in range(2):
                    s.op("tensor", lambda h, hh=hh: h.matmul(
                        pB[:, hh * 128:hh * 128 + 128],
                        vnstkL[:, hh * 128:hh * 128 + 128],
                        agtL[:, hh * 128:hh * 128 + 128], start=False, stop=False),
                        waits=[("vector", last_stk)] if hh == 0 else ())
                    s.op("tensor", lambda h, hh=hh: h.matmul(
                        pB[:, hh * 128:hh * 128 + 128],
                        vnstkH[:, hh * 128:hh * 128 + 128],
                        agtH[:, hh * 128:hh * 128 + 128], start=False, stop=True))
                    s.op("tensor", lambda h, hh=hh, g=g: h.matmul(
                        pS[:, hh * 128:hh * 128 + 128],
                        ketL[:, hh * 128:hh * 128 + 128],
                        vnstkL[:, hh * 128:hh * 128 + 128],
                        start=(g == 0 and hh == 0), stop=False))
                    s.op("tensor", lambda h, hh=hh, g=g: h.matmul(
                        pS[:, hh * 128:hh * 128 + 128],
                        ketH[:, hh * 128:hh * 128 + 128],
                        vnstkH[:, hh * 128:hh * 128 + 128],
                        start=False, stop=True))
                qh_pe = s.op("tensor", lambda h: h.nop(), inc=True)

                for hh in range(2):
                    qf = qs_f[:, ao + hh * 128: ao + hh * 128 + 128]
                    s.op("vector", lambda h, hh=hh: h.tensor_copy(
                        out=oscr[:, 0:128], in_=pB[:, hh * 128:hh * 128 + 128]),
                        waits=[("tensor", qh_pe)] if hh == 0 else ())
                    o_ev = s.op("vector", lambda h: h.tensor_tensor(
                        out=oscr[:, 128:256], in0=oscr[:, 0:128],
                        in1=oscr[:, 0:128], op=ALU.mult), inc=True)
                    o_mm = s.op("tensor", lambda h: h.matmul(
                        pSM[:, 256:512], c_ones[:], oscr[:, 0:256],
                        start=True, stop=True), inc=True, waits=[("vector", o_ev)])
                    s.op("vector", lambda h: h.tensor_copy(
                        out=oo[:], in_=pSM[:, 256:384]), waits=[("tensor", o_mm)])
                    s.op("vector", lambda h: h.tensor_tensor(
                        out=oo[:], in0=oo[:], in1=oo[:], op=ALU.mult))
                    o_v = s.op("vector", lambda h: h.scalar_tensor_tensor(
                        out=oo[:], in0=oo[:], scalar=-1.0 / DH,
                        in1=pSM[:, 384:512], op0=ALU.mult, op1=ALU.add), inc=True)
                    s.op("scalar", lambda h: h.activation(
                        out=oo2[:], in_=oo[:], func=AF.Ln, scale=1.0 / DH,
                        bias=c_epsA[:]), waits=[("vector", o_v)])
                    o_rs = s.op("scalar", lambda h: h.activation(
                        out=oo2[:], in_=oo2[:], func=AF.Exp, scale=-0.5), inc=True)
                    s.op("vector", lambda h: h.scalar_tensor_tensor(
                        out=oo[:], in0=pSM[:, 256:384], scalar=-1.0 / DH,
                        in1=oscr[:, 0:128], op0=ALU.mult, op1=ALU.add),
                        waits=[("scalar", o_rs)])
                    s.op("vector", lambda h: h.tensor_tensor(
                        out=oo[:], in0=oo[:], in1=oo2[:], op=ALU.mult))
                    s.op("vector", lambda h, hh=hh: h.tensor_scalar(
                        out=oo[:], in0=oo[:], scalar1=c_lnw[:, hh:hh + 1],
                        scalar2=c_lnb[:, hh:hh + 1], op0=ALU.mult, op1=ALU.add))
                    o_d = s.op("vector", lambda h, qf=qf: h.tensor_tensor(
                        out=oo[:], in0=oo[:], in1=qf, op=ALU.add), inc=True)
                    g_mm = s.op("tensor", lambda h, hh=hh, sg=sg: h.matmul(
                        pQM[:, 384:512], wg2b[:, hh * 128:hh * 128 + 128],
                        g1_b[:, sg * 128:sg * 128 + 128], start=True, stop=True),
                        inc=True, waits=[("vector", o_d), ("vector", guard)])
                    g_th = s.op("scalar", lambda h: h.activation(
                        out=oth[:], in_=pQM[:, 384:512], func=AF.Tanh, scale=0.5),
                        inc=True, waits=[("tensor", g_mm)])
                    s.op("vector", lambda h: h.tensor_scalar(
                        out=osig[:], in0=oth[:], scalar1=0.5, scalar2=0.5,
                        op0=ALU.mult, op1=ALU.add), waits=[("scalar", g_th)])
                    og_d = s.op("vector", lambda h: h.tensor_tensor(
                        out=oo[:], in0=oo[:], in1=osig[:], op=ALU.mult), inc=True)
                    og_sq = s.op("scalar", lambda h: h.activation(
                        out=oo2[:], in_=oo[:], func=AF.Square), inc=True,
                        waits=[("vector", og_d)])
                    r_mm = s.op("tensor", lambda h: h.matmul(
                        pQM[0:1, 128:256], c_onesc[:], oo2[:],
                        start=True, stop=True), inc=True,
                        waits=[("scalar", og_sq)])
                    s.op("scalar", lambda h: h.activation(
                        out=strow[0:1, 512:640], in_=pQM[0:1, 128:256], func=AF.Ln,
                        scale=1.0 / DH, bias=c_epsA[0:1, :]),
                        waits=[("tensor", r_mm)])
                    r_rs = s.op("scalar", lambda h: h.activation(
                        out=strow[0:1, 512:640], in_=strow[0:1, 512:640],
                        func=AF.Exp, scale=-0.5), inc=True)
                    r_bc = s.op("tensor", lambda h: h.matmul(
                        pQM[:, 128:256], c_onesrow[0:1, :], strow[0:1, 512:640],
                        start=True, stop=True), inc=True, waits=[("scalar", r_rs)])
                    s.op("vector", lambda h: h.tensor_tensor(
                        out=oo[:], in0=oo[:], in1=pQM[:, 128:256], op=ALU.mult),
                        waits=[("tensor", r_bc)])
                    ob_d = s.op("vector", lambda h, hh=hh: h.tensor_scalar(
                        out=obuf[:, hh * 128:hh * 128 + 128], in0=oo[:],
                        scalar1=c_nw[:, hh:hh + 1], scalar2=None, op0=ALU.mult),
                        inc=True)
                    s.barrier("tensor", ["vector"])
                    s.barrier("scalar", ["vector"])
                    s.op("sync", lambda h, g=g, hh=hh: h.dma_start(
                        out=o_st5[g // NLG, g % NLG, hh, :, :],
                        in_=obuf[:, hh * 128:hh * 128 + 128]), inc=True,
                        waits=[("vector", ob_d)])
                scan_done[g] = s.now("vector")
                s.barrier("vector", ["sync"])

            for g in range(NG + 1):
                if g < NG:
                    qkv_group(g)
                if g >= 1:
                    scan_group(g - 1)

            s.barrier("gpsimd", ["sync", "vector", "scalar", "tensor"])

        import os
        if os.environ.get("KERN_NO_CC"):
            def do_cc(h):
                return h.dma_start(out=o_gath.ap(), in_=o_stage.ap()).then_inc(
                    sems["cc"], 16)
        else:
            def do_cc(h):
                with allow_ici_all_to_all():
                    return h.collective_compute(
                        "AllToAll", ALU.bypass,
                        replica_groups=[list(range(NCORE))],
                        ins=[o_stage.ap().opt()],
                        outs=[o_gath.ap().opt()]).then_inc(sems["cc"], 1)
        s.ops["gpsimd"].append((do_cc, False, ()))
        s.barrier("sync", ["gpsimd"])
        import os as _os
        _ccv = 16 if _os.environ.get("KERN_NO_CC") else 1
        s.ops["sync"].append((None, False, (("cc", _ccv),)))
        s.ops["tensor"].append((None, False, (("cc", _ccv),)))

        zb = sb("zb", [128, 256], BF16)
        zsq = sb("zsq", [128, 256], BF16)
        strow2 = sb("strow2", [1, 8 * 256], F32)
        rB1 = sb("rB1", [128, 256], F32)
        csc1 = sb("csc1", [128, 256], F32)
        csc2 = sb("csc2", [128, 256], F32)
        x1t = sb("x1t", [128, 256], BF16)
        gscb = sb("gscb", [128, 48], F32)
        uscb = sb("uscb", [128, 48], F32)
        dscb = sb("dscb", [128, 16], F32)
        hgs = sb("hgs", [128, 48], F32)
        esX = contextlib.ExitStack()
        x2b = esX.enter_context(nc.sbuf_tensor("x2b", [128, 16 * Tc], BF16, side="right"))
        fuse = sb("fuse", [128, 48 * Tc], BF16)

        s.op("sync", lambda h: h.dma_start(out=gscb[:], in_=gsc), inc=True)
        s.op("sync", lambda h: h.dma_start(out=uscb[:], in_=usc), inc=True)
        s.op("sync", lambda h: h.dma_start(out=dscb[:], in_=dsc), inc=True)
        ldsc = s.now("sync")
        hgs_d = s.op("vector", lambda h: h.tensor_scalar(
            out=hgs[:], in0=gscb[:], scalar1=0.5, scalar2=None, op0=ALU.mult),
            inc=True, waits=[("sync", ldsc)])

        def ln_stats_and_bcast(src_psA, src_psB, dd):
            s.op("vector", lambda h: h.tensor_copy(
                out=strow2[0:1, 3 * TS:4 * TS], in_=src_psA[0:1, 0:TS]), waits=dd)
            v1 = s.op("vector", lambda h: h.tensor_tensor(
                out=strow2[0:1, 0:TS], in0=strow2[0:1, 3 * TS:4 * TS],
                in1=strow2[0:1, 3 * TS:4 * TS], op=ALU.mult), inc=True)
            v2 = s.op("vector", lambda h: h.scalar_tensor_tensor(
                out=strow2[0:1, 0:TS], in0=strow2[0:1, 0:TS], scalar=-1.0 / D,
                in1=src_psB[0:1, 0:TS], op0=ALU.mult, op1=ALU.add), inc=True)
            s.op("scalar", lambda h: h.activation(
                out=strow2[0:1, TS:2 * TS], in_=strow2[0:1, 0:TS], func=AF.Ln,
                scale=1.0 / D, bias=c_epsM[0:1, :]), waits=[("vector", v2)])
            a1 = s.op("scalar", lambda h: h.activation(
                out=strow2[0:1, TS:2 * TS], in_=strow2[0:1, TS:2 * TS],
                func=AF.Exp, scale=-0.5), inc=True)
            v3 = s.op("vector", lambda h: h.tensor_scalar(
                out=strow2[0:1, 2 * TS:3 * TS], in0=src_psA[0:1, 0:TS],
                scalar1=1.0 / D, scalar2=None, op0=ALU.mult), inc=True)
            s.op("tensor", lambda h: h.matmul(
                pbank[4][:, 0:TS], c_onesrow[0:1, :], strow2[0:1, TS:2 * TS],
                start=True, stop=True), waits=[("scalar", a1), ("vector", v3)])
            b1 = s.op("tensor", lambda h: h.matmul(
                pbank[5][:, 0:TS], c_onesrow[0:1, :], strow2[0:1, 2 * TS:3 * TS],
                start=True, stop=True), inc=True)
            s.op("vector", lambda h: h.tensor_copy(
                out=rB1[:, 0:TS], in_=pbank[4][:, 0:TS]), waits=[("tensor", b1)])
            vb = s.op("vector", lambda h: h.tensor_copy(
                out=csc1[:, 0:TS], in_=pbank[5][:, 0:TS]), inc=True)
            s.barrier("tensor", ["vector"])
            return vb

        esC1 = contextlib.ExitStack()
        with esC1:
            oslab = esC1.enter_context(nc.sbuf_tensor("oslab", [128, 16 * TS], BF16))
            xinb = esC1.enter_context(nc.sbuf_tensor("xinb", [128, 16 * TS], F32))
            zslab = esC1.enter_context(nc.sbuf_tensor("zslab", [128, 16 * TS], F32))
            wslab = esC1.enter_context(nc.sbuf_tensor("wslab", [128, 2 * 2048], BF16))

            for sl in range(NSL):
                t0 = sl * TS
                nlg4 = TS // GT
                if sl > 0:
                    s.barrier("sync", ["tensor", "vector"])
                for src in range(NCORE):
                    for hh in range(2):
                        dti = src * 2 + hh
                        s.op("sync", lambda h, src=src, hh=hh, dti=dti, sl=sl,
                             nlg4=nlg4: h.dma_start(
                                 out=oslab[:, dti * TS:dti * TS + TS].rearrange(
                                     "p (l c) -> p l c", l=nlg4),
                                 in_=o_gt5[src, sl * nlg4:(sl + 1) * nlg4,
                                           hh].rearrange("l p c -> p l c")),
                             inc=True)
                s.op("sync", lambda h, t0=t0: h.dma_start(
                    out=xinb[:].rearrange("p (d c) -> p d c", d=16),
                    in_=xres[:, :, t0:t0 + TS]), inc=True)
                lds = s.now("sync")

                pe_prev = {}
                wo_ld = {}
                for t in range(2):
                    if sl == 0 and t == 0:
                        wv = [("sync", ld0)]
                    elif t == 0:
                        wv = [("tensor", s.now("tensor"))]
                    else:
                        wv = ()
                    s.op("gpsimd", lambda h, t=t: h.dma_start(
                        out=wslab[:, (t % 2) * 2048:(t % 2) * 2048 + 2048],
                        in_=wo[t]), inc=True, waits=wv)
                    wo_ld[t] = s.now("gpsimd")
                for t in range(16):
                    wb_off = (t % 2) * 2048
                    pq = pbank[t % 2]
                    for d in range(16):
                        wv = [("gpsimd", wo_ld[t]), ("sync", lds)] if d == 0 else ()
                        s.op("tensor", lambda h, d=d, pq=pq, wb_off=wb_off:
                             h.matmul(pq[:, 0:TS],
                                      wslab[:, wb_off + d * 128:wb_off + d * 128 + 128],
                                      oslab[:, d * TS:d * TS + TS],
                                      start=(d == 0), stop=(d == 15)), waits=wv)
                    pe_t = s.op("tensor", lambda h: h.nop(), inc=True)
                    pe_prev[t] = pe_t
                    if t + 2 < 16:
                        s.op("gpsimd", lambda h, t=t: h.dma_start(
                            out=wslab[:, (t % 2) * 2048:(t % 2) * 2048 + 2048],
                            in_=wo[t + 2]), inc=True, waits=[("tensor", pe_t)])
                        wo_ld[t + 2] = s.now("gpsimd")
                    s.op("vector", lambda h, t=t, pq=pq: h.tensor_copy(
                        out=zslab[:, t * TS:t * TS + TS], in_=pq[:, 0:TS]),
                        waits=[("tensor", pe_t)])
                    z_ev = s.op("vector", lambda h, t=t: h.tensor_copy(
                        out=zb[:, 0:TS], in_=zslab[:, t * TS:t * TS + TS]),
                        inc=True)
                    z_sq = s.op("scalar", lambda h: h.activation(
                        out=zsq[:, 0:TS], in_=zb[:, 0:TS], func=AF.Square),
                        inc=True, waits=[("vector", z_ev)])
                    s.op("tensor", lambda h, t=t: h.matmul(
                        pbank[2][0:1, 0:TS], c_onesb[:], zb[:, 0:TS],
                        start=(t == 0), stop=(t == 15)), waits=[("vector", z_ev)])
                    s.op("tensor", lambda h, t=t: h.matmul(
                        pbank[3][0:1, 0:TS], c_onesb[:], zsq[:, 0:TS],
                        start=(t == 0), stop=(t == 15)), waits=[("scalar", z_sq)])
                    s.barrier("vector", ["tensor"])
                    s.barrier("scalar", ["tensor"])
                st_pe = s.op("tensor", lambda h: h.nop(), inc=True)

                ln_stats_and_bcast(pbank[2], pbank[3], [("tensor", st_pe)])

                for t in range(16):
                    s.op("vector", lambda h, t=t: h.tensor_tensor(
                        out=csc2[:, 0:TS], in0=zslab[:, t * TS:t * TS + TS],
                        in1=csc1[:, 0:TS], op=ALU.subtract))
                    s.op("vector", lambda h: h.tensor_tensor(
                        out=csc2[:, 0:TS], in0=csc2[:, 0:TS], in1=rB1[:, 0:TS],
                        op=ALU.mult))
                    s.op("vector", lambda h, t=t: h.tensor_tensor(
                        out=csc2[:, 0:TS], in0=csc2[:, 0:TS],
                        in1=xinb[:, t * TS:t * TS + TS], op=ALU.add))
                    x1d = s.op("vector", lambda h: h.tensor_copy(
                        out=x1t[:, 0:TS], in_=csc2[:, 0:TS]), inc=True)
                    s.op("sync", lambda h, t=t, t0=t0: h.dma_start(
                        out=x1_dram.ap()[:, t, t0:t0 + TS], in_=x1t[:, 0:TS]),
                        inc=True, waits=[("vector", x1d)])
                    s.barrier("vector", ["sync"])
                    x1sq = s.op("scalar", lambda h: h.activation(
                        out=zsq[:, 0:TS], in_=x1t[:, 0:TS], func=AF.Square),
                        inc=True, waits=[("vector", x1d)])
                    s.op("tensor", lambda h, t=t: h.matmul(
                        pbank[2][0:1, 0:TS], c_onesb[:], x1t[:, 0:TS],
                        start=(t == 0), stop=(t == 15)), waits=[("vector", x1d)])
                    s.op("tensor", lambda h, t=t: h.matmul(
                        pbank[3][0:1, 0:TS], c_onesb[:], zsq[:, 0:TS],
                        start=(t == 0), stop=(t == 15)), waits=[("scalar", x1sq)])
                    s.op("vector", lambda h, t=t: h.tensor_copy(
                        out=zslab[:, t * TS:t * TS + TS], in_=csc2[:, 0:TS]),
                        inc=True)
                    s.barrier("vector", ["tensor"])
                    s.barrier("scalar", ["vector"])
                st2_pe = s.op("tensor", lambda h: h.nop(), inc=True)

                ln_stats_and_bcast(pbank[2], pbank[3], [("tensor", st2_pe)])

                for t in range(16):
                    s.op("vector", lambda h, t=t: h.tensor_tensor(
                        out=csc2[:, 0:TS], in0=zslab[:, t * TS:t * TS + TS],
                        in1=csc1[:, 0:TS], op=ALU.subtract))
                    s.op("vector", lambda h, t=t, t0=t0: h.tensor_tensor(
                        out=x2b[:, t * Tc + t0: t * Tc + t0 + TS],
                        in0=csc2[:, 0:TS], in1=rB1[:, 0:TS], op=ALU.mult),
                        inc=(t == 15))
            x2_done = s.now("vector")

        esC2 = contextlib.ExitStack()
        with esC2:
            f8a = esC2.enter_context(nc.sbuf_tensor("f8a", [128, 2048], F8))
            f8b = esC2.enter_context(nc.sbuf_tensor("f8b", [128, 2048], F8))
            gslab = esC2.enter_context(nc.sbuf_tensor("gslab", [128, 2 * 2048], BF16))
            uslab = esC2.enter_context(nc.sbuf_tensor("uslab", [128, 2 * 2048], BF16))
            gth = esC2.enter_context(nc.sbuf_tensor("gth", [128, NDT], F32))
            gxh = esC2.enter_context(nc.sbuf_tensor("gxh", [128, NDT], F32))
            gv = esC2.enter_context(nc.sbuf_tensor("gv", [128, NDT], F32))
            uv = esC2.enter_context(nc.sbuf_tensor("uv", [128, NDT], F32))

            s.barrier("gpsimd", ["tensor", "vector", "scalar", "sync"])
            gu_pe_done = {}
            for ht in range(48):
                sl2 = ht % 2
                wv = [("tensor", gu_pe_done[ht - 2])] if ht >= 2 else ()
                s.op("gpsimd", lambda h, ht=ht: h.dma_start(
                    out=f8a[:], in_=gwt[ht]), inc=True, waits=wv)
                s.op("gpsimd", lambda h, ht=ht: h.dma_start(
                    out=f8b[:], in_=uwt[ht]), inc=True)
                w8 = s.now("gpsimd")
                s.op("gpsimd", lambda h, sl2=sl2: h.dma_start(
                    out=gslab[:, sl2 * 2048:sl2 * 2048 + 2048], in_=f8a[:]),
                    inc=True, waits=[("gpsimd", w8)])
                s.op("gpsimd", lambda h, sl2=sl2: h.dma_start(
                    out=uslab[:, sl2 * 2048:sl2 * 2048 + 2048], in_=f8b[:]),
                    inc=True)
                wbf = s.now("gpsimd")
                for nb in range(ND):
                    n0 = nb * NDT
                    for d in range(16):
                        wv = [("gpsimd", wbf), ("vector", x2_done)] if d == 0 else ()
                        s.op("tensor", lambda h, d=d, sl2=sl2, n0=n0: h.matmul(
                            pbank[0][:, 0:NDT],
                            gslab[:, sl2 * 2048 + d * 128:sl2 * 2048 + d * 128 + 128],
                            x2b[:, d * Tc + n0: d * Tc + n0 + NDT],
                            start=(d == 0), stop=(d == 15)), waits=wv)
                    for d in range(16):
                        s.op("tensor", lambda h, d=d, sl2=sl2, n0=n0: h.matmul(
                            pbank[1][:, 0:NDT],
                            uslab[:, sl2 * 2048 + d * 128:sl2 * 2048 + d * 128 + 128],
                            x2b[:, d * Tc + n0: d * Tc + n0 + NDT],
                            start=(d == 0), stop=(d == 15)))
                    gu_pe = s.op("tensor", lambda h: h.nop(), inc=True)
                    gu_pe_done[ht] = gu_pe
                    gt_d = s.op("scalar", lambda h, ht=ht: h.activation(
                        out=gth[:], in_=pbank[0][:, 0:NDT], func=AF.Tanh,
                        scale=hgs[:, ht:ht + 1]), inc=True,
                        waits=[("tensor", gu_pe), ("vector", hgs_d)])
                    s.op("vector", lambda h, ht=ht: h.tensor_scalar(
                        out=gxh[:], in0=pbank[0][:, 0:NDT],
                        scalar1=hgs[:, ht:ht + 1], scalar2=None, op0=ALU.mult),
                        waits=[("scalar", gt_d)])
                    s.op("vector", lambda h: h.scalar_tensor_tensor(
                        out=gv[:], in0=gth[:], scalar=1.0, in1=gxh[:],
                        op0=ALU.add, op1=ALU.mult))
                    s.op("vector", lambda h, ht=ht: h.tensor_scalar(
                        out=uv[:], in0=pbank[1][:, 0:NDT],
                        scalar1=uscb[:, ht:ht + 1], scalar2=None, op0=ALU.mult))
                    s.op("vector", lambda h, ht=ht, n0=n0: h.tensor_tensor(
                        out=fuse[:, ht * Tc + n0: ht * Tc + n0 + NDT], in0=gv[:],
                        in1=uv[:], op=ALU.mult), inc=True)
                    s.barrier("tensor", ["vector"])
                    s.barrier("scalar", ["vector"])
            fuse_done = s.now("vector")
        esX.close()

        esC3 = contextlib.ExitStack()
        with esC3:
            dslab = esC3.enter_context(
                nc.sbuf_tensor("dslab", [128, 2 * 48 * 128], BF16))
            f8d = esC3.enter_context(nc.sbuf_tensor("f8d", [128, 48 * 128], F8))
            dnslab = esC3.enter_context(
                nc.sbuf_tensor("dnslab", [128, 16 * Tc], BF16))
            x1rb = esC3.enter_context(nc.sbuf_tensor("x1rb", [128, 16 * TS], BF16))
            outs = esC3.enter_context(nc.sbuf_tensor("outs", [128, 16 * TS], F32))

            s.barrier("gpsimd", ["tensor", "vector", "scalar", "sync"])
            dn_pe_done = {}
            for t in range(16):
                sl2 = t % 2
                wv = [("tensor", dn_pe_done[t - 2])] if t >= 2 else ()
                s.op("gpsimd", lambda h, t=t: h.dma_start(out=f8d[:], in_=dwt[t]),
                     inc=True, waits=wv)
                f8_d = s.now("gpsimd")
                s.op("gpsimd", lambda h, sl2=sl2: h.dma_start(
                    out=dslab[:, sl2 * 6144:sl2 * 6144 + 6144], in_=f8d[:]),
                    inc=True, waits=[("gpsimd", f8_d)])
                dw_d = s.now("gpsimd")
                for nb in range(ND):
                    n0 = nb * NDT
                    for dd in range(48):
                        wv = [("gpsimd", dw_d), ("vector", fuse_done)] \
                            if dd == 0 else ()
                        s.op("tensor", lambda h, dd=dd, sl2=sl2, n0=n0, nb=nb:
                             h.matmul(
                                 pbank[nb % 2][:, 0:NDT],
                                 dslab[:, sl2 * 6144 + dd * 128:
                                       sl2 * 6144 + dd * 128 + 128],
                                 fuse[:, dd * Tc + n0: dd * Tc + n0 + NDT],
                                 start=(dd == 0), stop=(dd == 47)), waits=wv)
                    dn_pe = s.op("tensor", lambda h: h.nop(), inc=True)
                    dn_pe_done[t] = dn_pe
                    s.op("vector", lambda h, t=t, n0=n0, nb=nb: h.tensor_scalar(
                        out=dnslab[:, t * Tc + n0: t * Tc + n0 + NDT],
                        in0=pbank[nb % 2][:, 0:NDT], scalar1=dscb[:, t:t + 1],
                        scalar2=None, op0=ALU.mult), inc=True,
                        waits=[("tensor", dn_pe), ("sync", ldsc)])
                    s.barrier("tensor", ["vector"])
            dn_done = s.now("vector")

            for nb in range(NSL):
                t0 = nb * TS
                if nb > 0:
                    s.barrier("sync", ["vector"])
                s.op("sync", lambda h, t0=t0: h.dma_start(
                    out=x1rb[:].rearrange("p (d c) -> p d c", d=16),
                    in_=x1_dram.ap()[:, :, t0:t0 + TS]), inc=True,
                    waits=[("vector", dn_done)] if nb == 0 else ())
                x1r_d = s.now("sync")
                for t in range(16):
                    z_ev = s.op("vector", lambda h, t=t, t0=t0: h.tensor_copy(
                        out=zb[:, 0:TS],
                        in_=dnslab[:, t * Tc + t0: t * Tc + t0 + TS]), inc=True)
                    z_sq = s.op("scalar", lambda h: h.activation(
                        out=zsq[:, 0:TS], in_=zb[:, 0:TS], func=AF.Square),
                        inc=True, waits=[("vector", z_ev)])
                    s.op("tensor", lambda h, t=t: h.matmul(
                        pbank[2][0:1, 0:TS], c_onesb[:], zb[:, 0:TS],
                        start=(t == 0), stop=(t == 15)), waits=[("vector", z_ev)])
                    s.op("tensor", lambda h, t=t: h.matmul(
                        pbank[3][0:1, 0:TS], c_onesb[:], zsq[:, 0:TS],
                        start=(t == 0), stop=(t == 15)), waits=[("scalar", z_sq)])
                    s.barrier("vector", ["tensor"])
                    s.barrier("scalar", ["tensor"])
                st3 = s.op("tensor", lambda h: h.nop(), inc=True)

                ln_stats_and_bcast(pbank[2], pbank[3], [("tensor", st3)])

                for t in range(16):
                    s.op("vector", lambda h, t=t, t0=t0: h.tensor_tensor(
                        out=csc2[:, 0:TS],
                        in0=dnslab[:, t * Tc + t0: t * Tc + t0 + TS],
                        in1=csc1[:, 0:TS], op=ALU.subtract))
                    s.op("vector", lambda h: h.tensor_tensor(
                        out=csc2[:, 0:TS], in0=csc2[:, 0:TS], in1=rB1[:, 0:TS],
                        op=ALU.mult))
                    ov = s.op("vector", lambda h, t=t: h.tensor_tensor(
                        out=csc2[:, 0:TS], in0=csc2[:, 0:TS],
                        in1=x1rb[:, t * TS:t * TS + TS], op=ALU.add), inc=True,
                        waits=[("sync", x1r_d)] if t == 0 else ())
                    for q in range(TS // 128):
                        trq = s.op("tensor", lambda h, q=q: h.transpose(
                            pbank[6][:, 0:128], csc2[:, q * 128:q * 128 + 128],
                            c_idf[:]), inc=True,
                            waits=[("vector", ov)] if q == 0 else ())
                        otd = s.op("vector", lambda h, t=t, q=q: h.tensor_copy(
                            out=outs[:, t * TS + q * 128:t * TS + q * 128 + 128],
                            in_=pbank[6][:, 0:128]), inc=True,
                            waits=[("tensor", trq)])
                        s.barrier("tensor", ["vector"])
                        s.op("sync", lambda h, t=t, q=q, t0=t0: h.dma_start(
                            out=out[t0 + q * 128: t0 + q * 128 + 128,
                                    t * 128:t * 128 + 128],
                            in_=outs[:, t * TS + q * 128:t * TS + q * 128 + 128]),
                            inc=True, waits=[("vector", otd)])
                s.barrier("vector", ["sync"])
            fin = s.now("sync")
            s.ops["sync"].append((None, False, (("sync", fin),)))

        s.emit(block, sems)
    return nc


def stage_inputs(inputs, T):
    Tc = T // NCORE
    x = np.ascontiguousarray(inputs["L0_attn_x"][0, :T].T.astype(np.float32))
    xT_s = np.ascontiguousarray(x.reshape(16, 128, T).transpose(1, 0, 2))

    Wq, Wk, Wv = (inputs["L0_attn_Wq"], inputs["L0_attn_Wk"], inputs["L0_attn_Wv"])
    Wbeta = inputs["L0_attn_Wbeta"]
    Wg1 = inputs["L0_attn_Wg1"]
    Wo_ = inputs["L0_attn_Wo"]
    Wg2_ = inputs["L0_attn_Wg2"]
    lnw, lnb = inputs["L0_attn_ln_w"], inputs["L0_attn_ln_b"]
    nw = inputs["L0_attn_norm_w"]
    gw = inputs["L0_ffn_gate_w"]
    uw = inputs["L0_ffn_up_w"]
    dw = inputs["L0_ffn_down_w"]

    wo_s = np.ascontiguousarray(
        Wo_.reshape(16, 128, 16, 128).transpose(2, 1, 0, 3)).reshape(16, 128, -1)
    gwt_s = np.ascontiguousarray(
        gw.reshape(16, 128, 48, 128).transpose(2, 1, 0, 3)).reshape(48, 128, -1)
    uwt_s = np.ascontiguousarray(
        uw.reshape(16, 128, 48, 128).transpose(2, 1, 0, 3)).reshape(48, 128, -1)
    dwt_s = np.ascontiguousarray(
        dw.reshape(48, 128, 16, 128).transpose(2, 1, 0, 3)).reshape(16, 128, -1)
    gsc_s = np.ascontiguousarray(inputs["L0_ffn_gate_scale"].reshape(48, 128).T)
    usc_s = np.ascontiguousarray(inputs["L0_ffn_up_scale"].reshape(48, 128).T)
    dsc_s = np.ascontiguousarray(inputs["L0_ffn_down_scale"].reshape(16, 128).T)
    sidx = np.arange(128)
    mkK_s = ((sidx[:, None] // CH) < (sidx[None, :] // CH)).astype(np.float32)
    mkG_s = (sidx[:, None] <= sidx[None, :]).astype(np.float32)
    mkKL_s = np.ascontiguousarray(mkK_s[0:64]); mkKH_s = np.ascontiguousarray(mkK_s[64:128])
    mkGL_s = np.ascontiguousarray(mkG_s[0:64]); mkGH_s = np.ascontiguousarray(mkG_s[64:128])

    per_core = []
    for c in range(NCORE):
        h0, h1 = 2 * c, 2 * c + 1
        cols = np.zeros((D, 1024), np.float32)
        cols[:, 0:128] = Wq[:, h0 * 128:(h0 + 1) * 128]
        cols[:, 128:256] = Wq[:, h1 * 128:(h1 + 1) * 128]
        cols[:, 256:384] = Wk[:, h0 * 128:(h0 + 1) * 128]
        cols[:, 384:512] = Wk[:, h1 * 128:(h1 + 1) * 128]
        cols[:, 512:640] = Wv[:, h0 * 128:(h0 + 1) * 128]
        cols[:, 640:768] = Wv[:, h1 * 128:(h1 + 1) * 128]
        cols[:, 768:896] = Wg1
        cols[:, 896] = 1.0
        cols[:, 896 + 32] = Wbeta[:, h0]
        cols[:, 896 + 64] = Wbeta[:, h1]
        wf_s = np.ascontiguousarray(
            cols.reshape(16, 128, 8, 128).transpose(1, 0, 2, 3)).reshape(128, -1)
        wg2_s = np.ascontiguousarray(np.concatenate(
            [Wg2_[:, h0 * 128:(h0 + 1) * 128], Wg2_[:, h1 * 128:(h1 + 1) * 128]],
            axis=1))
        lnw_cs = np.ascontiguousarray(
            np.stack([lnw[h0], lnw[h1]], axis=1).astype(np.float32))
        lnb_cs = np.ascontiguousarray(
            np.stack([lnb[h0], lnb[h1]], axis=1).astype(np.float32))
        nw_cs = np.ascontiguousarray(np.stack(
            [nw[h0 * 128:(h0 + 1) * 128], nw[h1 * 128:(h1 + 1) * 128]],
            axis=1).astype(np.float32))
        xres_s = np.ascontiguousarray(xT_s[:, :, c * Tc:(c + 1) * Tc])

        per_core.append({
            "xT": xT_s, "wf": wf_s, "wo": wo_s, "wg2": wg2_s,
            "lnw_c": lnw_cs, "lnb_c": lnb_cs, "nw_c": nw_cs,
            "gwt": gwt_s, "uwt": uwt_s, "dwt": dwt_s,
            "gsc": gsc_s, "usc": usc_s, "dsc": dsc_s,
            "idn": np.eye(128, dtype=BF),
            "mkKL": mkKL_s, "mkKH": mkKH_s, "mkGL": mkGL_s, "mkGH": mkGH_s,
            "onesc": np.ones((128, 1), np.float32),
            "onesf": np.ones((128, 128), np.float32),
            "onesrow": np.ones((1, 128), np.float32),
            "xres": xres_s,
        })
    return per_core


def kernel(**inputs):
    T = inputs["L0_attn_x"].shape[1]
    nc = build_program(T)
    in_maps = stage_inputs(inputs, T)
    res = run_bass_kernel_spmd(nc, in_maps, core_ids=list(range(NCORE)))
    outs = [np.asarray(res.results[c]["out"]) for c in range(NCORE)]
    return np.concatenate(outs, axis=0)[None].astype(np.float32)


# version 05
# speedup vs baseline: 1.5883x; 1.5883x over previous
import numpy as np
import ml_dtypes
import concourse.bass as bass
import concourse.mybir as mybir
from concourse.bass_utils import run_bass_kernel_spmd
from concourse.aot_env import allow_ici_all_to_all

F32 = mybir.dt.float32
BF16 = mybir.dt.bfloat16
F8 = mybir.dt.float8e4
AF = mybir.ActivationFunctionType
ALU = mybir.AluOpType
BF = ml_dtypes.bfloat16

D = 2048
DH = 128
CH = 16
GT = 128
NCORE = 8
EPS_A = 1e-6
EPS_M = 1e-5

ENGS = ("sync", "gpsimd", "vector", "scalar", "tensor")


COMPUTE_ENGS = ("vector", "scalar", "tensor")


class Sched:

    def __init__(self):
        self.ops = {e: [] for e in ENGS}
        self.cnt = {e: 0 for e in ENGS}
        self.step = {"sync": 16, "gpsimd": 16, "vector": 1, "scalar": 1,
                     "tensor": 1}

    def op(self, eng, fn, inc=False, waits=()):
        self.ops[eng].append((fn, bool(inc), tuple(waits)))
        if eng in COMPUTE_ENGS or inc:
            self.cnt[eng] += self.step[eng]
        return self.cnt[eng]

    def now(self, eng):
        return self.cnt[eng]

    def barrier(self, eng, others):
        w = tuple((o, self.cnt[o]) for o in others if self.cnt[o] > 0)
        if w:
            self.ops[eng].append((None, False, w))

    def emit(self, block, sems):
        import os as _os
        import bisect
        ssv = _os.environ.get("BASS_SELFSYNC", "vector")
        full = ssv == "full"
        full_engs = set(ssv.split(",")) if ssv not in ("min", "full") else (
            set(COMPUTE_ENGS) if full else set())
        waited = {e: set() for e in ENGS}
        for e in list(self.ops):
            for fn, inc, waits in self.ops[e]:
                for we, wv in waits:
                    if we != e and wv > 0 and we in waited:
                        waited[we].add(wv)
        flagged = {e: sorted(waited[e]) for e in COMPUTE_ENGS}

        class _TProxy:
            def __init__(self, h):
                self._h = h

            def __getattr__(self, k):
                return getattr(self._h, k)

            def matmul(self, *a, **kw):
                kw.setdefault("skip_group_check", True)
                return self._h.matmul(*a, **kw)

        def rv(we, wv):
            if we not in COMPUTE_ENGS or we in full_engs:
                return wv
            import bisect as _b
            return _b.bisect_right(flagged[we], wv)

        def mk(eng):
            selfsync = eng in COMPUTE_ENGS
            efull = eng in full_engs
            miles = sorted(waited[eng]) if not selfsync else []
            fset = set(flagged.get(eng, ()))

            def body(h):
                n = 0
                enforced = 0
                mi = 0
                for fn, inc, waits in self.ops[eng]:
                    for we, wv in waits:
                        wv2 = rv(we, wv)
                        if wv2 > 0:
                            h.wait_ge(sems[we], wv2)
                    if fn is not None:
                        if eng == "tensor":
                            h2 = _TProxy(h)
                        else:
                            h2 = h
                        if selfsync:
                            if efull and n > 0:
                                h.wait_ge(sems[eng], n)
                        elif inc:
                            tgt = 0
                            while mi < len(miles) and miles[mi] <= n:
                                tgt = miles[mi]
                                mi += 1
                            if tgt > enforced:
                                h.wait_ge(sems[eng], tgt)
                                enforced = tgt
                        ins = fn(h2)
                        if selfsync:
                            n += 1
                            if efull:
                                ins.then_inc(sems[eng], 1)
                            elif n in fset:
                                ins.then_inc(sems[eng], 1)
                        elif inc:
                            ins.then_inc(sems[eng], self.step[eng])
                            n += self.step[eng]
            return body
        block.sync(mk("sync"))
        block.gpsimd(mk("gpsimd"))
        block.vector(mk("vector"))
        block.scalar(mk("scalar"))
        block.tensor(mk("tensor"))


def build_program(T):
    NG = T // GT
    Tc = T // NCORE
    NLG = Tc // GT
    NCH = GT // CH
    TS = min(256, Tc)
    NSL = Tc // TS
    NDT = min(512, Tc)
    ND = Tc // NDT
    assert T % (GT * NCORE) == 0

    import os as _os
    nc = bass.Bass("TRN2", target_bir_lowering=False,
                   detect_race_conditions=not _os.environ.get("KERN_NORACE"))
    s = Sched()

    def din(name, shape, dt):
        return nc.dram_tensor(name, shape, dt, kind="ExternalInput").ap()

    xT = din("xT", [128, 16, T], F32)
    wf = din("wf", [128, 16 * 8 * 128], F32)
    wo = din("wo", [16, 128, 16 * 128], F32)
    wg2 = din("wg2", [128, 256], F32)
    lnw_c = din("lnw_c", [128, 2], F32)
    lnb_c = din("lnb_c", [128, 2], F32)
    nw_c = din("nw_c", [128, 2], F32)
    gwt = din("gwt", [48, 128, 16 * 128], F32)
    uwt = din("uwt", [48, 128, 16 * 128], F32)
    dwt = din("dwt", [16, 128, 48 * 128], F32)
    gsc = din("gsc", [128, 48], F32)
    usc = din("usc", [128, 48], F32)
    dsc = din("dsc", [128, 16], F32)
    idn = din("idn", [128, 128], BF16)
    mkKL = din("mkKL", [64, 128], F32)
    mkKH = din("mkKH", [64, 128], F32)
    mkGL = din("mkGL", [64, 128], F32)
    mkGH = din("mkGH", [64, 128], F32)
    onesc = din("onesc", [128, 1], F32)
    onesf = din("onesf", [128, 128], F32)
    onesrow = din("onesrow", [1, 128], F32)
    xres = din("xres", [128, 16, Tc], F32)

    out = nc.dram_tensor("out", [Tc, D], F32, kind="ExternalOutput").ap()
    o_stage = nc.dram_tensor("o_stage", [NCORE * NLG * 2 * 128, GT], BF16)
    o_gath = nc.dram_tensor("o_gath", [NCORE * NLG * 2 * 128, GT], BF16)
    x1_dram = nc.dram_tensor("x1_dram", [128, 16, Tc], BF16)
    o_st5 = o_stage.ap().rearrange("(a l hh p) c -> a l hh p c",
                                   a=NCORE, l=NLG, hh=2, p=128)
    o_gt5 = o_gath.ap().rearrange("(a l hh p) c -> a l hh p c",
                                  a=NCORE, l=NLG, hh=2, p=128)

    import contextlib
    es = contextlib.ExitStack()
    with es:
        def sb(name, shape, dt):
            return es.enter_context(nc.sbuf_tensor(name, shape, dt))

        pbank = [es.enter_context(nc.psum_tensor("pb%d" % i, [128, 512], F32))
                 for i in range(8)]
        pqA, pqB, pA, pB, pS, pSC, pQM, pSM = pbank
        pSMb = pSM.ap().bitcast(BF16)

        sems = {e: es.enter_context(nc.semaphore("sem_" + e)) for e in ENGS}
        sems["cc"] = es.enter_context(nc.semaphore("sem_cc"))
        block = es.enter_context(nc.Block())

        c_id = sb("c_id", [128, 128], BF16)
        c_idf = sb("c_idf", [128, 128], F32)
        c_mkKL = sb("c_mkKL", [64, 128], F32)
        c_mkKH = sb("c_mkKH", [64, 128], F32)
        c_mkGL = sb("c_mkGL", [64, 128], F32)
        c_mkGH = sb("c_mkGH", [64, 128], F32)
        c_ones = sb("c_ones", [128, 128], F32)
        c_onesc = sb("c_onesc", [128, 1], F32)
        c_onesb = sb("c_onesb", [128, 1], BF16)
        c_onesrow = sb("c_onesrow", [1, 128], F32)
        c_lnw = sb("c_lnw", [128, 2], F32)
        c_lnb = sb("c_lnb", [128, 2], F32)
        c_nw = sb("c_nw", [128, 2], F32)
        c_w2 = sb("c_w2", [128, 2], F32)
        c_wb = sb("c_wb", [128, 2], F32)
        c_negw = sb("c_negw", [128, 2], F32)
        c_w2B = sb("c_w2B", [128, 32], F32)
        c_epsA = sb("c_epsA", [128, 1], F32)
        c_epsM = sb("c_epsM", [128, 1], F32)

        for dst, src in [(c_id, idn), (c_mkKL, mkKL), (c_mkKH, mkKH),
                         (c_mkGL, mkGL), (c_mkGH, mkGH), (c_ones, onesf),
                         (c_onesc, onesc), (c_onesrow, onesrow),
                         (c_lnw, lnw_c), (c_lnb, lnb_c), (c_nw, nw_c)]:
            s.op("sync", lambda h, d=dst, sr=src: h.dma_start(out=d[:], in_=sr),
                 inc=True)
        ld0 = s.now("sync")

        scan_done = {}
        qkv_evac_done = {}
        qkv_pe_done = {}
        for g in range(-4, 0):
            scan_done[g] = 0
            qkv_evac_done[g] = 0
            qkv_pe_done[g] = 0

        esA = contextlib.ExitStack()
        with esA:
            def sba(name, shape, dt):
                return esA.enter_context(nc.sbuf_tensor(name, shape, dt))

            wfb = sba("wfb", [128, 16 * 8 * 128], BF16)
            wg2b = sba("wg2b", [128, 256], BF16)
            csWb = sba("csWb", [1, 8 * 128], BF16)
            W = 2
            xb = sba("xb", [128, W * 2048], BF16)
            sqb = sba("sqb", [128, W * 2048], BF16)
            qs_f = sba("qs_f", [128, W * 256], F32)
            qs_b = sba("qs_b", [128, W * 256], BF16)
            kn_f = sba("kn_f", [128, W * 256], F32)
            kn_b = sba("kn_b", [128, W * 256], BF16)
            v_f = sba("v_f", [128, W * 256], F32)
            g1_b = sba("g1_b", [128, W * 128], BF16)
            eT = sba("eT", [128, W * 4], F32)
            rowbuf = sba("rowbuf", [128, W * 128], F32)
            s_op = sba("s_op", [128, 256], BF16)
            vnstkL = sba("vnstkL", [64, 256], BF16)
            vnstkH = sba("vnstkH", [64, 256], BF16)
            aktL = sba("aktL", [64, 256], BF16)
            aktH = sba("aktH", [64, 256], BF16)
            agtL = sba("agtL", [64, 256], BF16)
            agtH = sba("agtH", [64, 256], BF16)
            vnstg = sba("vnstg", [128, 64], BF16)
            wbm = sba("wbm", [128, 256], F32)
            ketL = sba("ketL", [64, 256], BF16)
            ketH = sba("ketH", [64, 256], BF16)
            kescr = sba("kescr", [128, 128], BF16)
            scrA = sba("scrA", [128, 128], F32)
            scr1 = sba("scr1", [128, 32], F32)
            scr2 = sba("scr2", [128, 32], F32)
            rstdc = sba("rstdc", [128, 32], F32)
            khh = sba("khh", [128, 32], F32)
            strow = sba("strow", [1, 6 * 128], F32)
            strowb = sba("strowb", [1, 128], BF16)
            rstdB = sba("rstdB", [128, 128], F32)
            qscr1 = sba("qscr1", [128, 128], F32)
            qscr2 = sba("qscr2", [128, 128], F32)
            oscr = sba("oscr", [128, 256], F32)
            oth = sba("oth", [128, 128], F32)
            osig = sba("osig", [128, 128], F32)
            oo = sba("oo", [128, 128], F32)
            oo2 = sba("oo2", [128, 128], F32)
            obuf = sba("obuf", [128, 256], BF16)

            s.op("gpsimd", lambda h: h.dma_start(out=wfb[:], in_=wf), inc=True)
            s.op("gpsimd", lambda h: h.dma_start(out=wg2b[:], in_=wg2), inc=True)
            ld1 = s.now("gpsimd")

            s.op("vector", lambda h: h.tensor_tensor(
                out=c_w2[:], in0=c_lnw[:], in1=c_lnw[:], op=ALU.mult),
                waits=[("sync", ld0)])
            s.op("vector", lambda h: h.tensor_tensor(
                out=c_wb[:], in0=c_lnw[:], in1=c_lnb[:], op=ALU.mult))
            s.op("vector", lambda h: h.tensor_scalar(
                out=c_negw[:], in0=c_lnw[:], scalar1=-1.0, scalar2=None,
                op0=ALU.mult))
            s.op("vector", lambda h: h.tensor_copy(out=c_idf[:], in_=c_id[:]))
            s.op("vector", lambda h: h.tensor_copy(out=c_onesb[:], in_=c_onesc[:]))
            for hh in range(2):
                s.op("vector", lambda h, hh=hh: h.tensor_scalar(
                    out=c_w2B[:, hh * 16:hh * 16 + 16], in0=c_ones[:, 0:16],
                    scalar1=c_w2[:, hh:hh + 1], scalar2=None, op0=ALU.mult))
            s.op("vector", lambda h: h.memset(s_op[:], 0.0))
            s.op("vector", lambda h: h.memset(rowbuf[:], 0.0))
            s.op("vector", lambda h: h.memset(c_epsA[:], EPS_A))
            s.op("vector", lambda h: h.memset(c_epsM[:], EPS_M), inc=True)
            pre0 = s.now("vector")

            for t in range(8):
                for d in range(16):
                    wv = [("gpsimd", ld1), ("vector", pre0)] \
                        if (t == 0 and d == 0) else ()
                    s.op("tensor", lambda h, t=t, d=d: h.matmul(
                        pQM[0:1, 0:128], c_onesb[:],
                        wfb[:, (d * 8 + t) * 128:(d * 8 + t) * 128 + 128],
                        start=(d == 0), stop=(d == 15)), waits=wv)
                pe_cs = s.now("tensor")
                s.op("vector", lambda h, t=t: h.tensor_scalar(
                    out=csWb[0:1, t * 128:t * 128 + 128], in0=pQM[0:1, 0:128],
                    scalar1=-1.0, scalar2=None, op0=ALU.mult),
                    inc=True, waits=[("tensor", pe_cs)])
                s.barrier("tensor", ["vector"])

            def qkv_group(g):
                sg = g % W
                xo = sg * 2048
                ao = sg * 256
                wv = [("tensor", qkv_pe_done[g - W])] if g >= W else ()
                s.op("gpsimd", lambda h, g=g, xo=xo: h.dma_start(
                    out=xb[:, xo:xo + 2048].rearrange("p (d c) -> p d c", d=16),
                    in_=xT[:, :, g * GT:(g + 1) * GT]), inc=True, waits=wv)
                ldx = s.now("gpsimd")
                s.op("scalar", lambda h, xo=xo: h.activation(
                    out=sqb[:, xo:xo + 2048], in_=xb[:, xo:xo + 2048],
                    func=AF.Square), inc=True, waits=[("gpsimd", ldx)])
                sqd = s.now("scalar")

                for t in range(8):
                    pq = pqA if t < 4 else pqB
                    for d in range(16):
                        wv = [("gpsimd", ldx), ("vector", qkv_evac_done[g - 1]),
                              ("vector", scan_done[g - W])] \
                            if (t == 0 and d == 0) else ()
                        s.op("tensor", lambda h, t=t, d=d, pq=pq, xo=xo: h.matmul(
                            pq[:, (t % 4) * 128:(t % 4) * 128 + 128],
                            wfb[:, (d * 8 + t) * 128:(d * 8 + t) * 128 + 128],
                            xb[:, xo + d * 128: xo + d * 128 + 128],
                            start=(t % 4 == 0 and d == 0), stop=(d == 15)),
                            waits=wv)
                for d in range(16):
                    wv = [("scalar", sqd), ("vector", scan_done[g - 1]),
                          ("scalar", s.now("scalar"))] if d == 0 else ()
                    s.op("tensor", lambda h, d=d, xo=xo: h.matmul(
                        pQM[0:1, 0:128], c_onesb[:],
                        sqb[:, xo + d * 128: xo + d * 128 + 128],
                        start=(d == 0), stop=(d == 15)), waits=wv)
                pe_mm = s.now("tensor")

                ext = pqB[:, 384:512]
                s.op("vector", lambda h, ext=ext: h.tensor_copy(
                    out=strow[0:1, 640:768], in_=ext[0:1, 0:128]),
                    waits=[("tensor", pe_mm)])
                s.op("vector", lambda h: h.tensor_tensor(
                    out=strow[0:1, 128:256], in0=strow[0:1, 640:768],
                    in1=strow[0:1, 640:768], op=ALU.mult))
                s.op("vector", lambda h: h.scalar_tensor_tensor(
                    out=strow[0:1, 128:256], in0=strow[0:1, 128:256],
                    scalar=-1.0 / D, in1=pQM[0:1, 0:128], op0=ALU.mult,
                    op1=ALU.add))
                s.op("vector", lambda h, ext=ext: h.tensor_scalar(
                    out=strowb[0:1, :], in0=ext[0:1, 0:128], scalar1=1.0 / D,
                    scalar2=None, op0=ALU.mult), inc=True)
                dve_r = s.now("vector")
                s.op("scalar", lambda h: h.activation(
                    out=strow[0:1, 256:384], in_=strow[0:1, 128:256], func=AF.Ln,
                    scale=1.0 / D, bias=c_epsM[0:1, :]), waits=[("vector", dve_r)])
                s.op("scalar", lambda h: h.activation(
                    out=strow[0:1, 0:128], in_=strow[0:1, 256:384], func=AF.Exp,
                    scale=-0.5), inc=True)
                act_r = s.now("scalar")

                for t in range(8):
                    pq = pqA if t < 4 else pqB
                    s.op("tensor", lambda h, t=t, pq=pq: h.matmul(
                        pq[:, (t % 4) * 128:(t % 4) * 128 + 128],
                        csWb[0:1, t * 128:t * 128 + 128], strowb[0:1, :],
                        start=False, stop=True),
                        waits=[("scalar", act_r)] if t == 0 else ())
                pe_fix = s.op("tensor", lambda h: h.matmul(
                    pQM[:, 128:256], c_onesrow[0:1, :], strow[0:1, 0:128],
                    start=True, stop=True), inc=True)
                qkv_pe_done[g] = pe_fix
                s.op("vector", lambda h: h.tensor_copy(
                    out=rstdB[:], in_=pQM[:, 128:256]), inc=True,
                    waits=[("tensor", pe_fix), ("vector", scan_done[g - W])])
                s.barrier("tensor", ["vector"])

                for hh in range(2):
                    tq = pqA[:, hh * 128:hh * 128 + 128]
                    qf = qs_f[:, ao + hh * 128: ao + hh * 128 + 128]
                    q_ev = s.op("vector", lambda h, tq=tq, qf=qf: h.tensor_tensor(
                        out=qf, in0=tq, in1=rstdB[:], op=ALU.mult), inc=True)
                    s.op("scalar", lambda h, qf=qf: h.activation(
                        out=oth[:], in_=qf, func=AF.Exp, scale=-1.0),
                        waits=[("vector", q_ev)])
                    s.op("scalar", lambda h: h.activation(
                        out=oth[:], in_=oth[:], func=AF.Ln, scale=1.0,
                        bias=c_onesc[:]))
                    a_t = s.op("scalar", lambda h: h.activation(
                        out=oth[:], in_=oth[:], func=AF.Exp, scale=-1.0),
                        inc=True)
                    s.op("vector", lambda h, qf=qf: h.tensor_tensor(
                        out=qf, in0=qf, in1=oth[:], op=ALU.mult),
                        waits=[("scalar", a_t)])
                    s.op("vector", lambda h, qf=qf, hh=hh, ao=ao: h.tensor_copy(
                        out=qs_b[:, ao + hh * 128: ao + hh * 128 + 128], in_=qf),
                        inc=True)
                    s.barrier("scalar", ["vector"])
                for hh in range(2):
                    tk = pqA[:, (2 + hh) * 128:(2 + hh) * 128 + 128]
                    kf = kn_f[:, ao + hh * 128: ao + hh * 128 + 128]
                    k_ev = s.op("vector", lambda h, tk=tk, kf=kf: h.tensor_tensor(
                        out=kf, in0=tk, in1=rstdB[:], op=ALU.mult), inc=True)
                    k_sq = s.op("scalar", lambda h, kf=kf: h.activation(
                        out=qscr1[:], in_=kf, func=AF.Square),
                        inc=True, waits=[("vector", k_ev)])
                    k_mm = s.op("tensor", lambda h: h.matmul(
                        pQM[0:1, 0:128], c_onesc[:], qscr1[:],
                        start=True, stop=True), inc=True, waits=[("scalar", k_sq)])
                    s.op("scalar", lambda h: h.activation(
                        out=strow[0:1, 384:512], in_=pQM[0:1, 0:128], func=AF.Ln,
                        scale=1.0, bias=c_epsA[0:1, :]), waits=[("tensor", k_mm)])
                    k_rn = s.op("scalar", lambda h: h.activation(
                        out=strow[0:1, 384:512], in_=strow[0:1, 384:512],
                        func=AF.Exp, scale=-0.5), inc=True)
                    k_bc = s.op("tensor", lambda h: h.matmul(
                        pQM[:, 128:256], c_onesrow[0:1, :], strow[0:1, 384:512],
                        start=True, stop=True), inc=True, waits=[("scalar", k_rn)])
                    s.op("vector", lambda h, kf=kf: h.tensor_tensor(
                        out=kf, in0=kf, in1=pQM[:, 128:256], op=ALU.mult),
                        waits=[("tensor", k_bc)])
                    s.op("vector", lambda h, kf=kf, hh=hh, ao=ao: h.tensor_copy(
                        out=kn_b[:, ao + hh * 128: ao + hh * 128 + 128], in_=kf),
                        inc=True)
                    s.barrier("tensor", ["vector"])
                    s.barrier("scalar", ["vector"])
                for hh in range(2):
                    tv = pqB[:, hh * 128:hh * 128 + 128]
                    s.op("vector", lambda h, tv=tv, hh=hh, ao=ao: h.tensor_tensor(
                        out=v_f[:, ao + hh * 128: ao + hh * 128 + 128],
                        in0=tv, in1=rstdB[:], op=ALU.mult))
                s.op("vector", lambda h: h.tensor_tensor(
                    out=qscr2[:], in0=pqB[:, 256:384], in1=rstdB[:], op=ALU.mult))
                s.op("vector", lambda h, sg=sg: h.tensor_copy(
                    out=g1_b[:, sg * 128:sg * 128 + 128], in_=qscr2[:]))
                s.op("vector", lambda h, sg=sg, ext=ext: h.tensor_tensor(
                    out=rowbuf[32:33, sg * 128:sg * 128 + 128], in0=ext[32:33, 0:128],
                    in1=rstdB[32:33, 0:128], op=ALU.mult))
                v_done = s.op("vector", lambda h, sg=sg, ext=ext: h.tensor_tensor(
                    out=rowbuf[64:65, sg * 128:sg * 128 + 128],
                    in0=ext[64:65, 0:128], in1=rstdB[64:65, 0:128], op=ALU.mult),
                    inc=True)
                for i, base in enumerate((32, 64)):
                    kw = [("vector", v_done)] if i == 0 else ()
                    rb = lambda h, base=base, sg=sg: rowbuf[
                        base:base + 1, sg * 128:sg * 128 + 128]
                    s.op("scalar", lambda h, base=base, sg=sg: h.activation(
                        out=rowbuf[base:base + 1, sg * 128:sg * 128 + 128],
                        in_=rowbuf[base:base + 1, sg * 128:sg * 128 + 128],
                        func=AF.Exp, scale=-1.0), waits=kw)
                    s.op("scalar", lambda h, base=base, sg=sg: h.activation(
                        out=rowbuf[base:base + 1, sg * 128:sg * 128 + 128],
                        in_=rowbuf[base:base + 1, sg * 128:sg * 128 + 128],
                        func=AF.Ln, scale=1.0, bias=c_onesc[0:1, :]))
                    s.op("scalar", lambda h, base=base, sg=sg: h.activation(
                        out=rowbuf[base:base + 1, sg * 128:sg * 128 + 128],
                        in_=rowbuf[base:base + 1, sg * 128:sg * 128 + 128],
                        func=AF.Exp, scale=-1.0), inc=(i == 1))
                e_th = s.now("scalar")
                for i, base in enumerate((32, 64)):
                    kw = [("scalar", e_th)] if i == 0 else ()
                    s.op("vector", lambda h, base=base, sg=sg: h.tensor_scalar(
                        out=rowbuf[base:base + 1, sg * 128:sg * 128 + 128],
                        in0=rowbuf[base:base + 1, sg * 128:sg * 128 + 128],
                        scalar1=-2.0, scalar2=None, op0=ALU.mult),
                        inc=(i == 1), waits=kw)
                e_neg = s.now("vector")
                i = 0
                for hhb, base in enumerate((32, 64)):
                    for half in range(2):
                        kw = [("vector", e_neg)] if i == 0 else ()
                        s.op("tensor", lambda h, base=base, sg=sg, i=i, half=half:
                             h.matmul(
                                 pQM[0:64, 256 + i:257 + i],
                                 rowbuf[base:base + 1,
                                        sg * 128 + half * 64:
                                        sg * 128 + half * 64 + 64],
                                 c_ones[base:base + 1, 0:1],
                                 start=True, stop=True),
                             inc=(i == 3), waits=kw)
                        i += 1
                e_tr = s.now("tensor")
                s.op("vector", lambda h, sg=sg: h.tensor_copy(
                    out=eT[0:64, sg * 4: sg * 4 + 4], in_=pQM[0:64, 256:260]),
                    inc=True, waits=[("tensor", e_tr)])
                s.barrier("tensor", ["vector"])
                qkv_evac_done[g] = s.now("vector")

            def scan_group(g):
                sg = g % W
                ao = sg * 256
                qev = qkv_evac_done[g]
                guard = qev

                if g >= 1:
                    for hh in range(2):
                        s.op("vector", lambda h, hh=hh: h.tensor_copy(
                            out=s_op[:, hh * 128:hh * 128 + 128],
                            in_=pS[:, hh * 128:hh * 128 + 128]),
                            inc=(hh == 1),
                            waits=[("vector", qev)] if hh == 0 else ())

                for hh in range(2):
                    kb = kn_b[:, ao + hh * 128: ao + hh * 128 + 128]
                    qb = qs_b[:, ao + hh * 128: ao + hh * 128 + 128]
                    for half, (aktX, agtX, mKX, mGX) in enumerate(
                            ((aktL, agtL, c_mkKL, c_mkGL),
                             (aktH, agtH, c_mkKH, c_mkGH))):
                        ec = eT[0:64, sg * 4 + hh * 2 + half:
                                sg * 4 + hh * 2 + half + 1]
                        mm1 = s.op("tensor", lambda h, kb=kb, half=half: h.matmul(
                            pSC[0:64, 0:128], kb[:, half * 64:half * 64 + 64],
                            kb, start=True, stop=True), inc=True,
                            waits=[("vector", qev)] if (hh == 0 and half == 0)
                            else ())
                        dv1 = s.op("vector", lambda h, hh=hh, ec=ec, aktX=aktX,
                                   mKX=mKX: h.scalar_tensor_tensor(
                                       out=aktX[:, hh * 128:hh * 128 + 128],
                                       in0=pSC[0:64, 0:128], scalar=ec,
                                       in1=mKX[:], op0=ALU.mult, op1=ALU.mult),
                                   inc=True, waits=[("tensor", mm1)])
                        mm2 = s.op("tensor", lambda h, kb=kb, qb=qb, half=half:
                                   h.matmul(
                                       pSC[0:64, 0:128],
                                       kb[:, half * 64:half * 64 + 64], qb,
                                       start=True, stop=True), inc=True,
                                   waits=[("vector", dv1)])
                        s.op("vector", lambda h, hh=hh, ec=ec, agtX=agtX,
                             mGX=mGX: h.scalar_tensor_tensor(
                                 out=agtX[:, hh * 128:hh * 128 + 128],
                                 in0=pSC[0:64, 0:128], scalar=ec, in1=mGX[:],
                                 op0=ALU.mult, op1=ALU.mult),
                             inc=True, waits=[("tensor", mm2)])
                        s.barrier("tensor", ["vector"])

                for hh in range(2):
                    s.op("vector", lambda h, hh=hh, ao=ao: h.tensor_tensor(
                        out=oscr[:, 0:128],
                        in0=v_f[:, ao + hh * 128:ao + hh * 128 + 128],
                        in1=kn_f[:, ao + hh * 128:ao + hh * 128 + 128],
                        op=ALU.subtract))
                    s.op("vector", lambda h, hh=hh: h.tensor_scalar(
                        out=wbm[:, hh * 128:hh * 128 + 128], in0=oscr[:, 0:128],
                        scalar1=c_negw[:, hh:hh + 1], scalar2=c_wb[:, hh:hh + 1],
                        op0=ALU.mult, op1=ALU.add), inc=True)
                    base = 32 + hh * 32
                    ebc = s.op("tensor", lambda h, sg=sg, base=base: h.matmul(
                        pQM[:, 128:256], c_ones[base:base + 1, 0:128],
                        rowbuf[base:base + 1, sg * 128:sg * 128 + 128],
                        start=True, stop=True), inc=True,
                        waits=[("vector", guard)] if hh == 0 else ())
                    kes = s.op("vector", lambda h, hh=hh, ao=ao: h.tensor_tensor(
                        out=kescr[:],
                        in0=kn_f[:, ao + hh * 128:ao + hh * 128 + 128],
                        in1=pQM[:, 128:256], op=ALU.mult), inc=True,
                        waits=[("tensor", ebc)])
                    ktr = 0
                    for half in range(2):
                        ktr = s.op("tensor", lambda h, half=half: h.transpose(
                            pSMb[0:64, half * 128:half * 128 + 128],
                            kescr[:, half * 64:half * 64 + 64], c_id[:]),
                            inc=True,
                            waits=[("vector", kes)] if half == 0 else ())
                    for half, ketX in enumerate((ketL, ketH)):
                        s.op("vector", lambda h, hh=hh, half=half, ketX=ketX:
                             h.tensor_copy(
                                 out=ketX[:, hh * 128:hh * 128 + 128],
                                 in_=pSMb[0:64, half * 128:half * 128 + 128]),
                             inc=(half == 1),
                             waits=[("tensor", ktr)] if half == 0 else ())
                    s.barrier("tensor", ["vector"])
                prep_done = s.now("vector")

                for hh in range(2):
                    s.op("tensor", lambda h, hh=hh, ao=ao: h.matmul(
                        pA[:, hh * 128:hh * 128 + 128],
                        s_op[:, hh * 128:hh * 128 + 128],
                        kn_b[:, ao + hh * 128:ao + hh * 128 + 128],
                        start=(hh == 0), stop=True),
                        waits=[("vector", prep_done)] if hh == 0 else ())
                    s.op("tensor", lambda h, hh=hh, ao=ao: h.matmul(
                        pB[:, hh * 128:hh * 128 + 128],
                        s_op[:, hh * 128:hh * 128 + 128],
                        qs_b[:, ao + hh * 128:ao + hh * 128 + 128],
                        start=(hh == 0), stop=True))
                _ = s.now("tensor")

                last_stk = s.now("vector")
                for j in range(NCH):
                    m = j // 2
                    c0 = j * CH
                    if j > 0:
                        KL = min(16 * j, 64)
                        KH = 16 * j - KL
                        for hh in range(2):
                            s.op("tensor", lambda h, hh=hh, KL=KL, c0=c0: h.matmul(
                                pA[:, hh * 128 + c0: hh * 128 + c0 + CH],
                                vnstkL[0:KL, hh * 128:hh * 128 + 128],
                                aktL[0:KL, hh * 128 + c0:hh * 128 + c0 + CH],
                                start=False, stop=True),
                                waits=[("vector", last_stk)] if hh == 0 else ())
                            if KH > 0:
                                s.op("tensor", lambda h, hh=hh, KH=KH, c0=c0:
                                     h.matmul(
                                         pA[:, hh * 128 + c0:
                                            hh * 128 + c0 + CH],
                                         vnstkH[0:KH, hh * 128:hh * 128 + 128],
                                         aktH[0:KH,
                                              hh * 128 + c0:hh * 128 + c0 + CH],
                                         start=False, stop=True))
                    corr_pe = s.now("tensor")

                    s.op("vector", lambda h, c0=c0: h.tensor_copy(
                        out=scrA[:, 0:16], in_=pA[:, c0:c0 + CH]),
                        waits=[("tensor", corr_pe)])
                    s.op("vector", lambda h, c0=c0: h.tensor_copy(
                        out=scrA[:, 16:32], in_=pA[:, 128 + c0:128 + c0 + CH]))
                    kh_ev = s.op("vector", lambda h: h.tensor_tensor(
                        out=scrA[:, 32:64], in0=scrA[:, 0:32], in1=scrA[:, 0:32],
                        op=ALU.mult), inc=True)
                    st1 = s.op("tensor", lambda h: h.matmul(
                        pSM[:, 128:192], c_ones[:], scrA[:, 0:64],
                        start=True, stop=True), inc=True, waits=[("vector", kh_ev)])
                    s.op("vector", lambda h: h.tensor_copy(
                        out=scr2[:], in_=pSM[:, 128:160]), waits=[("tensor", st1)])
                    s.op("vector", lambda h: h.tensor_tensor(
                        out=scr1[:], in0=scr2[:], in1=scr2[:], op=ALU.mult))
                    vpre = s.op("vector", lambda h: h.scalar_tensor_tensor(
                        out=scr1[:], in0=scr1[:], scalar=-1.0 / DH,
                        in1=pSM[:, 160:192], op0=ALU.mult, op1=ALU.add), inc=True)
                    s.op("scalar", lambda h: h.activation(
                        out=scr2[:], in_=scr1[:], func=AF.Ln, scale=1.0 / DH,
                        bias=c_epsA[:]), waits=[("vector", vpre)])
                    a_rs = s.op("scalar", lambda h: h.activation(
                        out=rstdc[:], in_=scr2[:], func=AF.Exp, scale=-0.5),
                        inc=True)
                    s.op("vector", lambda h: h.scalar_tensor_tensor(
                        out=khh[:], in0=pSM[:, 128:160], scalar=-1.0 / DH,
                        in1=scrA[:, 0:32], op0=ALU.mult, op1=ALU.add),
                        waits=[("scalar", a_rs)])
                    s.op("vector", lambda h: h.tensor_tensor(
                        out=khh[:], in0=khh[:], in1=rstdc[:], op=ALU.mult))
                    s.op("vector", lambda h: h.tensor_tensor(
                        out=scrA[:, 64:96], in0=khh[:], in1=c_w2B[:], op=ALU.mult))
                    s.op("vector", lambda h, c0=c0: h.tensor_tensor(
                        out=scrA[:, 64:80], in0=scrA[:, 64:80],
                        in1=wbm[:, c0:c0 + CH], op=ALU.add))
                    s.op("vector", lambda h, c0=c0: h.tensor_tensor(
                        out=scrA[:, 80:96], in0=scrA[:, 80:96],
                        in1=wbm[:, 128 + c0:128 + c0 + CH], op=ALU.add))
                    gw_d = s.op("vector", lambda h: h.tensor_tensor(
                        out=scrA[:, 96:128], in0=scrA[:, 64:96], in1=khh[:],
                        op=ALU.mult), inc=True)
                    st2 = s.op("tensor", lambda h: h.matmul(
                        pSM[:, 192:256], c_ones[:], scrA[:, 64:128],
                        start=True, stop=True), inc=True, waits=[("vector", gw_d)])
                    s.op("vector", lambda h: h.scalar_tensor_tensor(
                        out=scr1[:], in0=khh[:], scalar=1.0 / DH,
                        in1=pSM[:, 224:256], op0=ALU.mult, op1=ALU.mult),
                        waits=[("tensor", st2)])
                    s.op("vector", lambda h: h.tensor_tensor(
                        out=scr1[:], in0=scrA[:, 64:96], in1=scr1[:],
                        op=ALU.subtract))
                    s.op("vector", lambda h: h.scalar_tensor_tensor(
                        out=scr1[:], in0=pSM[:, 192:224], scalar=-1.0 / DH,
                        in1=scr1[:], op0=ALU.mult, op1=ALU.add))
                    s.op("vector", lambda h: h.tensor_tensor(
                        out=scr2[:], in0=scr1[:], in1=rstdc[:], op=ALU.mult))
                    po = (j % 2) * 16
                    s.op("vector", lambda h, po=po: h.tensor_copy(
                        out=vnstg[:, po:po + 16], in_=scr2[:, 0:16]))
                    vn_d = s.op("vector", lambda h, po=po: h.tensor_copy(
                        out=vnstg[:, 32 + po:32 + po + 16], in_=scr2[:, 16:32]),
                        inc=True)

                    vstk = vnstkL if m < 2 else vnstkH
                    r0 = 32 * (m % 2)
                    nr = 16 if j % 2 == 0 else 32
                    src0 = 0 if j % 2 == 0 else 0
                    trs = 0
                    for hh in range(2):
                        trs = s.op("tensor", lambda h, hh=hh, r0=r0, nr=nr:
                                   h.transpose(
                                       pSMb[r0:r0 + nr,
                                            hh * 128:hh * 128 + 128],
                                       vnstg[:, hh * 32:hh * 32 + nr], c_id[:]),
                                   inc=True,
                                   waits=[("vector", vn_d)] if hh == 0 else ())
                    for hh in range(2):
                        s.op("vector", lambda h, hh=hh, r0=r0, nr=nr, vstk=vstk:
                             h.tensor_copy(
                                 out=vstk[r0:r0 + nr, hh * 128:hh * 128 + 128],
                                 in_=pSMb[r0:r0 + nr, hh * 128:hh * 128 + 128]),
                             inc=(hh == 1),
                             waits=[("tensor", trs)] if hh == 0 else ())
                    last_stk = s.now("vector")
                    s.barrier("tensor", ["vector"])

                for hh in range(2):
                    s.op("tensor", lambda h, hh=hh: h.matmul(
                        pB[:, hh * 128:hh * 128 + 128],
                        vnstkL[:, hh * 128:hh * 128 + 128],
                        agtL[:, hh * 128:hh * 128 + 128], start=False, stop=False),
                        waits=[("vector", last_stk)] if hh == 0 else ())
                    s.op("tensor", lambda h, hh=hh: h.matmul(
                        pB[:, hh * 128:hh * 128 + 128],
                        vnstkH[:, hh * 128:hh * 128 + 128],
                        agtH[:, hh * 128:hh * 128 + 128], start=False, stop=True))
                    s.op("tensor", lambda h, hh=hh, g=g: h.matmul(
                        pS[:, hh * 128:hh * 128 + 128],
                        ketL[:, hh * 128:hh * 128 + 128],
                        vnstkL[:, hh * 128:hh * 128 + 128],
                        start=(g == 0 and hh == 0), stop=False))
                    s.op("tensor", lambda h, hh=hh, g=g: h.matmul(
                        pS[:, hh * 128:hh * 128 + 128],
                        ketH[:, hh * 128:hh * 128 + 128],
                        vnstkH[:, hh * 128:hh * 128 + 128],
                        start=False, stop=True))
                qh_pe = s.now("tensor")

                for hh in range(2):
                    qf = qs_f[:, ao + hh * 128: ao + hh * 128 + 128]
                    s.op("vector", lambda h, hh=hh: h.tensor_copy(
                        out=oscr[:, 0:128], in_=pB[:, hh * 128:hh * 128 + 128]),
                        waits=[("tensor", qh_pe)] if hh == 0 else ())
                    o_ev = s.op("vector", lambda h: h.tensor_tensor(
                        out=oscr[:, 128:256], in0=oscr[:, 0:128],
                        in1=oscr[:, 0:128], op=ALU.mult), inc=True)
                    o_mm = s.op("tensor", lambda h: h.matmul(
                        pSM[:, 256:512], c_ones[:], oscr[:, 0:256],
                        start=True, stop=True), inc=True, waits=[("vector", o_ev)])
                    s.op("vector", lambda h: h.tensor_copy(
                        out=oo[:], in_=pSM[:, 256:384]), waits=[("tensor", o_mm)])
                    s.op("vector", lambda h: h.tensor_tensor(
                        out=oo[:], in0=oo[:], in1=oo[:], op=ALU.mult))
                    o_v = s.op("vector", lambda h: h.scalar_tensor_tensor(
                        out=oo[:], in0=oo[:], scalar=-1.0 / DH,
                        in1=pSM[:, 384:512], op0=ALU.mult, op1=ALU.add), inc=True)
                    s.op("scalar", lambda h: h.activation(
                        out=oo2[:], in_=oo[:], func=AF.Ln, scale=1.0 / DH,
                        bias=c_epsA[:]), waits=[("vector", o_v)])
                    o_rs = s.op("scalar", lambda h: h.activation(
                        out=oo2[:], in_=oo2[:], func=AF.Exp, scale=-0.5), inc=True)
                    s.op("vector", lambda h: h.scalar_tensor_tensor(
                        out=oo[:], in0=pSM[:, 256:384], scalar=-1.0 / DH,
                        in1=oscr[:, 0:128], op0=ALU.mult, op1=ALU.add),
                        waits=[("scalar", o_rs)])
                    s.op("vector", lambda h: h.tensor_tensor(
                        out=oo[:], in0=oo[:], in1=oo2[:], op=ALU.mult))
                    s.op("vector", lambda h, hh=hh: h.tensor_scalar(
                        out=oo[:], in0=oo[:], scalar1=c_lnw[:, hh:hh + 1],
                        scalar2=c_lnb[:, hh:hh + 1], op0=ALU.mult, op1=ALU.add))
                    o_d = s.op("vector", lambda h, qf=qf: h.tensor_tensor(
                        out=oo[:], in0=oo[:], in1=qf, op=ALU.add), inc=True)
                    g_mm = s.op("tensor", lambda h, hh=hh, sg=sg: h.matmul(
                        pQM[:, 384:512], wg2b[:, hh * 128:hh * 128 + 128],
                        g1_b[:, sg * 128:sg * 128 + 128], start=True, stop=True),
                        inc=True, waits=[("vector", o_d), ("vector", guard)])
                    s.op("scalar", lambda h: h.activation(
                        out=oth[:], in_=pQM[:, 384:512], func=AF.Exp, scale=-1.0),
                        waits=[("tensor", g_mm)])
                    s.op("scalar", lambda h: h.activation(
                        out=oth[:], in_=oth[:], func=AF.Ln, scale=1.0,
                        bias=c_onesc[:]))
                    g_th = s.op("scalar", lambda h: h.activation(
                        out=oth[:], in_=oth[:], func=AF.Exp, scale=-1.0), inc=True)
                    og_d = s.op("vector", lambda h: h.tensor_tensor(
                        out=oo[:], in0=oo[:], in1=oth[:], op=ALU.mult), inc=True,
                        waits=[("scalar", g_th)])
                    og_sq = s.op("scalar", lambda h: h.activation(
                        out=oo2[:], in_=oo[:], func=AF.Square), inc=True,
                        waits=[("vector", og_d)])
                    r_mm = s.op("tensor", lambda h: h.matmul(
                        pQM[0:1, 128:256], c_onesc[:], oo2[:],
                        start=True, stop=True), inc=True,
                        waits=[("scalar", og_sq)])
                    s.op("scalar", lambda h: h.activation(
                        out=strow[0:1, 512:640], in_=pQM[0:1, 128:256], func=AF.Ln,
                        scale=1.0 / DH, bias=c_epsA[0:1, :]),
                        waits=[("tensor", r_mm)])
                    r_rs = s.op("scalar", lambda h: h.activation(
                        out=strow[0:1, 512:640], in_=strow[0:1, 512:640],
                        func=AF.Exp, scale=-0.5), inc=True)
                    r_bc = s.op("tensor", lambda h: h.matmul(
                        pQM[:, 128:256], c_onesrow[0:1, :], strow[0:1, 512:640],
                        start=True, stop=True), inc=True, waits=[("scalar", r_rs)])
                    s.op("vector", lambda h: h.tensor_tensor(
                        out=oo[:], in0=oo[:], in1=pQM[:, 128:256], op=ALU.mult),
                        waits=[("tensor", r_bc)])
                    ob_d = s.op("vector", lambda h, hh=hh: h.tensor_scalar(
                        out=obuf[:, hh * 128:hh * 128 + 128], in0=oo[:],
                        scalar1=c_nw[:, hh:hh + 1], scalar2=None, op0=ALU.mult),
                        inc=True)
                    s.barrier("tensor", ["vector"])
                    s.barrier("scalar", ["vector"])
                    s.op("sync", lambda h, g=g, hh=hh: h.dma_start(
                        out=o_st5[g // NLG, g % NLG, hh, :, :],
                        in_=obuf[:, hh * 128:hh * 128 + 128]), inc=True,
                        waits=[("vector", ob_d)])
                scan_done[g] = s.now("vector")
                s.barrier("vector", ["sync"])

            for g in range(NG + 1):
                if g >= 1:
                    scan_group(g - 1)
                if g < NG:
                    qkv_group(g)

            s.barrier("gpsimd", ["sync", "vector", "scalar", "tensor"])

        import os
        if os.environ.get("KERN_NO_CC"):
            def do_cc(h):
                return h.dma_start(out=o_gath.ap(), in_=o_stage.ap()).then_inc(
                    sems["cc"], 16)
        else:
            def do_cc(h):
                with allow_ici_all_to_all():
                    return h.collective_compute(
                        "AllToAll", ALU.bypass,
                        replica_groups=[list(range(NCORE))],
                        ins=[o_stage.ap().opt()],
                        outs=[o_gath.ap().opt()]).then_inc(sems["cc"], 1)
        s.ops["gpsimd"].append((do_cc, False, ()))
        s.barrier("sync", ["gpsimd"])
        import os as _os
        _ccv = 16 if _os.environ.get("KERN_NO_CC") else 1
        s.ops["sync"].append((None, False, (("cc", _ccv),)))
        s.ops["tensor"].append((None, False, (("cc", _ccv),)))

        zb = sb("zb", [128, 256], BF16)
        zsq = sb("zsq", [128, 256], BF16)
        strow2 = sb("strow2", [1, 8 * 256], F32)
        rB1 = sb("rB1", [128, 256], F32)
        csc1 = sb("csc1", [128, 256], F32)
        csc2 = sb("csc2", [128, 256], F32)
        x1t = sb("x1t", [128, 256], BF16)
        gscb = sb("gscb", [128, 48], F32)
        uscb = sb("uscb", [128, 48], F32)
        dscb = sb("dscb", [128, 16], F32)
        ngs = sb("ngs", [128, 48], F32)
        esX = contextlib.ExitStack()
        x2b = esX.enter_context(nc.sbuf_tensor("x2b", [128, 16 * Tc], BF16, side="right"))
        fuse = sb("fuse", [128, 48 * Tc], BF16)

        s.op("sync", lambda h: h.dma_start(out=gscb[:], in_=gsc), inc=True)
        s.op("sync", lambda h: h.dma_start(out=uscb[:], in_=usc), inc=True)
        s.op("sync", lambda h: h.dma_start(out=dscb[:], in_=dsc), inc=True)
        ldsc = s.now("sync")
        hgs_d = s.op("vector", lambda h: h.tensor_scalar(
            out=ngs[:], in0=gscb[:], scalar1=-1.0, scalar2=None, op0=ALU.mult),
            inc=True, waits=[("sync", ldsc)])

        def ln_stats_and_bcast(src_psA, src_psB, dd):
            s.op("vector", lambda h: h.tensor_copy(
                out=strow2[0:1, 3 * TS:4 * TS], in_=src_psA[0:1, 0:TS]), waits=dd)
            v1 = s.op("vector", lambda h: h.tensor_tensor(
                out=strow2[0:1, 0:TS], in0=strow2[0:1, 3 * TS:4 * TS],
                in1=strow2[0:1, 3 * TS:4 * TS], op=ALU.mult), inc=True)
            v2 = s.op("vector", lambda h: h.scalar_tensor_tensor(
                out=strow2[0:1, 0:TS], in0=strow2[0:1, 0:TS], scalar=-1.0 / D,
                in1=src_psB[0:1, 0:TS], op0=ALU.mult, op1=ALU.add), inc=True)
            s.op("scalar", lambda h: h.activation(
                out=strow2[0:1, TS:2 * TS], in_=strow2[0:1, 0:TS], func=AF.Ln,
                scale=1.0 / D, bias=c_epsM[0:1, :]), waits=[("vector", v2)])
            a1 = s.op("scalar", lambda h: h.activation(
                out=strow2[0:1, TS:2 * TS], in_=strow2[0:1, TS:2 * TS],
                func=AF.Exp, scale=-0.5), inc=True)
            v3 = s.op("vector", lambda h: h.tensor_scalar(
                out=strow2[0:1, 2 * TS:3 * TS], in0=src_psA[0:1, 0:TS],
                scalar1=1.0 / D, scalar2=None, op0=ALU.mult), inc=True)
            s.op("tensor", lambda h: h.matmul(
                pbank[4][:, 0:TS], c_onesrow[0:1, :], strow2[0:1, TS:2 * TS],
                start=True, stop=True), waits=[("scalar", a1), ("vector", v3)])
            b1 = s.op("tensor", lambda h: h.matmul(
                pbank[5][:, 0:TS], c_onesrow[0:1, :], strow2[0:1, 2 * TS:3 * TS],
                start=True, stop=True), inc=True)
            s.op("vector", lambda h: h.tensor_copy(
                out=rB1[:, 0:TS], in_=pbank[4][:, 0:TS]), waits=[("tensor", b1)])
            vb = s.op("vector", lambda h: h.tensor_copy(
                out=csc1[:, 0:TS], in_=pbank[5][:, 0:TS]), inc=True)
            s.barrier("tensor", ["vector"])
            return vb

        esC1 = contextlib.ExitStack()
        with esC1:
            oslab = esC1.enter_context(nc.sbuf_tensor("oslab", [128, 16 * TS], BF16))
            xinb = esC1.enter_context(nc.sbuf_tensor("xinb", [128, 16 * TS], F32))
            zslab = esC1.enter_context(nc.sbuf_tensor("zslab", [128, 16 * TS], F32))
            wslab = esC1.enter_context(nc.sbuf_tensor("wslab", [128, 2 * 2048], BF16))

            for sl in range(NSL):
                t0 = sl * TS
                nlg4 = TS // GT
                if sl > 0:
                    s.barrier("sync", ["tensor", "vector"])
                for src in range(NCORE):
                    for hh in range(2):
                        dti = src * 2 + hh
                        s.op("sync", lambda h, src=src, hh=hh, dti=dti, sl=sl,
                             nlg4=nlg4: h.dma_start(
                                 out=oslab[:, dti * TS:dti * TS + TS].rearrange(
                                     "p (l c) -> p l c", l=nlg4),
                                 in_=o_gt5[src, sl * nlg4:(sl + 1) * nlg4,
                                           hh].rearrange("l p c -> p l c")),
                             inc=True)
                s.op("sync", lambda h, t0=t0: h.dma_start(
                    out=xinb[:].rearrange("p (d c) -> p d c", d=16),
                    in_=xres[:, :, t0:t0 + TS]), inc=True)
                lds = s.now("sync")

                pe_prev = {}
                wo_ld = {}
                for t in range(2):
                    if sl == 0 and t == 0:
                        wv = [("sync", ld0)]
                    elif t == 0:
                        wv = [("tensor", s.now("tensor"))]
                    else:
                        wv = ()
                    s.op("gpsimd", lambda h, t=t: h.dma_start(
                        out=wslab[:, (t % 2) * 2048:(t % 2) * 2048 + 2048],
                        in_=wo[t]), inc=True, waits=wv)
                    wo_ld[t] = s.now("gpsimd")
                bank_rd = {0: 0, 1: 0}
                for t in range(16):
                    wb_off = (t % 2) * 2048
                    pq = pbank[t % 2]
                    for d in range(16):
                        wv = [("gpsimd", wo_ld[t]), ("sync", lds),
                              ("vector", bank_rd[t % 2])] if d == 0 else ()
                        s.op("tensor", lambda h, d=d, pq=pq, wb_off=wb_off:
                             h.matmul(pq[:, 0:TS],
                                      wslab[:, wb_off + d * 128:wb_off + d * 128 + 128],
                                      oslab[:, d * TS:d * TS + TS],
                                      start=(d == 0), stop=(d == 15)), waits=wv)
                    pe_t = s.now("tensor")
                    pe_prev[t] = pe_t
                    if t + 2 < 16:
                        s.op("gpsimd", lambda h, t=t: h.dma_start(
                            out=wslab[:, (t % 2) * 2048:(t % 2) * 2048 + 2048],
                            in_=wo[t + 2]), inc=True, waits=[("tensor", pe_t)])
                        wo_ld[t + 2] = s.now("gpsimd")
                    s.op("vector", lambda h, t=t, pq=pq: h.tensor_copy(
                        out=zslab[:, t * TS:t * TS + TS], in_=pq[:, 0:TS]),
                        waits=[("tensor", pe_t)])
                    z_ev = s.op("vector", lambda h, t=t: h.tensor_copy(
                        out=zb[:, 0:TS], in_=zslab[:, t * TS:t * TS + TS]),
                        inc=True)
                    bank_rd[t % 2] = z_ev
                    z_sq = s.op("scalar", lambda h: h.activation(
                        out=zsq[:, 0:TS], in_=zb[:, 0:TS], func=AF.Square),
                        inc=True, waits=[("vector", z_ev)])
                    s.op("tensor", lambda h, t=t: h.matmul(
                        pbank[2][0:1, 0:TS], c_onesb[:], zb[:, 0:TS],
                        start=(t == 0), stop=(t == 15)), waits=[("vector", z_ev)])
                    s.op("tensor", lambda h, t=t: h.matmul(
                        pbank[3][0:1, 0:TS], c_onesb[:], zsq[:, 0:TS],
                        start=(t == 0), stop=(t == 15)), waits=[("scalar", z_sq)])
                    s.barrier("vector", ["tensor"])
                    s.barrier("scalar", ["tensor"])
                st_pe = s.now("tensor")

                ln_stats_and_bcast(pbank[2], pbank[3], [("tensor", st_pe)])

                for t in range(16):
                    s.op("vector", lambda h, t=t: h.tensor_tensor(
                        out=csc2[:, 0:TS], in0=zslab[:, t * TS:t * TS + TS],
                        in1=csc1[:, 0:TS], op=ALU.subtract))
                    s.op("vector", lambda h: h.tensor_tensor(
                        out=csc2[:, 0:TS], in0=csc2[:, 0:TS], in1=rB1[:, 0:TS],
                        op=ALU.mult))
                    s.op("vector", lambda h, t=t: h.tensor_tensor(
                        out=csc2[:, 0:TS], in0=csc2[:, 0:TS],
                        in1=xinb[:, t * TS:t * TS + TS], op=ALU.add))
                    x1d = s.op("vector", lambda h: h.tensor_copy(
                        out=x1t[:, 0:TS], in_=csc2[:, 0:TS]), inc=True)
                    s.op("sync", lambda h, t=t, t0=t0: h.dma_start(
                        out=x1_dram.ap()[:, t, t0:t0 + TS], in_=x1t[:, 0:TS]),
                        inc=True, waits=[("vector", x1d)])
                    s.barrier("vector", ["sync"])
                    x1sq = s.op("scalar", lambda h: h.activation(
                        out=zsq[:, 0:TS], in_=x1t[:, 0:TS], func=AF.Square),
                        inc=True, waits=[("vector", x1d)])
                    s.op("tensor", lambda h, t=t: h.matmul(
                        pbank[2][0:1, 0:TS], c_onesb[:], x1t[:, 0:TS],
                        start=(t == 0), stop=(t == 15)), waits=[("vector", x1d)])
                    s.op("tensor", lambda h, t=t: h.matmul(
                        pbank[3][0:1, 0:TS], c_onesb[:], zsq[:, 0:TS],
                        start=(t == 0), stop=(t == 15)), waits=[("scalar", x1sq)])
                    s.op("vector", lambda h, t=t: h.tensor_copy(
                        out=zslab[:, t * TS:t * TS + TS], in_=csc2[:, 0:TS]),
                        inc=True)
                    s.barrier("vector", ["tensor"])
                    s.barrier("scalar", ["vector"])
                st2_pe = s.now("tensor")

                ln_stats_and_bcast(pbank[2], pbank[3], [("tensor", st2_pe)])

                for t in range(16):
                    s.op("vector", lambda h, t=t: h.tensor_tensor(
                        out=csc2[:, 0:TS], in0=zslab[:, t * TS:t * TS + TS],
                        in1=csc1[:, 0:TS], op=ALU.subtract))
                    s.op("vector", lambda h, t=t, t0=t0: h.tensor_tensor(
                        out=x2b[:, t * Tc + t0: t * Tc + t0 + TS],
                        in0=csc2[:, 0:TS], in1=rB1[:, 0:TS], op=ALU.mult),
                        inc=(t == 15))
            x2_done = s.now("vector")

        esC2 = contextlib.ExitStack()
        with esC2:
            f8a = esC2.enter_context(nc.sbuf_tensor("f8a", [128, 2048], F8))
            f8b = esC2.enter_context(nc.sbuf_tensor("f8b", [128, 2048], F8))
            gslab = esC2.enter_context(nc.sbuf_tensor("gslab", [128, 2 * 2048], BF16))
            uslab = esC2.enter_context(nc.sbuf_tensor("uslab", [128, 2 * 2048], BF16))
            gth = esC2.enter_context(nc.sbuf_tensor("gth", [128, NDT], F32))
            gxh = esC2.enter_context(nc.sbuf_tensor("gxh", [128, NDT], F32))
            gv = esC2.enter_context(nc.sbuf_tensor("gv", [128, NDT], F32))
            uv = esC2.enter_context(nc.sbuf_tensor("uv", [128, NDT], F32))

            s.barrier("gpsimd", ["tensor", "vector", "scalar", "sync"])
            gu_pe_done = {}
            for ht in range(48):
                sl2 = ht % 2
                wv = [("tensor", gu_pe_done[ht - 2])] if ht >= 2 else ()
                s.op("gpsimd", lambda h, ht=ht: h.dma_start(
                    out=f8a[:], in_=gwt[ht]), inc=True, waits=wv)
                s.op("gpsimd", lambda h, ht=ht: h.dma_start(
                    out=f8b[:], in_=uwt[ht]), inc=True)
                w8 = s.now("gpsimd")
                s.op("gpsimd", lambda h, sl2=sl2: h.dma_start(
                    out=gslab[:, sl2 * 2048:sl2 * 2048 + 2048], in_=f8a[:]),
                    inc=True, waits=[("gpsimd", w8)])
                s.op("gpsimd", lambda h, sl2=sl2: h.dma_start(
                    out=uslab[:, sl2 * 2048:sl2 * 2048 + 2048], in_=f8b[:]),
                    inc=True)
                wbf = s.now("gpsimd")
                for nb in range(ND):
                    n0 = nb * NDT
                    for d in range(16):
                        wv = [("gpsimd", wbf), ("vector", x2_done),
                              ("vector", s.now("vector")),
                              ("scalar", s.now("scalar"))] if d == 0 else ()
                        s.op("tensor", lambda h, d=d, sl2=sl2, n0=n0: h.matmul(
                            pbank[0][:, 0:NDT],
                            gslab[:, sl2 * 2048 + d * 128:sl2 * 2048 + d * 128 + 128],
                            x2b[:, d * Tc + n0: d * Tc + n0 + NDT],
                            start=(d == 0), stop=(d == 15)), waits=wv)
                    for d in range(16):
                        s.op("tensor", lambda h, d=d, sl2=sl2, n0=n0: h.matmul(
                            pbank[1][:, 0:NDT],
                            uslab[:, sl2 * 2048 + d * 128:sl2 * 2048 + d * 128 + 128],
                            x2b[:, d * Tc + n0: d * Tc + n0 + NDT],
                            start=(d == 0), stop=(d == 15)))
                    gu_pe = s.now("tensor")
                    gu_pe_done[ht] = gu_pe
                    s.op("scalar", lambda h, ht=ht: h.activation(
                        out=gth[:], in_=pbank[0][:, 0:NDT], func=AF.Exp,
                        scale=ngs[:, ht:ht + 1]),
                        waits=[("tensor", gu_pe), ("vector", hgs_d)])
                    s.op("scalar", lambda h: h.activation(
                        out=gth[:], in_=gth[:], func=AF.Ln, scale=1.0,
                        bias=c_onesc[:]))
                    gt_d = s.op("scalar", lambda h: h.activation(
                        out=gth[:], in_=gth[:], func=AF.Exp, scale=-1.0),
                        inc=True)
                    s.op("vector", lambda h, ht=ht: h.tensor_scalar(
                        out=gxh[:], in0=pbank[0][:, 0:NDT],
                        scalar1=gscb[:, ht:ht + 1], scalar2=None, op0=ALU.mult),
                        waits=[("scalar", gt_d)])
                    s.op("vector", lambda h: h.tensor_tensor(
                        out=gv[:], in0=gth[:], in1=gxh[:], op=ALU.mult))
                    s.op("vector", lambda h, ht=ht: h.tensor_scalar(
                        out=uv[:], in0=pbank[1][:, 0:NDT],
                        scalar1=uscb[:, ht:ht + 1], scalar2=None, op0=ALU.mult))
                    s.op("vector", lambda h, ht=ht, n0=n0: h.tensor_tensor(
                        out=fuse[:, ht * Tc + n0: ht * Tc + n0 + NDT], in0=gv[:],
                        in1=uv[:], op=ALU.mult), inc=True)
                    s.barrier("tensor", ["vector"])
                    s.barrier("scalar", ["vector"])
            fuse_done = s.now("vector")
        esX.close()

        esC3 = contextlib.ExitStack()
        with esC3:
            dslab = esC3.enter_context(
                nc.sbuf_tensor("dslab", [128, 2 * 48 * 128], BF16))
            f8d = esC3.enter_context(nc.sbuf_tensor("f8d", [128, 48 * 128], F8))
            dnslab = esC3.enter_context(
                nc.sbuf_tensor("dnslab", [128, 16 * Tc], BF16))
            x1rb = esC3.enter_context(nc.sbuf_tensor("x1rb", [128, 16 * TS], BF16))
            outs = esC3.enter_context(nc.sbuf_tensor("outs", [128, 16 * TS], F32))

            s.barrier("gpsimd", ["tensor", "vector", "scalar", "sync"])
            dn_pe_done = {}
            for t in range(16):
                sl2 = t % 2
                wv = [("tensor", dn_pe_done[t - 2])] if t >= 2 else ()
                s.op("gpsimd", lambda h, t=t: h.dma_start(out=f8d[:], in_=dwt[t]),
                     inc=True, waits=wv)
                f8_d = s.now("gpsimd")
                s.op("gpsimd", lambda h, sl2=sl2: h.dma_start(
                    out=dslab[:, sl2 * 6144:sl2 * 6144 + 6144], in_=f8d[:]),
                    inc=True, waits=[("gpsimd", f8_d)])
                dw_d = s.now("gpsimd")
                for nb in range(ND):
                    n0 = nb * NDT
                    for dd in range(48):
                        wv = [("gpsimd", dw_d), ("vector", fuse_done),
                              ("vector", s.now("vector"))] \
                            if dd == 0 else ()
                        s.op("tensor", lambda h, dd=dd, sl2=sl2, n0=n0, nb=nb:
                             h.matmul(
                                 pbank[nb % 2][:, 0:NDT],
                                 dslab[:, sl2 * 6144 + dd * 128:
                                       sl2 * 6144 + dd * 128 + 128],
                                 fuse[:, dd * Tc + n0: dd * Tc + n0 + NDT],
                                 start=(dd == 0), stop=(dd == 47)), waits=wv)
                    dn_pe = s.now("tensor")
                    dn_pe_done[t] = dn_pe
                    s.op("vector", lambda h, t=t, n0=n0, nb=nb: h.tensor_scalar(
                        out=dnslab[:, t * Tc + n0: t * Tc + n0 + NDT],
                        in0=pbank[nb % 2][:, 0:NDT], scalar1=dscb[:, t:t + 1],
                        scalar2=None, op0=ALU.mult), inc=True,
                        waits=[("tensor", dn_pe), ("sync", ldsc)])
                    s.barrier("tensor", ["vector"])
            dn_done = s.now("vector")

            for nb in range(NSL):
                t0 = nb * TS
                if nb > 0:
                    s.barrier("sync", ["vector"])
                s.op("sync", lambda h, t0=t0: h.dma_start(
                    out=x1rb[:].rearrange("p (d c) -> p d c", d=16),
                    in_=x1_dram.ap()[:, :, t0:t0 + TS]), inc=True,
                    waits=[("vector", dn_done)] if nb == 0 else ())
                x1r_d = s.now("sync")
                for t in range(16):
                    z_ev = s.op("vector", lambda h, t=t, t0=t0: h.tensor_copy(
                        out=zb[:, 0:TS],
                        in_=dnslab[:, t * Tc + t0: t * Tc + t0 + TS]), inc=True)
                    z_sq = s.op("scalar", lambda h: h.activation(
                        out=zsq[:, 0:TS], in_=zb[:, 0:TS], func=AF.Square),
                        inc=True, waits=[("vector", z_ev)])
                    s.op("tensor", lambda h, t=t: h.matmul(
                        pbank[2][0:1, 0:TS], c_onesb[:], zb[:, 0:TS],
                        start=(t == 0), stop=(t == 15)), waits=[("vector", z_ev)])
                    s.op("tensor", lambda h, t=t: h.matmul(
                        pbank[3][0:1, 0:TS], c_onesb[:], zsq[:, 0:TS],
                        start=(t == 0), stop=(t == 15)), waits=[("scalar", z_sq)])
                    s.barrier("vector", ["tensor"])
                    s.barrier("scalar", ["tensor"])
                st3 = s.now("tensor")

                ln_stats_and_bcast(pbank[2], pbank[3], [("tensor", st3)])

                for t in range(16):
                    s.op("vector", lambda h, t=t, t0=t0: h.tensor_tensor(
                        out=csc2[:, 0:TS],
                        in0=dnslab[:, t * Tc + t0: t * Tc + t0 + TS],
                        in1=csc1[:, 0:TS], op=ALU.subtract))
                    s.op("vector", lambda h: h.tensor_tensor(
                        out=csc2[:, 0:TS], in0=csc2[:, 0:TS], in1=rB1[:, 0:TS],
                        op=ALU.mult))
                    ov = s.op("vector", lambda h, t=t: h.tensor_tensor(
                        out=csc2[:, 0:TS], in0=csc2[:, 0:TS],
                        in1=x1rb[:, t * TS:t * TS + TS], op=ALU.add), inc=True,
                        waits=[("sync", x1r_d)] if t == 0 else ())
                    for q in range(TS // 128):
                        trq = s.op("tensor", lambda h, q=q: h.transpose(
                            pbank[6][:, 0:128], csc2[:, q * 128:q * 128 + 128],
                            c_idf[:]), inc=True,
                            waits=[("vector", ov)] if q == 0 else ())
                        otd = s.op("vector", lambda h, t=t, q=q: h.tensor_copy(
                            out=outs[:, t * TS + q * 128:t * TS + q * 128 + 128],
                            in_=pbank[6][:, 0:128]), inc=True,
                            waits=[("tensor", trq)])
                        s.barrier("tensor", ["vector"])
                        s.op("sync", lambda h, t=t, q=q, t0=t0: h.dma_start(
                            out=out[t0 + q * 128: t0 + q * 128 + 128,
                                    t * 128:t * 128 + 128],
                            in_=outs[:, t * TS + q * 128:t * TS + q * 128 + 128]),
                            inc=True, waits=[("vector", otd)])
                s.barrier("vector", ["sync"])
            fin = s.now("sync")
            s.ops["sync"].append((None, False, (("sync", fin),)))

        s.emit(block, sems)
    return nc


def stage_inputs(inputs, T):
    Tc = T // NCORE
    x = np.ascontiguousarray(inputs["L0_attn_x"][0, :T].T.astype(np.float32))
    xT_s = np.ascontiguousarray(x.reshape(16, 128, T).transpose(1, 0, 2))

    Wq, Wk, Wv = (inputs["L0_attn_Wq"], inputs["L0_attn_Wk"], inputs["L0_attn_Wv"])
    Wbeta = inputs["L0_attn_Wbeta"]
    Wg1 = inputs["L0_attn_Wg1"]
    Wo_ = inputs["L0_attn_Wo"]
    Wg2_ = inputs["L0_attn_Wg2"]
    lnw, lnb = inputs["L0_attn_ln_w"], inputs["L0_attn_ln_b"]
    nw = inputs["L0_attn_norm_w"]
    gw = inputs["L0_ffn_gate_w"]
    uw = inputs["L0_ffn_up_w"]
    dw = inputs["L0_ffn_down_w"]

    wo_s = np.ascontiguousarray(
        Wo_.reshape(16, 128, 16, 128).transpose(2, 1, 0, 3)).reshape(16, 128, -1)
    gwt_s = np.ascontiguousarray(
        gw.reshape(16, 128, 48, 128).transpose(2, 1, 0, 3)).reshape(48, 128, -1)
    uwt_s = np.ascontiguousarray(
        uw.reshape(16, 128, 48, 128).transpose(2, 1, 0, 3)).reshape(48, 128, -1)
    dwt_s = np.ascontiguousarray(
        dw.reshape(48, 128, 16, 128).transpose(2, 1, 0, 3)).reshape(16, 128, -1)
    gsc_s = np.ascontiguousarray(inputs["L0_ffn_gate_scale"].reshape(48, 128).T)
    usc_s = np.ascontiguousarray(inputs["L0_ffn_up_scale"].reshape(48, 128).T)
    dsc_s = np.ascontiguousarray(inputs["L0_ffn_down_scale"].reshape(16, 128).T)
    sidx = np.arange(128)
    mkK_s = ((sidx[:, None] // CH) < (sidx[None, :] // CH)).astype(np.float32)
    mkG_s = (sidx[:, None] <= sidx[None, :]).astype(np.float32)
    mkKL_s = np.ascontiguousarray(mkK_s[0:64]); mkKH_s = np.ascontiguousarray(mkK_s[64:128])
    mkGL_s = np.ascontiguousarray(mkG_s[0:64]); mkGH_s = np.ascontiguousarray(mkG_s[64:128])

    per_core = []
    for c in range(NCORE):
        h0, h1 = 2 * c, 2 * c + 1
        cols = np.zeros((D, 1024), np.float32)
        cols[:, 0:128] = Wq[:, h0 * 128:(h0 + 1) * 128]
        cols[:, 128:256] = Wq[:, h1 * 128:(h1 + 1) * 128]
        cols[:, 256:384] = Wk[:, h0 * 128:(h0 + 1) * 128]
        cols[:, 384:512] = Wk[:, h1 * 128:(h1 + 1) * 128]
        cols[:, 512:640] = Wv[:, h0 * 128:(h0 + 1) * 128]
        cols[:, 640:768] = Wv[:, h1 * 128:(h1 + 1) * 128]
        cols[:, 768:896] = Wg1
        cols[:, 896] = 1.0
        cols[:, 896 + 32] = Wbeta[:, h0]
        cols[:, 896 + 64] = Wbeta[:, h1]
        wf_s = np.ascontiguousarray(
            cols.reshape(16, 128, 8, 128).transpose(1, 0, 2, 3)).reshape(128, -1)
        wg2_s = np.ascontiguousarray(np.concatenate(
            [Wg2_[:, h0 * 128:(h0 + 1) * 128], Wg2_[:, h1 * 128:(h1 + 1) * 128]],
            axis=1))
        lnw_cs = np.ascontiguousarray(
            np.stack([lnw[h0], lnw[h1]], axis=1).astype(np.float32))
        lnb_cs = np.ascontiguousarray(
            np.stack([lnb[h0], lnb[h1]], axis=1).astype(np.float32))
        nw_cs = np.ascontiguousarray(np.stack(
            [nw[h0 * 128:(h0 + 1) * 128], nw[h1 * 128:(h1 + 1) * 128]],
            axis=1).astype(np.float32))
        xres_s = np.ascontiguousarray(xT_s[:, :, c * Tc:(c + 1) * Tc])

        per_core.append({
            "xT": xT_s, "wf": wf_s, "wo": wo_s, "wg2": wg2_s,
            "lnw_c": lnw_cs, "lnb_c": lnb_cs, "nw_c": nw_cs,
            "gwt": gwt_s, "uwt": uwt_s, "dwt": dwt_s,
            "gsc": gsc_s, "usc": usc_s, "dsc": dsc_s,
            "idn": np.eye(128, dtype=BF),
            "mkKL": mkKL_s, "mkKH": mkKH_s, "mkGL": mkGL_s, "mkGH": mkGH_s,
            "onesc": np.ones((128, 1), np.float32),
            "onesf": np.ones((128, 128), np.float32),
            "onesrow": np.ones((1, 128), np.float32),
            "xres": xres_s,
        })
    return per_core


def kernel(**inputs):
    T = inputs["L0_attn_x"].shape[1]
    nc = build_program(T)
    in_maps = stage_inputs(inputs, T)
    res = run_bass_kernel_spmd(nc, in_maps, core_ids=list(range(NCORE)))
    outs = [np.asarray(res.results[c]["out"]) for c in range(NCORE)]
    return np.concatenate(outs, axis=0)[None].astype(np.float32)


# version 06
# speedup vs baseline: 1.6421x; 1.0339x over previous
import numpy as np
import ml_dtypes
import concourse.bass as bass
import concourse.mybir as mybir
from concourse.bass_utils import run_bass_kernel_spmd
from concourse.aot_env import allow_ici_all_to_all

F32 = mybir.dt.float32
BF16 = mybir.dt.bfloat16
F8 = mybir.dt.float8e4
AF = mybir.ActivationFunctionType
ALU = mybir.AluOpType
BF = ml_dtypes.bfloat16

D = 2048
DH = 128
CH = 16
GT = 128
NCORE = 8
EPS_A = 1e-6
EPS_M = 1e-5

ENGS = ("sync", "gpsimd", "vector", "scalar", "tensor")


COMPUTE_ENGS = ("vector", "scalar", "tensor")


class Sched:

    def __init__(self):
        self.ops = {e: [] for e in ENGS}
        self.cnt = {e: 0 for e in ENGS}
        self.step = {"sync": 16, "gpsimd": 16, "vector": 1, "scalar": 1,
                     "tensor": 1}

    def op(self, eng, fn, inc=False, waits=()):
        self.ops[eng].append((fn, bool(inc), tuple(waits)))
        if eng in COMPUTE_ENGS or inc:
            self.cnt[eng] += self.step[eng]
        return self.cnt[eng]

    def now(self, eng):
        return self.cnt[eng]

    def barrier(self, eng, others):
        w = tuple((o, self.cnt[o]) for o in others if self.cnt[o] > 0)
        if w:
            self.ops[eng].append((None, False, w))

    def emit(self, block, sems):
        import os as _os
        import bisect
        ssv = _os.environ.get("BASS_SELFSYNC", "vector")
        full = ssv == "full"
        full_engs = set(ssv.split(",")) if ssv not in ("min", "full") else (
            set(COMPUTE_ENGS) if full else set())
        waited = {e: set() for e in ENGS}
        for e in list(self.ops):
            for fn, inc, waits in self.ops[e]:
                for we, wv in waits:
                    if we != e and wv > 0 and we in waited:
                        waited[we].add(wv)
        flagged = {e: sorted(waited[e]) for e in COMPUTE_ENGS}

        class _TProxy:
            def __init__(self, h):
                self._h = h

            def __getattr__(self, k):
                return getattr(self._h, k)

            def matmul(self, *a, **kw):
                kw.setdefault("skip_group_check", True)
                return self._h.matmul(*a, **kw)

        def rv(we, wv):
            if we not in COMPUTE_ENGS or we in full_engs:
                return wv
            import bisect as _b
            return _b.bisect_right(flagged[we], wv)

        def mk(eng):
            selfsync = eng in COMPUTE_ENGS
            efull = eng in full_engs
            miles = sorted(waited[eng]) if not selfsync else []
            fset = set(flagged.get(eng, ()))

            def body(h):
                n = 0
                enforced = 0
                mi = 0
                for fn, inc, waits in self.ops[eng]:
                    for we, wv in waits:
                        wv2 = rv(we, wv)
                        if wv2 > 0:
                            h.wait_ge(sems[we], wv2)
                    if fn is not None:
                        if eng == "tensor":
                            h2 = _TProxy(h)
                        else:
                            h2 = h
                        if selfsync:
                            if efull and n > 0:
                                h.wait_ge(sems[eng], n)
                        elif inc:
                            tgt = 0
                            while mi < len(miles) and miles[mi] <= n:
                                tgt = miles[mi]
                                mi += 1
                            if tgt > enforced:
                                h.wait_ge(sems[eng], tgt)
                                enforced = tgt
                        ins = fn(h2)
                        if selfsync:
                            n += 1
                            if efull:
                                ins.then_inc(sems[eng], 1)
                            elif n in fset:
                                ins.then_inc(sems[eng], 1)
                        elif inc:
                            ins.then_inc(sems[eng], self.step[eng])
                            n += self.step[eng]
            return body
        block.sync(mk("sync"))
        block.gpsimd(mk("gpsimd"))
        block.vector(mk("vector"))
        block.scalar(mk("scalar"))
        block.tensor(mk("tensor"))



def blk2(ap0, c0, w, blk):
    a = ap0[:, c0:c0 + w]
    return bass.AP(tensor=a.tensor, offset=a.offset,
                   ap=[list(a.ap[0]), [blk, 2], [1, w]])

def build_program(T):
    NG = T // GT
    Tc = T // NCORE
    NLG = Tc // GT
    NCH = GT // CH
    TS = min(256, Tc)
    NSL = Tc // TS
    NDT = min(512, Tc)
    ND = Tc // NDT
    assert T % (GT * NCORE) == 0

    import os as _os
    nc = bass.Bass("TRN2", target_bir_lowering=False,
                   detect_race_conditions=not _os.environ.get("KERN_NORACE"))
    s = Sched()

    def din(name, shape, dt):
        return nc.dram_tensor(name, shape, dt, kind="ExternalInput").ap()

    xT = din("xT", [128, 16, T], F32)
    wf = din("wf", [128, 16 * 8 * 128], F32)
    wo = din("wo", [16, 128, 16 * 128], F32)
    wg2 = din("wg2", [128, 256], F32)
    lnw_c = din("lnw_c", [128, 2], F32)
    lnb_c = din("lnb_c", [128, 2], F32)
    nw_c = din("nw_c", [128, 2], F32)
    gwt = din("gwt", [48, 128, 16 * 128], F32)
    uwt = din("uwt", [48, 128, 16 * 128], F32)
    dwt = din("dwt", [16, 128, 48 * 128], F32)
    gsc = din("gsc", [128, 48], F32)
    usc = din("usc", [128, 48], F32)
    dsc = din("dsc", [128, 16], F32)
    idn = din("idn", [128, 128], BF16)
    mkKL = din("mkKL", [64, 128], F32)
    mkKH = din("mkKH", [64, 128], F32)
    mkGL = din("mkGL", [64, 128], F32)
    mkGH = din("mkGH", [64, 128], F32)
    onesc = din("onesc", [128, 1], F32)
    onesf = din("onesf", [128, 128], F32)
    onesrow = din("onesrow", [1, 128], F32)
    xres = din("xres", [128, 16, Tc], F32)

    out = nc.dram_tensor("out", [Tc, D], F32, kind="ExternalOutput").ap()
    o_stage = nc.dram_tensor("o_stage", [NCORE * NLG * 2 * 128, GT], BF16)
    o_gath = nc.dram_tensor("o_gath", [NCORE * NLG * 2 * 128, GT], BF16)
    x1_dram = nc.dram_tensor("x1_dram", [128, 16, Tc], BF16)
    o_st5 = o_stage.ap().rearrange("(a l hh p) c -> a l hh p c",
                                   a=NCORE, l=NLG, hh=2, p=128)
    o_gt5 = o_gath.ap().rearrange("(a l hh p) c -> a l hh p c",
                                  a=NCORE, l=NLG, hh=2, p=128)

    import contextlib
    es = contextlib.ExitStack()
    with es:
        def sb(name, shape, dt):
            return es.enter_context(nc.sbuf_tensor(name, shape, dt))

        pbank = [es.enter_context(nc.psum_tensor("pb%d" % i, [128, 512], F32))
                 for i in range(8)]
        pqA, pqB, pA, pB, pS, pSC, pQM, pSM = pbank
        pSMb = pSM.ap().bitcast(BF16)

        sems = {e: es.enter_context(nc.semaphore("sem_" + e)) for e in ENGS}
        sems["cc"] = es.enter_context(nc.semaphore("sem_cc"))
        block = es.enter_context(nc.Block())

        c_id = sb("c_id", [128, 128], BF16)
        c_idf = sb("c_idf", [128, 128], F32)
        c_mkKL = sb("c_mkKL", [64, 128], F32)
        c_mkKH = sb("c_mkKH", [64, 128], F32)
        c_mkGL = sb("c_mkGL", [64, 128], F32)
        c_mkGH = sb("c_mkGH", [64, 128], F32)
        c_ones = sb("c_ones", [128, 128], F32)
        c_onesc = sb("c_onesc", [128, 1], F32)
        c_onesb = sb("c_onesb", [128, 1], BF16)
        c_onesrow = sb("c_onesrow", [1, 128], F32)
        c_lnw = sb("c_lnw", [128, 2], F32)
        c_lnb = sb("c_lnb", [128, 2], F32)
        c_nw = sb("c_nw", [128, 2], F32)
        c_w2 = sb("c_w2", [128, 2], F32)
        c_wb = sb("c_wb", [128, 2], F32)
        c_negw = sb("c_negw", [128, 2], F32)
        c_w2B = sb("c_w2B", [128, 32], F32)
        c_epsA = sb("c_epsA", [128, 1], F32)
        c_epsM = sb("c_epsM", [128, 1], F32)

        for dst, src in [(c_id, idn), (c_mkKL, mkKL), (c_mkKH, mkKH),
                         (c_mkGL, mkGL), (c_mkGH, mkGH), (c_ones, onesf),
                         (c_onesc, onesc), (c_onesrow, onesrow),
                         (c_lnw, lnw_c), (c_lnb, lnb_c), (c_nw, nw_c)]:
            s.op("sync", lambda h, d=dst, sr=src: h.dma_start(out=d[:], in_=sr),
                 inc=True)
        ld0 = s.now("sync")

        scan_done = {}
        qkv_evac_done = {}
        qkv_pe_done = {}
        for g in range(-4, 0):
            scan_done[g] = 0
            qkv_evac_done[g] = 0
            qkv_pe_done[g] = 0

        esA = contextlib.ExitStack()
        with esA:
            def sba(name, shape, dt):
                return esA.enter_context(nc.sbuf_tensor(name, shape, dt))

            wfb = sba("wfb", [128, 16 * 8 * 128], BF16)
            wg2b = sba("wg2b", [128, 256], BF16)
            csWb = sba("csWb", [1, 8 * 128], BF16)
            W = 2
            xb = sba("xb", [128, W * 2048], BF16)
            sqb = sba("sqb", [128, W * 2048], BF16)
            qs_f = sba("qs_f", [128, W * 256], F32)
            qs_b = sba("qs_b", [128, W * 256], BF16)
            kn_f = sba("kn_f", [128, W * 256], F32)
            kn_b = sba("kn_b", [128, W * 256], BF16)
            v_f = sba("v_f", [128, W * 256], F32)
            g1_b = sba("g1_b", [128, W * 128], BF16)
            eT = sba("eT", [128, W * 4], F32)
            rowbuf = sba("rowbuf", [128, W * 128], F32)
            s_op = sba("s_op", [128, 256], BF16)
            vnstkL = sba("vnstkL", [64, 256], BF16)
            vnstkH = sba("vnstkH", [64, 256], BF16)
            aktL = sba("aktL", [64, 256], BF16)
            aktH = sba("aktH", [64, 256], BF16)
            agtL = sba("agtL", [64, 256], BF16)
            agtH = sba("agtH", [64, 256], BF16)
            vnstg = sba("vnstg", [128, 64], BF16)
            wbm = sba("wbm", [128, 256], F32)
            ketL = sba("ketL", [64, 256], BF16)
            ketH = sba("ketH", [64, 256], BF16)
            kescr = sba("kescr", [128, 128], BF16)
            scrA = sba("scrA", [128, 128], F32)
            scr1 = sba("scr1", [128, 32], F32)
            scr2 = sba("scr2", [128, 32], F32)
            rstdc = sba("rstdc", [128, 32], F32)
            khh = sba("khh", [128, 32], F32)
            strow = sba("strow", [1, 6 * 128], F32)
            strowb = sba("strowb", [1, 128], BF16)
            rstdB = sba("rstdB", [128, 128], F32)
            qscr1 = sba("qscr1", [128, 128], F32)
            qscr2 = sba("qscr2", [128, 128], F32)
            oscr = sba("oscr", [128, 256], F32)
            oth = sba("oth", [128, 128], F32)
            osig = sba("osig", [128, 128], F32)
            oo = sba("oo", [128, 128], F32)
            oo2 = sba("oo2", [128, 128], F32)
            obuf = sba("obuf", [128, 256], BF16)

            s.op("gpsimd", lambda h: h.dma_start(out=wfb[:], in_=wf), inc=True)
            s.op("gpsimd", lambda h: h.dma_start(out=wg2b[:], in_=wg2), inc=True)
            ld1 = s.now("gpsimd")

            s.op("vector", lambda h: h.tensor_tensor(
                out=c_w2[:], in0=c_lnw[:], in1=c_lnw[:], op=ALU.mult),
                waits=[("sync", ld0)])
            s.op("vector", lambda h: h.tensor_tensor(
                out=c_wb[:], in0=c_lnw[:], in1=c_lnb[:], op=ALU.mult))
            s.op("vector", lambda h: h.tensor_scalar(
                out=c_negw[:], in0=c_lnw[:], scalar1=-1.0, scalar2=None,
                op0=ALU.mult))
            s.op("vector", lambda h: h.tensor_copy(out=c_idf[:], in_=c_id[:]))
            s.op("vector", lambda h: h.tensor_copy(out=c_onesb[:], in_=c_onesc[:]))
            for hh in range(2):
                s.op("vector", lambda h, hh=hh: h.tensor_scalar(
                    out=c_w2B[:, hh * 16:hh * 16 + 16], in0=c_ones[:, 0:16],
                    scalar1=c_w2[:, hh:hh + 1], scalar2=None, op0=ALU.mult))
            s.op("vector", lambda h: h.memset(s_op[:], 0.0))
            s.op("vector", lambda h: h.memset(rowbuf[:], 0.0))
            s.op("vector", lambda h: h.memset(c_epsA[:], EPS_A))
            s.op("vector", lambda h: h.memset(c_epsM[:], EPS_M), inc=True)
            pre0 = s.now("vector")

            for t in range(8):
                for d in range(16):
                    wv = [("gpsimd", ld1), ("vector", pre0)] \
                        if (t == 0 and d == 0) else ()
                    s.op("tensor", lambda h, t=t, d=d: h.matmul(
                        pQM[0:1, 0:128], c_onesb[:],
                        wfb[:, (d * 8 + t) * 128:(d * 8 + t) * 128 + 128],
                        start=(d == 0), stop=(d == 15)), waits=wv)
                pe_cs = s.now("tensor")
                s.op("vector", lambda h, t=t: h.tensor_scalar(
                    out=csWb[0:1, t * 128:t * 128 + 128], in0=pQM[0:1, 0:128],
                    scalar1=-1.0, scalar2=None, op0=ALU.mult),
                    inc=True, waits=[("tensor", pe_cs)])
                s.barrier("tensor", ["vector"])

            def qkv_group(g):
                sg = g % W
                xo = sg * 2048
                ao = sg * 256
                wv = [("tensor", qkv_pe_done[g - W])] if g >= W else ()
                s.op("gpsimd", lambda h, g=g, xo=xo: h.dma_start(
                    out=xb[:, xo:xo + 2048].rearrange("p (d c) -> p d c", d=16),
                    in_=xT[:, :, g * GT:(g + 1) * GT]), inc=True, waits=wv)
                ldx = s.now("gpsimd")
                s.op("scalar", lambda h, xo=xo: h.activation(
                    out=sqb[:, xo:xo + 2048], in_=xb[:, xo:xo + 2048],
                    func=AF.Square), inc=True, waits=[("gpsimd", ldx)])
                sqd = s.now("scalar")

                for t in range(8):
                    pq = pqA if t < 4 else pqB
                    for d in range(16):
                        wv = [("gpsimd", ldx), ("vector", qkv_evac_done[g - 1]),
                              ("vector", scan_done[g - W])] \
                            if (t == 0 and d == 0) else ()
                        s.op("tensor", lambda h, t=t, d=d, pq=pq, xo=xo: h.matmul(
                            pq[:, (t % 4) * 128:(t % 4) * 128 + 128],
                            wfb[:, (d * 8 + t) * 128:(d * 8 + t) * 128 + 128],
                            xb[:, xo + d * 128: xo + d * 128 + 128],
                            start=(t % 4 == 0 and d == 0), stop=(d == 15)),
                            waits=wv)
                for d in range(16):
                    wv = [("scalar", sqd), ("vector", scan_done[g - 1]),
                          ("scalar", s.now("scalar"))] if d == 0 else ()
                    s.op("tensor", lambda h, d=d, xo=xo: h.matmul(
                        pQM[0:1, 0:128], c_onesb[:],
                        sqb[:, xo + d * 128: xo + d * 128 + 128],
                        start=(d == 0), stop=(d == 15)), waits=wv)
                pe_mm = s.now("tensor")

                ext = pqB[:, 384:512]
                s.op("vector", lambda h, ext=ext: h.tensor_copy(
                    out=strow[0:1, 640:768], in_=ext[0:1, 0:128]),
                    waits=[("tensor", pe_mm)])
                s.op("vector", lambda h: h.tensor_tensor(
                    out=strow[0:1, 128:256], in0=strow[0:1, 640:768],
                    in1=strow[0:1, 640:768], op=ALU.mult))
                s.op("vector", lambda h: h.scalar_tensor_tensor(
                    out=strow[0:1, 128:256], in0=strow[0:1, 128:256],
                    scalar=-1.0 / D, in1=pQM[0:1, 0:128], op0=ALU.mult,
                    op1=ALU.add))
                s.op("vector", lambda h, ext=ext: h.tensor_scalar(
                    out=strowb[0:1, :], in0=ext[0:1, 0:128], scalar1=1.0 / D,
                    scalar2=None, op0=ALU.mult), inc=True)
                dve_r = s.now("vector")
                s.op("scalar", lambda h: h.activation(
                    out=strow[0:1, 256:384], in_=strow[0:1, 128:256], func=AF.Ln,
                    scale=1.0 / D, bias=c_epsM[0:1, :]), waits=[("vector", dve_r)])
                s.op("scalar", lambda h: h.activation(
                    out=strow[0:1, 0:128], in_=strow[0:1, 256:384], func=AF.Exp,
                    scale=-0.5), inc=True)
                act_r = s.now("scalar")

                for t in range(8):
                    pq = pqA if t < 4 else pqB
                    s.op("tensor", lambda h, t=t, pq=pq: h.matmul(
                        pq[:, (t % 4) * 128:(t % 4) * 128 + 128],
                        csWb[0:1, t * 128:t * 128 + 128], strowb[0:1, :],
                        start=False, stop=True),
                        waits=[("scalar", act_r)] if t == 0 else ())
                pe_fix = s.op("tensor", lambda h: h.matmul(
                    pQM[:, 128:256], c_onesrow[0:1, :], strow[0:1, 0:128],
                    start=True, stop=True), inc=True)
                qkv_pe_done[g] = pe_fix
                s.op("vector", lambda h: h.tensor_copy(
                    out=rstdB[:], in_=pQM[:, 128:256]), inc=True,
                    waits=[("tensor", pe_fix), ("vector", scan_done[g - W])])
                s.barrier("tensor", ["vector"])

                for hh in range(2):
                    tq = pqA[:, hh * 128:hh * 128 + 128]
                    qf = qs_f[:, ao + hh * 128: ao + hh * 128 + 128]
                    q_ev = s.op("vector", lambda h, tq=tq, qf=qf: h.tensor_tensor(
                        out=qf, in0=tq, in1=rstdB[:], op=ALU.mult), inc=True)
                    s.op("scalar", lambda h, qf=qf: h.activation(
                        out=oth[:], in_=qf, func=AF.Exp, scale=-1.0),
                        waits=[("vector", q_ev)])
                    s.op("scalar", lambda h: h.activation(
                        out=oth[:], in_=oth[:], func=AF.Ln, scale=1.0,
                        bias=c_onesc[:]))
                    a_t = s.op("scalar", lambda h: h.activation(
                        out=oth[:], in_=oth[:], func=AF.Exp, scale=-1.0),
                        inc=True)
                    s.op("vector", lambda h, qf=qf: h.tensor_tensor(
                        out=qf, in0=qf, in1=oth[:], op=ALU.mult),
                        waits=[("scalar", a_t)])
                    s.op("vector", lambda h, qf=qf, hh=hh, ao=ao: h.tensor_copy(
                        out=qs_b[:, ao + hh * 128: ao + hh * 128 + 128], in_=qf),
                        inc=True)
                    s.barrier("scalar", ["vector"])
                for hh in range(2):
                    tk = pqA[:, (2 + hh) * 128:(2 + hh) * 128 + 128]
                    kf = kn_f[:, ao + hh * 128: ao + hh * 128 + 128]
                    k_ev = s.op("vector", lambda h, tk=tk, kf=kf: h.tensor_tensor(
                        out=kf, in0=tk, in1=rstdB[:], op=ALU.mult), inc=True)
                    k_sq = s.op("scalar", lambda h, kf=kf: h.activation(
                        out=qscr1[:], in_=kf, func=AF.Square),
                        inc=True, waits=[("vector", k_ev)])
                    k_mm = s.op("tensor", lambda h: h.matmul(
                        pQM[0:1, 0:128], c_onesc[:], qscr1[:],
                        start=True, stop=True), inc=True, waits=[("scalar", k_sq)])
                    s.op("scalar", lambda h: h.activation(
                        out=strow[0:1, 384:512], in_=pQM[0:1, 0:128], func=AF.Ln,
                        scale=1.0, bias=c_epsA[0:1, :]), waits=[("tensor", k_mm)])
                    k_rn = s.op("scalar", lambda h: h.activation(
                        out=strow[0:1, 384:512], in_=strow[0:1, 384:512],
                        func=AF.Exp, scale=-0.5), inc=True)
                    k_bc = s.op("tensor", lambda h: h.matmul(
                        pQM[:, 128:256], c_onesrow[0:1, :], strow[0:1, 384:512],
                        start=True, stop=True), inc=True, waits=[("scalar", k_rn)])
                    s.op("vector", lambda h, kf=kf: h.tensor_tensor(
                        out=kf, in0=kf, in1=pQM[:, 128:256], op=ALU.mult),
                        waits=[("tensor", k_bc)])
                    s.op("vector", lambda h, kf=kf, hh=hh, ao=ao: h.tensor_copy(
                        out=kn_b[:, ao + hh * 128: ao + hh * 128 + 128], in_=kf),
                        inc=True)
                    s.barrier("tensor", ["vector"])
                    s.barrier("scalar", ["vector"])
                for hh in range(2):
                    tv = pqB[:, hh * 128:hh * 128 + 128]
                    s.op("vector", lambda h, tv=tv, hh=hh, ao=ao: h.tensor_tensor(
                        out=v_f[:, ao + hh * 128: ao + hh * 128 + 128],
                        in0=tv, in1=rstdB[:], op=ALU.mult))
                s.op("vector", lambda h: h.tensor_tensor(
                    out=qscr2[:], in0=pqB[:, 256:384], in1=rstdB[:], op=ALU.mult))
                s.op("vector", lambda h, sg=sg: h.tensor_copy(
                    out=g1_b[:, sg * 128:sg * 128 + 128], in_=qscr2[:]))
                s.op("vector", lambda h, sg=sg, ext=ext: h.tensor_tensor(
                    out=rowbuf[32:33, sg * 128:sg * 128 + 128], in0=ext[32:33, 0:128],
                    in1=rstdB[32:33, 0:128], op=ALU.mult))
                v_done = s.op("vector", lambda h, sg=sg, ext=ext: h.tensor_tensor(
                    out=rowbuf[64:65, sg * 128:sg * 128 + 128],
                    in0=ext[64:65, 0:128], in1=rstdB[64:65, 0:128], op=ALU.mult),
                    inc=True)
                for i, base in enumerate((32, 64)):
                    kw = [("vector", v_done)] if i == 0 else ()
                    rb = lambda h, base=base, sg=sg: rowbuf[
                        base:base + 1, sg * 128:sg * 128 + 128]
                    s.op("scalar", lambda h, base=base, sg=sg: h.activation(
                        out=rowbuf[base:base + 1, sg * 128:sg * 128 + 128],
                        in_=rowbuf[base:base + 1, sg * 128:sg * 128 + 128],
                        func=AF.Exp, scale=-1.0), waits=kw)
                    s.op("scalar", lambda h, base=base, sg=sg: h.activation(
                        out=rowbuf[base:base + 1, sg * 128:sg * 128 + 128],
                        in_=rowbuf[base:base + 1, sg * 128:sg * 128 + 128],
                        func=AF.Ln, scale=1.0, bias=c_onesc[0:1, :]))
                    s.op("scalar", lambda h, base=base, sg=sg: h.activation(
                        out=rowbuf[base:base + 1, sg * 128:sg * 128 + 128],
                        in_=rowbuf[base:base + 1, sg * 128:sg * 128 + 128],
                        func=AF.Exp, scale=-1.0), inc=(i == 1))
                e_th = s.now("scalar")
                for i, base in enumerate((32, 64)):
                    kw = [("scalar", e_th)] if i == 0 else ()
                    s.op("vector", lambda h, base=base, sg=sg: h.tensor_scalar(
                        out=rowbuf[base:base + 1, sg * 128:sg * 128 + 128],
                        in0=rowbuf[base:base + 1, sg * 128:sg * 128 + 128],
                        scalar1=-2.0, scalar2=None, op0=ALU.mult),
                        inc=(i == 1), waits=kw)
                e_neg = s.now("vector")
                i = 0
                for hhb, base in enumerate((32, 64)):
                    for half in range(2):
                        kw = [("vector", e_neg)] if i == 0 else ()
                        s.op("tensor", lambda h, base=base, sg=sg, i=i, half=half:
                             h.matmul(
                                 pQM[0:64, 256 + i:257 + i],
                                 rowbuf[base:base + 1,
                                        sg * 128 + half * 64:
                                        sg * 128 + half * 64 + 64],
                                 c_ones[base:base + 1, 0:1],
                                 start=True, stop=True),
                             inc=(i == 3), waits=kw)
                        i += 1
                e_tr = s.now("tensor")
                s.op("vector", lambda h, sg=sg: h.tensor_copy(
                    out=eT[0:64, sg * 4: sg * 4 + 4], in_=pQM[0:64, 256:260]),
                    inc=True, waits=[("tensor", e_tr)])
                s.barrier("tensor", ["vector"])
                qkv_evac_done[g] = s.now("vector")

            def scan_group(g):
                sg = g % W
                ao = sg * 256
                qev = qkv_evac_done[g]
                guard = qev

                if g >= 1:
                    for hh in range(2):
                        s.op("vector", lambda h, hh=hh: h.tensor_copy(
                            out=s_op[:, hh * 128:hh * 128 + 128],
                            in_=pS[:, hh * 128:hh * 128 + 128]),
                            inc=(hh == 1),
                            waits=[("vector", qev)] if hh == 0 else ())

                for hh in range(2):
                    kb = kn_b[:, ao + hh * 128: ao + hh * 128 + 128]
                    qb = qs_b[:, ao + hh * 128: ao + hh * 128 + 128]
                    for half, (aktX, agtX, mKX, mGX) in enumerate(
                            ((aktL, agtL, c_mkKL, c_mkGL),
                             (aktH, agtH, c_mkKH, c_mkGH))):
                        ec = eT[0:64, sg * 4 + hh * 2 + half:
                                sg * 4 + hh * 2 + half + 1]
                        mm1 = s.op("tensor", lambda h, kb=kb, half=half: h.matmul(
                            pSC[0:64, 0:128], kb[:, half * 64:half * 64 + 64],
                            kb, start=True, stop=True), inc=True,
                            waits=[("vector", qev)] if (hh == 0 and half == 0)
                            else ())
                        dv1 = s.op("vector", lambda h, hh=hh, ec=ec, aktX=aktX,
                                   mKX=mKX: h.scalar_tensor_tensor(
                                       out=aktX[:, hh * 128:hh * 128 + 128],
                                       in0=pSC[0:64, 0:128], scalar=ec,
                                       in1=mKX[:], op0=ALU.mult, op1=ALU.mult),
                                   inc=True, waits=[("tensor", mm1)])
                        mm2 = s.op("tensor", lambda h, kb=kb, qb=qb, half=half:
                                   h.matmul(
                                       pSC[0:64, 0:128],
                                       kb[:, half * 64:half * 64 + 64], qb,
                                       start=True, stop=True), inc=True,
                                   waits=[("vector", dv1)])
                        s.op("vector", lambda h, hh=hh, ec=ec, agtX=agtX,
                             mGX=mGX: h.scalar_tensor_tensor(
                                 out=agtX[:, hh * 128:hh * 128 + 128],
                                 in0=pSC[0:64, 0:128], scalar=ec, in1=mGX[:],
                                 op0=ALU.mult, op1=ALU.mult),
                             inc=True, waits=[("tensor", mm2)])
                        s.barrier("tensor", ["vector"])

                for hh in range(2):
                    s.op("vector", lambda h, hh=hh, ao=ao: h.tensor_tensor(
                        out=oscr[:, 0:128],
                        in0=v_f[:, ao + hh * 128:ao + hh * 128 + 128],
                        in1=kn_f[:, ao + hh * 128:ao + hh * 128 + 128],
                        op=ALU.subtract))
                    s.op("vector", lambda h, hh=hh: h.tensor_scalar(
                        out=wbm[:, hh * 128:hh * 128 + 128], in0=oscr[:, 0:128],
                        scalar1=c_negw[:, hh:hh + 1], scalar2=c_wb[:, hh:hh + 1],
                        op0=ALU.mult, op1=ALU.add), inc=True)
                    base = 32 + hh * 32
                    ebc = s.op("tensor", lambda h, sg=sg, base=base: h.matmul(
                        pQM[:, 128:256], c_ones[base:base + 1, 0:128],
                        rowbuf[base:base + 1, sg * 128:sg * 128 + 128],
                        start=True, stop=True), inc=True,
                        waits=[("vector", guard)] if hh == 0 else ())
                    kes = s.op("vector", lambda h, hh=hh, ao=ao: h.tensor_tensor(
                        out=kescr[:],
                        in0=kn_f[:, ao + hh * 128:ao + hh * 128 + 128],
                        in1=pQM[:, 128:256], op=ALU.mult), inc=True,
                        waits=[("tensor", ebc)])
                    ktr = 0
                    for half in range(2):
                        ktr = s.op("tensor", lambda h, half=half: h.transpose(
                            pSMb[0:64, half * 128:half * 128 + 128],
                            kescr[:, half * 64:half * 64 + 64], c_id[:]),
                            inc=True,
                            waits=[("vector", kes)] if half == 0 else ())
                    for half, ketX in enumerate((ketL, ketH)):
                        s.op("vector", lambda h, hh=hh, half=half, ketX=ketX:
                             h.tensor_copy(
                                 out=ketX[:, hh * 128:hh * 128 + 128],
                                 in_=pSMb[0:64, half * 128:half * 128 + 128]),
                             inc=(half == 1),
                             waits=[("tensor", ktr)] if half == 0 else ())
                    s.barrier("tensor", ["vector"])
                prep_done = s.now("vector")

                for hh in range(2):
                    s.op("tensor", lambda h, hh=hh, ao=ao: h.matmul(
                        pA[:, hh * 128:hh * 128 + 128],
                        s_op[:, hh * 128:hh * 128 + 128],
                        kn_b[:, ao + hh * 128:ao + hh * 128 + 128],
                        start=(hh == 0), stop=True),
                        waits=[("vector", prep_done)] if hh == 0 else ())
                    s.op("tensor", lambda h, hh=hh, ao=ao: h.matmul(
                        pB[:, hh * 128:hh * 128 + 128],
                        s_op[:, hh * 128:hh * 128 + 128],
                        qs_b[:, ao + hh * 128:ao + hh * 128 + 128],
                        start=(hh == 0), stop=True))
                _ = s.now("tensor")

                last_stk = s.now("vector")
                for j in range(NCH):
                    m = j // 2
                    c0 = j * CH
                    if j > 0:
                        KL = min(16 * j, 64)
                        KH = 16 * j - KL
                        for hh in range(2):
                            s.op("tensor", lambda h, hh=hh, KL=KL, c0=c0: h.matmul(
                                pA[:, hh * 128 + c0: hh * 128 + c0 + CH],
                                vnstkL[0:KL, hh * 128:hh * 128 + 128],
                                aktL[0:KL, hh * 128 + c0:hh * 128 + c0 + CH],
                                start=False, stop=True),
                                waits=[("vector", last_stk)] if hh == 0 else ())
                            if KH > 0:
                                s.op("tensor", lambda h, hh=hh, KH=KH, c0=c0:
                                     h.matmul(
                                         pA[:, hh * 128 + c0:
                                            hh * 128 + c0 + CH],
                                         vnstkH[0:KH, hh * 128:hh * 128 + 128],
                                         aktH[0:KH,
                                              hh * 128 + c0:hh * 128 + c0 + CH],
                                         start=False, stop=True))
                    corr_pe = s.now("tensor")

                    s.op("vector", lambda h, c0=c0: h.tensor_copy(
                        out=scrA[:, 0:32],
                        in_=blk2(pA, c0, CH, 128)),
                        waits=[("tensor", corr_pe)])
                    kh_ev = s.op("vector", lambda h: h.tensor_tensor(
                        out=scrA[:, 32:64], in0=scrA[:, 0:32], in1=scrA[:, 0:32],
                        op=ALU.mult), inc=True)
                    st1 = s.op("tensor", lambda h: h.matmul(
                        pSM[:, 128:192], c_ones[:], scrA[:, 0:64],
                        start=True, stop=True), inc=True, waits=[("vector", kh_ev)])
                    s.op("vector", lambda h: h.tensor_copy(
                        out=scr2[:], in_=pSM[:, 128:160]), waits=[("tensor", st1)])
                    s.op("vector", lambda h: h.tensor_tensor(
                        out=scr1[:], in0=scr2[:], in1=scr2[:], op=ALU.mult))
                    vpre = s.op("vector", lambda h: h.scalar_tensor_tensor(
                        out=scr1[:], in0=scr1[:], scalar=-1.0 / DH,
                        in1=pSM[:, 160:192], op0=ALU.mult, op1=ALU.add), inc=True)
                    s.op("scalar", lambda h: h.activation(
                        out=scr2[:], in_=scr1[:], func=AF.Ln, scale=1.0 / DH,
                        bias=c_epsA[:]), waits=[("vector", vpre)])
                    a_rs = s.op("scalar", lambda h: h.activation(
                        out=rstdc[:], in_=scr2[:], func=AF.Exp, scale=-0.5),
                        inc=True)
                    s.op("vector", lambda h: h.scalar_tensor_tensor(
                        out=khh[:], in0=pSM[:, 128:160], scalar=-1.0 / DH,
                        in1=scrA[:, 0:32], op0=ALU.mult, op1=ALU.add),
                        waits=[("scalar", a_rs)])
                    s.op("vector", lambda h: h.tensor_tensor(
                        out=khh[:], in0=khh[:], in1=rstdc[:], op=ALU.mult))
                    s.op("vector", lambda h: h.tensor_tensor(
                        out=scrA[:, 64:96], in0=khh[:], in1=c_w2B[:], op=ALU.mult))
                    s.op("vector", lambda h, c0=c0: h.tensor_tensor(
                        out=scrA[:, 64:96], in0=scrA[:, 64:96],
                        in1=blk2(wbm, c0, CH, 128), op=ALU.add))
                    gw_d = s.op("vector", lambda h: h.tensor_tensor(
                        out=scrA[:, 96:128], in0=scrA[:, 64:96], in1=khh[:],
                        op=ALU.mult), inc=True)
                    st2 = s.op("tensor", lambda h: h.matmul(
                        pSM[:, 192:256], c_ones[:], scrA[:, 64:128],
                        start=True, stop=True), inc=True, waits=[("vector", gw_d)])
                    s.op("vector", lambda h: h.scalar_tensor_tensor(
                        out=scr1[:], in0=khh[:], scalar=1.0 / DH,
                        in1=pSM[:, 224:256], op0=ALU.mult, op1=ALU.mult),
                        waits=[("tensor", st2)])
                    s.op("vector", lambda h: h.tensor_tensor(
                        out=scr1[:], in0=scrA[:, 64:96], in1=scr1[:],
                        op=ALU.subtract))
                    s.op("vector", lambda h: h.scalar_tensor_tensor(
                        out=scr1[:], in0=pSM[:, 192:224], scalar=-1.0 / DH,
                        in1=scr1[:], op0=ALU.mult, op1=ALU.add))
                    s.op("vector", lambda h: h.tensor_tensor(
                        out=scr2[:], in0=scr1[:], in1=rstdc[:], op=ALU.mult))
                    po = (j % 2) * 16
                    s.op("vector", lambda h, po=po: h.tensor_copy(
                        out=vnstg[:, po:po + 16], in_=scr2[:, 0:16]))
                    vn_d = s.op("vector", lambda h, po=po: h.tensor_copy(
                        out=vnstg[:, 32 + po:32 + po + 16], in_=scr2[:, 16:32]),
                        inc=True)

                    vstk = vnstkL if m < 2 else vnstkH
                    r0 = 32 * (m % 2)
                    nr = 16 if j % 2 == 0 else 32
                    src0 = 0 if j % 2 == 0 else 0
                    trs = 0
                    for hh in range(2):
                        trs = s.op("tensor", lambda h, hh=hh, r0=r0, nr=nr:
                                   h.transpose(
                                       pSMb[r0:r0 + nr,
                                            hh * 128:hh * 128 + 128],
                                       vnstg[:, hh * 32:hh * 32 + nr], c_id[:]),
                                   inc=True,
                                   waits=[("vector", vn_d)] if hh == 0 else ())
                    s.op("vector", lambda h, r0=r0, nr=nr, vstk=vstk:
                         h.tensor_copy(
                             out=vstk[r0:r0 + nr, 0:256],
                             in_=pSMb[r0:r0 + nr, 0:256]),
                         inc=True, waits=[("tensor", trs)])
                    last_stk = s.now("vector")
                    s.barrier("tensor", ["vector"])

                for hh in range(2):
                    s.op("tensor", lambda h, hh=hh: h.matmul(
                        pB[:, hh * 128:hh * 128 + 128],
                        vnstkL[:, hh * 128:hh * 128 + 128],
                        agtL[:, hh * 128:hh * 128 + 128], start=False, stop=False),
                        waits=[("vector", last_stk)] if hh == 0 else ())
                    s.op("tensor", lambda h, hh=hh: h.matmul(
                        pB[:, hh * 128:hh * 128 + 128],
                        vnstkH[:, hh * 128:hh * 128 + 128],
                        agtH[:, hh * 128:hh * 128 + 128], start=False, stop=True))
                    s.op("tensor", lambda h, hh=hh, g=g: h.matmul(
                        pS[:, hh * 128:hh * 128 + 128],
                        ketL[:, hh * 128:hh * 128 + 128],
                        vnstkL[:, hh * 128:hh * 128 + 128],
                        start=(g == 0 and hh == 0), stop=False))
                    s.op("tensor", lambda h, hh=hh, g=g: h.matmul(
                        pS[:, hh * 128:hh * 128 + 128],
                        ketH[:, hh * 128:hh * 128 + 128],
                        vnstkH[:, hh * 128:hh * 128 + 128],
                        start=False, stop=True))
                qh_pe = s.now("tensor")

                for hh in range(2):
                    qf = qs_f[:, ao + hh * 128: ao + hh * 128 + 128]
                    s.op("vector", lambda h, hh=hh: h.tensor_copy(
                        out=oscr[:, 0:128], in_=pB[:, hh * 128:hh * 128 + 128]),
                        waits=[("tensor", qh_pe)] if hh == 0 else ())
                    o_ev = s.op("vector", lambda h: h.tensor_tensor(
                        out=oscr[:, 128:256], in0=oscr[:, 0:128],
                        in1=oscr[:, 0:128], op=ALU.mult), inc=True)
                    o_mm = s.op("tensor", lambda h: h.matmul(
                        pSM[:, 256:512], c_ones[:], oscr[:, 0:256],
                        start=True, stop=True), inc=True, waits=[("vector", o_ev)])
                    s.op("vector", lambda h: h.tensor_copy(
                        out=oo[:], in_=pSM[:, 256:384]), waits=[("tensor", o_mm)])
                    s.op("vector", lambda h: h.tensor_tensor(
                        out=oo[:], in0=oo[:], in1=oo[:], op=ALU.mult))
                    o_v = s.op("vector", lambda h: h.scalar_tensor_tensor(
                        out=oo[:], in0=oo[:], scalar=-1.0 / DH,
                        in1=pSM[:, 384:512], op0=ALU.mult, op1=ALU.add), inc=True)
                    s.op("scalar", lambda h: h.activation(
                        out=oo2[:], in_=oo[:], func=AF.Ln, scale=1.0 / DH,
                        bias=c_epsA[:]), waits=[("vector", o_v)])
                    o_rs = s.op("scalar", lambda h: h.activation(
                        out=oo2[:], in_=oo2[:], func=AF.Exp, scale=-0.5), inc=True)
                    s.op("vector", lambda h: h.scalar_tensor_tensor(
                        out=oo[:], in0=pSM[:, 256:384], scalar=-1.0 / DH,
                        in1=oscr[:, 0:128], op0=ALU.mult, op1=ALU.add),
                        waits=[("scalar", o_rs)])
                    s.op("vector", lambda h: h.tensor_tensor(
                        out=oo[:], in0=oo[:], in1=oo2[:], op=ALU.mult))
                    s.op("vector", lambda h, hh=hh: h.tensor_scalar(
                        out=oo[:], in0=oo[:], scalar1=c_lnw[:, hh:hh + 1],
                        scalar2=c_lnb[:, hh:hh + 1], op0=ALU.mult, op1=ALU.add))
                    o_d = s.op("vector", lambda h, qf=qf: h.tensor_tensor(
                        out=oo[:], in0=oo[:], in1=qf, op=ALU.add), inc=True)
                    g_mm = s.op("tensor", lambda h, hh=hh, sg=sg: h.matmul(
                        pQM[:, 384:512], wg2b[:, hh * 128:hh * 128 + 128],
                        g1_b[:, sg * 128:sg * 128 + 128], start=True, stop=True),
                        inc=True, waits=[("vector", o_d), ("vector", guard)])
                    s.op("scalar", lambda h: h.activation(
                        out=oth[:], in_=pQM[:, 384:512], func=AF.Exp, scale=-1.0),
                        waits=[("tensor", g_mm)])
                    s.op("scalar", lambda h: h.activation(
                        out=oth[:], in_=oth[:], func=AF.Ln, scale=1.0,
                        bias=c_onesc[:]))
                    g_th = s.op("scalar", lambda h: h.activation(
                        out=oth[:], in_=oth[:], func=AF.Exp, scale=-1.0), inc=True)
                    og_d = s.op("vector", lambda h: h.tensor_tensor(
                        out=oo[:], in0=oo[:], in1=oth[:], op=ALU.mult), inc=True,
                        waits=[("scalar", g_th)])
                    og_sq = s.op("scalar", lambda h: h.activation(
                        out=oo2[:], in_=oo[:], func=AF.Square), inc=True,
                        waits=[("vector", og_d)])
                    r_mm = s.op("tensor", lambda h: h.matmul(
                        pQM[0:1, 128:256], c_onesc[:], oo2[:],
                        start=True, stop=True), inc=True,
                        waits=[("scalar", og_sq)])
                    s.op("scalar", lambda h: h.activation(
                        out=strow[0:1, 512:640], in_=pQM[0:1, 128:256], func=AF.Ln,
                        scale=1.0 / DH, bias=c_epsA[0:1, :]),
                        waits=[("tensor", r_mm)])
                    r_rs = s.op("scalar", lambda h: h.activation(
                        out=strow[0:1, 512:640], in_=strow[0:1, 512:640],
                        func=AF.Exp, scale=-0.5), inc=True)
                    r_bc = s.op("tensor", lambda h: h.matmul(
                        pQM[:, 128:256], c_onesrow[0:1, :], strow[0:1, 512:640],
                        start=True, stop=True), inc=True, waits=[("scalar", r_rs)])
                    s.op("vector", lambda h: h.tensor_tensor(
                        out=oo[:], in0=oo[:], in1=pQM[:, 128:256], op=ALU.mult),
                        waits=[("tensor", r_bc)])
                    ob_d = s.op("vector", lambda h, hh=hh: h.tensor_scalar(
                        out=obuf[:, hh * 128:hh * 128 + 128], in0=oo[:],
                        scalar1=c_nw[:, hh:hh + 1], scalar2=None, op0=ALU.mult),
                        inc=True)
                    s.barrier("tensor", ["vector"])
                    s.barrier("scalar", ["vector"])
                    s.op("sync", lambda h, g=g, hh=hh: h.dma_start(
                        out=o_st5[g // NLG, g % NLG, hh, :, :],
                        in_=obuf[:, hh * 128:hh * 128 + 128]), inc=True,
                        waits=[("vector", ob_d)])
                scan_done[g] = s.now("vector")
                s.barrier("vector", ["sync"])

            for g in range(NG + 1):
                if g >= 1:
                    scan_group(g - 1)
                if g < NG:
                    qkv_group(g)

            s.barrier("gpsimd", ["sync", "vector", "scalar", "tensor"])

        import os
        if os.environ.get("KERN_NO_CC"):
            def do_cc(h):
                return h.dma_start(out=o_gath.ap(), in_=o_stage.ap()).then_inc(
                    sems["cc"], 16)
        else:
            def do_cc(h):
                with allow_ici_all_to_all():
                    return h.collective_compute(
                        "AllToAll", ALU.bypass,
                        replica_groups=[list(range(NCORE))],
                        ins=[o_stage.ap().opt()],
                        outs=[o_gath.ap().opt()]).then_inc(sems["cc"], 1)
        s.ops["gpsimd"].append((do_cc, False, ()))
        s.barrier("sync", ["gpsimd"])
        import os as _os
        _ccv = 16 if _os.environ.get("KERN_NO_CC") else 1
        s.ops["sync"].append((None, False, (("cc", _ccv),)))
        s.ops["tensor"].append((None, False, (("cc", _ccv),)))

        zb = sb("zb", [128, 256], BF16)
        zsq = sb("zsq", [128, 256], BF16)
        strow2 = sb("strow2", [1, 8 * 256], F32)
        rB1 = sb("rB1", [128, 256], F32)
        csc1 = sb("csc1", [128, 256], F32)
        csc2 = sb("csc2", [128, 256], F32)
        x1t = sb("x1t", [128, 256], BF16)
        gscb = sb("gscb", [128, 48], F32)
        uscb = sb("uscb", [128, 48], F32)
        dscb = sb("dscb", [128, 16], F32)
        ngs = sb("ngs", [128, 48], F32)
        esX = contextlib.ExitStack()
        x2b = esX.enter_context(nc.sbuf_tensor("x2b", [128, 16 * Tc], BF16, side="right"))
        fuse = sb("fuse", [128, 48 * Tc], BF16)

        s.op("sync", lambda h: h.dma_start(out=gscb[:], in_=gsc), inc=True)
        s.op("sync", lambda h: h.dma_start(out=uscb[:], in_=usc), inc=True)
        s.op("sync", lambda h: h.dma_start(out=dscb[:], in_=dsc), inc=True)
        ldsc = s.now("sync")
        hgs_d = s.op("vector", lambda h: h.tensor_scalar(
            out=ngs[:], in0=gscb[:], scalar1=-1.0, scalar2=None, op0=ALU.mult),
            inc=True, waits=[("sync", ldsc)])

        def ln_stats_and_bcast(src_psA, src_psB, dd):
            s.op("vector", lambda h: h.tensor_copy(
                out=strow2[0:1, 3 * TS:4 * TS], in_=src_psA[0:1, 0:TS]), waits=dd)
            v1 = s.op("vector", lambda h: h.tensor_tensor(
                out=strow2[0:1, 0:TS], in0=strow2[0:1, 3 * TS:4 * TS],
                in1=strow2[0:1, 3 * TS:4 * TS], op=ALU.mult), inc=True)
            v2 = s.op("vector", lambda h: h.scalar_tensor_tensor(
                out=strow2[0:1, 0:TS], in0=strow2[0:1, 0:TS], scalar=-1.0 / D,
                in1=src_psB[0:1, 0:TS], op0=ALU.mult, op1=ALU.add), inc=True)
            s.op("scalar", lambda h: h.activation(
                out=strow2[0:1, TS:2 * TS], in_=strow2[0:1, 0:TS], func=AF.Ln,
                scale=1.0 / D, bias=c_epsM[0:1, :]), waits=[("vector", v2)])
            a1 = s.op("scalar", lambda h: h.activation(
                out=strow2[0:1, TS:2 * TS], in_=strow2[0:1, TS:2 * TS],
                func=AF.Exp, scale=-0.5), inc=True)
            v3 = s.op("vector", lambda h: h.tensor_scalar(
                out=strow2[0:1, 2 * TS:3 * TS], in0=src_psA[0:1, 0:TS],
                scalar1=1.0 / D, scalar2=None, op0=ALU.mult), inc=True)
            s.op("tensor", lambda h: h.matmul(
                pbank[4][:, 0:TS], c_onesrow[0:1, :], strow2[0:1, TS:2 * TS],
                start=True, stop=True), waits=[("scalar", a1), ("vector", v3)])
            b1 = s.op("tensor", lambda h: h.matmul(
                pbank[5][:, 0:TS], c_onesrow[0:1, :], strow2[0:1, 2 * TS:3 * TS],
                start=True, stop=True), inc=True)
            s.op("vector", lambda h: h.tensor_copy(
                out=rB1[:, 0:TS], in_=pbank[4][:, 0:TS]), waits=[("tensor", b1)])
            vb = s.op("vector", lambda h: h.tensor_copy(
                out=csc1[:, 0:TS], in_=pbank[5][:, 0:TS]), inc=True)
            s.barrier("tensor", ["vector"])
            return vb

        esC1 = contextlib.ExitStack()
        with esC1:
            oslab = esC1.enter_context(nc.sbuf_tensor("oslab", [128, 16 * TS], BF16))
            xinb = esC1.enter_context(nc.sbuf_tensor("xinb", [128, 16 * TS], F32))
            zslab = esC1.enter_context(nc.sbuf_tensor("zslab", [128, 16 * TS], F32))
            wslab = esC1.enter_context(nc.sbuf_tensor("wslab", [128, 2 * 2048], BF16))

            for sl in range(NSL):
                t0 = sl * TS
                nlg4 = TS // GT
                if sl > 0:
                    s.barrier("sync", ["tensor", "vector"])
                for src in range(NCORE):
                    for hh in range(2):
                        dti = src * 2 + hh
                        s.op("sync", lambda h, src=src, hh=hh, dti=dti, sl=sl,
                             nlg4=nlg4: h.dma_start(
                                 out=oslab[:, dti * TS:dti * TS + TS].rearrange(
                                     "p (l c) -> p l c", l=nlg4),
                                 in_=o_gt5[src, sl * nlg4:(sl + 1) * nlg4,
                                           hh].rearrange("l p c -> p l c")),
                             inc=True)
                s.op("sync", lambda h, t0=t0: h.dma_start(
                    out=xinb[:].rearrange("p (d c) -> p d c", d=16),
                    in_=xres[:, :, t0:t0 + TS]), inc=True)
                lds = s.now("sync")

                pe_prev = {}
                wo_ld = {}
                for t in range(2):
                    if sl == 0 and t == 0:
                        wv = [("sync", ld0)]
                    elif t == 0:
                        wv = [("tensor", s.now("tensor"))]
                    else:
                        wv = ()
                    s.op("gpsimd", lambda h, t=t: h.dma_start(
                        out=wslab[:, (t % 2) * 2048:(t % 2) * 2048 + 2048],
                        in_=wo[t]), inc=True, waits=wv)
                    wo_ld[t] = s.now("gpsimd")
                bank_rd = {0: 0, 1: 0}
                for t in range(16):
                    wb_off = (t % 2) * 2048
                    pq = pbank[t % 2]
                    for d in range(16):
                        wv = [("gpsimd", wo_ld[t]), ("sync", lds),
                              ("vector", bank_rd[t % 2])] if d == 0 else ()
                        s.op("tensor", lambda h, d=d, pq=pq, wb_off=wb_off:
                             h.matmul(pq[:, 0:TS],
                                      wslab[:, wb_off + d * 128:wb_off + d * 128 + 128],
                                      oslab[:, d * TS:d * TS + TS],
                                      start=(d == 0), stop=(d == 15)), waits=wv)
                    pe_t = s.now("tensor")
                    pe_prev[t] = pe_t
                    if t + 2 < 16:
                        s.op("gpsimd", lambda h, t=t: h.dma_start(
                            out=wslab[:, (t % 2) * 2048:(t % 2) * 2048 + 2048],
                            in_=wo[t + 2]), inc=True, waits=[("tensor", pe_t)])
                        wo_ld[t + 2] = s.now("gpsimd")
                    s.op("vector", lambda h, t=t, pq=pq: h.tensor_copy(
                        out=zslab[:, t * TS:t * TS + TS], in_=pq[:, 0:TS]),
                        waits=[("tensor", pe_t)])
                    z_ev = s.op("vector", lambda h, t=t: h.tensor_copy(
                        out=zb[:, 0:TS], in_=zslab[:, t * TS:t * TS + TS]),
                        inc=True)
                    bank_rd[t % 2] = z_ev
                    z_sq = s.op("scalar", lambda h: h.activation(
                        out=zsq[:, 0:TS], in_=zb[:, 0:TS], func=AF.Square),
                        inc=True, waits=[("vector", z_ev)])
                    s.op("tensor", lambda h, t=t: h.matmul(
                        pbank[2][0:1, 0:TS], c_onesb[:], zb[:, 0:TS],
                        start=(t == 0), stop=(t == 15)), waits=[("vector", z_ev)])
                    s.op("tensor", lambda h, t=t: h.matmul(
                        pbank[3][0:1, 0:TS], c_onesb[:], zsq[:, 0:TS],
                        start=(t == 0), stop=(t == 15)), waits=[("scalar", z_sq)])
                    s.barrier("vector", ["tensor"])
                    s.barrier("scalar", ["tensor"])
                st_pe = s.now("tensor")

                ln_stats_and_bcast(pbank[2], pbank[3], [("tensor", st_pe)])

                for t in range(16):
                    s.op("vector", lambda h, t=t: h.tensor_tensor(
                        out=csc2[:, 0:TS], in0=zslab[:, t * TS:t * TS + TS],
                        in1=csc1[:, 0:TS], op=ALU.subtract))
                    s.op("vector", lambda h: h.tensor_tensor(
                        out=csc2[:, 0:TS], in0=csc2[:, 0:TS], in1=rB1[:, 0:TS],
                        op=ALU.mult))
                    s.op("vector", lambda h, t=t: h.tensor_tensor(
                        out=csc2[:, 0:TS], in0=csc2[:, 0:TS],
                        in1=xinb[:, t * TS:t * TS + TS], op=ALU.add))
                    x1d = s.op("vector", lambda h: h.tensor_copy(
                        out=x1t[:, 0:TS], in_=csc2[:, 0:TS]), inc=True)
                    s.op("sync", lambda h, t=t, t0=t0: h.dma_start(
                        out=x1_dram.ap()[:, t, t0:t0 + TS], in_=x1t[:, 0:TS]),
                        inc=True, waits=[("vector", x1d)])
                    s.barrier("vector", ["sync"])
                    x1sq = s.op("scalar", lambda h: h.activation(
                        out=zsq[:, 0:TS], in_=x1t[:, 0:TS], func=AF.Square),
                        inc=True, waits=[("vector", x1d)])
                    s.op("tensor", lambda h, t=t: h.matmul(
                        pbank[2][0:1, 0:TS], c_onesb[:], x1t[:, 0:TS],
                        start=(t == 0), stop=(t == 15)), waits=[("vector", x1d)])
                    s.op("tensor", lambda h, t=t: h.matmul(
                        pbank[3][0:1, 0:TS], c_onesb[:], zsq[:, 0:TS],
                        start=(t == 0), stop=(t == 15)), waits=[("scalar", x1sq)])
                    s.op("vector", lambda h, t=t: h.tensor_copy(
                        out=zslab[:, t * TS:t * TS + TS], in_=csc2[:, 0:TS]),
                        inc=True)
                    s.barrier("vector", ["tensor"])
                    s.barrier("scalar", ["vector"])
                st2_pe = s.now("tensor")

                ln_stats_and_bcast(pbank[2], pbank[3], [("tensor", st2_pe)])

                for t in range(16):
                    s.op("vector", lambda h, t=t: h.tensor_tensor(
                        out=csc2[:, 0:TS], in0=zslab[:, t * TS:t * TS + TS],
                        in1=csc1[:, 0:TS], op=ALU.subtract))
                    s.op("vector", lambda h, t=t, t0=t0: h.tensor_tensor(
                        out=x2b[:, t * Tc + t0: t * Tc + t0 + TS],
                        in0=csc2[:, 0:TS], in1=rB1[:, 0:TS], op=ALU.mult),
                        inc=(t == 15))
            x2_done = s.now("vector")

        esC2 = contextlib.ExitStack()
        with esC2:
            f8a = esC2.enter_context(nc.sbuf_tensor("f8a", [128, 2048], F8))
            f8b = esC2.enter_context(nc.sbuf_tensor("f8b", [128, 2048], F8))
            gslab = esC2.enter_context(nc.sbuf_tensor("gslab", [128, 2 * 2048], BF16))
            uslab = esC2.enter_context(nc.sbuf_tensor("uslab", [128, 2 * 2048], BF16))
            gth = esC2.enter_context(nc.sbuf_tensor("gth", [128, NDT], F32))
            gxh = esC2.enter_context(nc.sbuf_tensor("gxh", [128, NDT], F32))
            gv = esC2.enter_context(nc.sbuf_tensor("gv", [128, NDT], F32))
            uv = esC2.enter_context(nc.sbuf_tensor("uv", [128, NDT], F32))

            s.barrier("gpsimd", ["tensor", "vector", "scalar", "sync"])
            gu_pe_done = {}
            for ht in range(48):
                sl2 = ht % 2
                wv = [("tensor", gu_pe_done[ht - 2])] if ht >= 2 else ()
                s.op("gpsimd", lambda h, ht=ht: h.dma_start(
                    out=f8a[:], in_=gwt[ht]), inc=True, waits=wv)
                s.op("gpsimd", lambda h, ht=ht: h.dma_start(
                    out=f8b[:], in_=uwt[ht]), inc=True)
                w8 = s.now("gpsimd")
                s.op("gpsimd", lambda h, sl2=sl2: h.dma_start(
                    out=gslab[:, sl2 * 2048:sl2 * 2048 + 2048], in_=f8a[:]),
                    inc=True, waits=[("gpsimd", w8)])
                s.op("gpsimd", lambda h, sl2=sl2: h.dma_start(
                    out=uslab[:, sl2 * 2048:sl2 * 2048 + 2048], in_=f8b[:]),
                    inc=True)
                wbf = s.now("gpsimd")
                for nb in range(ND):
                    n0 = nb * NDT
                    for d in range(16):
                        wv = [("gpsimd", wbf), ("vector", x2_done),
                              ("vector", s.now("vector")),
                              ("scalar", s.now("scalar"))] if d == 0 else ()
                        s.op("tensor", lambda h, d=d, sl2=sl2, n0=n0: h.matmul(
                            pbank[0][:, 0:NDT],
                            gslab[:, sl2 * 2048 + d * 128:sl2 * 2048 + d * 128 + 128],
                            x2b[:, d * Tc + n0: d * Tc + n0 + NDT],
                            start=(d == 0), stop=(d == 15)), waits=wv)
                    for d in range(16):
                        s.op("tensor", lambda h, d=d, sl2=sl2, n0=n0: h.matmul(
                            pbank[1][:, 0:NDT],
                            uslab[:, sl2 * 2048 + d * 128:sl2 * 2048 + d * 128 + 128],
                            x2b[:, d * Tc + n0: d * Tc + n0 + NDT],
                            start=(d == 0), stop=(d == 15)))
                    gu_pe = s.now("tensor")
                    gu_pe_done[ht] = gu_pe
                    s.op("scalar", lambda h, ht=ht: h.activation(
                        out=gth[:], in_=pbank[0][:, 0:NDT], func=AF.Exp,
                        scale=ngs[:, ht:ht + 1]),
                        waits=[("tensor", gu_pe), ("vector", hgs_d)])
                    s.op("scalar", lambda h: h.activation(
                        out=gth[:], in_=gth[:], func=AF.Ln, scale=1.0,
                        bias=c_onesc[:]))
                    gt_d = s.op("scalar", lambda h: h.activation(
                        out=gth[:], in_=gth[:], func=AF.Exp, scale=-1.0),
                        inc=True)
                    s.op("vector", lambda h, ht=ht: h.tensor_scalar(
                        out=gxh[:], in0=pbank[0][:, 0:NDT],
                        scalar1=gscb[:, ht:ht + 1], scalar2=None, op0=ALU.mult),
                        waits=[("scalar", gt_d)])
                    s.op("vector", lambda h: h.tensor_tensor(
                        out=gv[:], in0=gth[:], in1=gxh[:], op=ALU.mult))
                    s.op("vector", lambda h, ht=ht: h.tensor_scalar(
                        out=uv[:], in0=pbank[1][:, 0:NDT],
                        scalar1=uscb[:, ht:ht + 1], scalar2=None, op0=ALU.mult))
                    s.op("vector", lambda h, ht=ht, n0=n0: h.tensor_tensor(
                        out=fuse[:, ht * Tc + n0: ht * Tc + n0 + NDT], in0=gv[:],
                        in1=uv[:], op=ALU.mult), inc=True)
                    s.barrier("tensor", ["vector"])
                    s.barrier("scalar", ["vector"])
            fuse_done = s.now("vector")
        esX.close()

        esC3 = contextlib.ExitStack()
        with esC3:
            dslab = esC3.enter_context(
                nc.sbuf_tensor("dslab", [128, 2 * 48 * 128], BF16))
            f8d = esC3.enter_context(nc.sbuf_tensor("f8d", [128, 48 * 128], F8))
            dnslab = esC3.enter_context(
                nc.sbuf_tensor("dnslab", [128, 16 * Tc], BF16))
            x1rb = esC3.enter_context(nc.sbuf_tensor("x1rb", [128, 16 * TS], BF16))
            outs = esC3.enter_context(nc.sbuf_tensor("outs", [128, 16 * TS], F32))

            s.barrier("gpsimd", ["tensor", "vector", "scalar", "sync"])
            dn_pe_done = {}
            for t in range(16):
                sl2 = t % 2
                wv = [("tensor", dn_pe_done[t - 2])] if t >= 2 else ()
                s.op("gpsimd", lambda h, t=t: h.dma_start(out=f8d[:], in_=dwt[t]),
                     inc=True, waits=wv)
                f8_d = s.now("gpsimd")
                s.op("gpsimd", lambda h, sl2=sl2: h.dma_start(
                    out=dslab[:, sl2 * 6144:sl2 * 6144 + 6144], in_=f8d[:]),
                    inc=True, waits=[("gpsimd", f8_d)])
                dw_d = s.now("gpsimd")
                for nb in range(ND):
                    n0 = nb * NDT
                    for dd in range(48):
                        wv = [("gpsimd", dw_d), ("vector", fuse_done),
                              ("vector", s.now("vector"))] \
                            if dd == 0 else ()
                        s.op("tensor", lambda h, dd=dd, sl2=sl2, n0=n0, nb=nb:
                             h.matmul(
                                 pbank[nb % 2][:, 0:NDT],
                                 dslab[:, sl2 * 6144 + dd * 128:
                                       sl2 * 6144 + dd * 128 + 128],
                                 fuse[:, dd * Tc + n0: dd * Tc + n0 + NDT],
                                 start=(dd == 0), stop=(dd == 47)), waits=wv)
                    dn_pe = s.now("tensor")
                    dn_pe_done[t] = dn_pe
                    s.op("vector", lambda h, t=t, n0=n0, nb=nb: h.tensor_scalar(
                        out=dnslab[:, t * Tc + n0: t * Tc + n0 + NDT],
                        in0=pbank[nb % 2][:, 0:NDT], scalar1=dscb[:, t:t + 1],
                        scalar2=None, op0=ALU.mult), inc=True,
                        waits=[("tensor", dn_pe), ("sync", ldsc)])
                    s.barrier("tensor", ["vector"])
            dn_done = s.now("vector")

            for nb in range(NSL):
                t0 = nb * TS
                if nb > 0:
                    s.barrier("sync", ["vector"])
                s.op("sync", lambda h, t0=t0: h.dma_start(
                    out=x1rb[:].rearrange("p (d c) -> p d c", d=16),
                    in_=x1_dram.ap()[:, :, t0:t0 + TS]), inc=True,
                    waits=[("vector", dn_done)] if nb == 0 else ())
                x1r_d = s.now("sync")
                for t in range(16):
                    z_ev = s.op("vector", lambda h, t=t, t0=t0: h.tensor_copy(
                        out=zb[:, 0:TS],
                        in_=dnslab[:, t * Tc + t0: t * Tc + t0 + TS]), inc=True)
                    z_sq = s.op("scalar", lambda h: h.activation(
                        out=zsq[:, 0:TS], in_=zb[:, 0:TS], func=AF.Square),
                        inc=True, waits=[("vector", z_ev)])
                    s.op("tensor", lambda h, t=t: h.matmul(
                        pbank[2][0:1, 0:TS], c_onesb[:], zb[:, 0:TS],
                        start=(t == 0), stop=(t == 15)), waits=[("vector", z_ev)])
                    s.op("tensor", lambda h, t=t: h.matmul(
                        pbank[3][0:1, 0:TS], c_onesb[:], zsq[:, 0:TS],
                        start=(t == 0), stop=(t == 15)), waits=[("scalar", z_sq)])
                    s.barrier("vector", ["tensor"])
                    s.barrier("scalar", ["tensor"])
                st3 = s.now("tensor")

                ln_stats_and_bcast(pbank[2], pbank[3], [("tensor", st3)])

                for t in range(16):
                    s.op("vector", lambda h, t=t, t0=t0: h.tensor_tensor(
                        out=csc2[:, 0:TS],
                        in0=dnslab[:, t * Tc + t0: t * Tc + t0 + TS],
                        in1=csc1[:, 0:TS], op=ALU.subtract))
                    s.op("vector", lambda h: h.tensor_tensor(
                        out=csc2[:, 0:TS], in0=csc2[:, 0:TS], in1=rB1[:, 0:TS],
                        op=ALU.mult))
                    ov = s.op("vector", lambda h, t=t: h.tensor_tensor(
                        out=csc2[:, 0:TS], in0=csc2[:, 0:TS],
                        in1=x1rb[:, t * TS:t * TS + TS], op=ALU.add), inc=True,
                        waits=[("sync", x1r_d)] if t == 0 else ())
                    for q in range(TS // 128):
                        trq = s.op("tensor", lambda h, q=q: h.transpose(
                            pbank[6][:, 0:128], csc2[:, q * 128:q * 128 + 128],
                            c_idf[:]), inc=True,
                            waits=[("vector", ov)] if q == 0 else ())
                        otd = s.op("vector", lambda h, t=t, q=q: h.tensor_copy(
                            out=outs[:, t * TS + q * 128:t * TS + q * 128 + 128],
                            in_=pbank[6][:, 0:128]), inc=True,
                            waits=[("tensor", trq)])
                        s.barrier("tensor", ["vector"])
                        s.op("sync", lambda h, t=t, q=q, t0=t0: h.dma_start(
                            out=out[t0 + q * 128: t0 + q * 128 + 128,
                                    t * 128:t * 128 + 128],
                            in_=outs[:, t * TS + q * 128:t * TS + q * 128 + 128]),
                            inc=True, waits=[("vector", otd)])
                s.barrier("vector", ["sync"])
            fin = s.now("sync")
            s.ops["sync"].append((None, False, (("sync", fin),)))

        s.emit(block, sems)
    return nc


def stage_inputs(inputs, T):
    Tc = T // NCORE
    x = np.ascontiguousarray(inputs["L0_attn_x"][0, :T].T.astype(np.float32))
    xT_s = np.ascontiguousarray(x.reshape(16, 128, T).transpose(1, 0, 2))

    Wq, Wk, Wv = (inputs["L0_attn_Wq"], inputs["L0_attn_Wk"], inputs["L0_attn_Wv"])
    Wbeta = inputs["L0_attn_Wbeta"]
    Wg1 = inputs["L0_attn_Wg1"]
    Wo_ = inputs["L0_attn_Wo"]
    Wg2_ = inputs["L0_attn_Wg2"]
    lnw, lnb = inputs["L0_attn_ln_w"], inputs["L0_attn_ln_b"]
    nw = inputs["L0_attn_norm_w"]
    gw = inputs["L0_ffn_gate_w"]
    uw = inputs["L0_ffn_up_w"]
    dw = inputs["L0_ffn_down_w"]

    wo_s = np.ascontiguousarray(
        Wo_.reshape(16, 128, 16, 128).transpose(2, 1, 0, 3)).reshape(16, 128, -1)
    gwt_s = np.ascontiguousarray(
        gw.reshape(16, 128, 48, 128).transpose(2, 1, 0, 3)).reshape(48, 128, -1)
    uwt_s = np.ascontiguousarray(
        uw.reshape(16, 128, 48, 128).transpose(2, 1, 0, 3)).reshape(48, 128, -1)
    dwt_s = np.ascontiguousarray(
        dw.reshape(48, 128, 16, 128).transpose(2, 1, 0, 3)).reshape(16, 128, -1)
    gsc_s = np.ascontiguousarray(inputs["L0_ffn_gate_scale"].reshape(48, 128).T)
    usc_s = np.ascontiguousarray(inputs["L0_ffn_up_scale"].reshape(48, 128).T)
    dsc_s = np.ascontiguousarray(inputs["L0_ffn_down_scale"].reshape(16, 128).T)
    sidx = np.arange(128)
    mkK_s = ((sidx[:, None] // CH) < (sidx[None, :] // CH)).astype(np.float32)
    mkG_s = (sidx[:, None] <= sidx[None, :]).astype(np.float32)
    mkKL_s = np.ascontiguousarray(mkK_s[0:64]); mkKH_s = np.ascontiguousarray(mkK_s[64:128])
    mkGL_s = np.ascontiguousarray(mkG_s[0:64]); mkGH_s = np.ascontiguousarray(mkG_s[64:128])

    per_core = []
    for c in range(NCORE):
        h0, h1 = 2 * c, 2 * c + 1
        cols = np.zeros((D, 1024), np.float32)
        cols[:, 0:128] = Wq[:, h0 * 128:(h0 + 1) * 128]
        cols[:, 128:256] = Wq[:, h1 * 128:(h1 + 1) * 128]
        cols[:, 256:384] = Wk[:, h0 * 128:(h0 + 1) * 128]
        cols[:, 384:512] = Wk[:, h1 * 128:(h1 + 1) * 128]
        cols[:, 512:640] = Wv[:, h0 * 128:(h0 + 1) * 128]
        cols[:, 640:768] = Wv[:, h1 * 128:(h1 + 1) * 128]
        cols[:, 768:896] = Wg1
        cols[:, 896] = 1.0
        cols[:, 896 + 32] = Wbeta[:, h0]
        cols[:, 896 + 64] = Wbeta[:, h1]
        wf_s = np.ascontiguousarray(
            cols.reshape(16, 128, 8, 128).transpose(1, 0, 2, 3)).reshape(128, -1)
        wg2_s = np.ascontiguousarray(np.concatenate(
            [Wg2_[:, h0 * 128:(h0 + 1) * 128], Wg2_[:, h1 * 128:(h1 + 1) * 128]],
            axis=1))
        lnw_cs = np.ascontiguousarray(
            np.stack([lnw[h0], lnw[h1]], axis=1).astype(np.float32))
        lnb_cs = np.ascontiguousarray(
            np.stack([lnb[h0], lnb[h1]], axis=1).astype(np.float32))
        nw_cs = np.ascontiguousarray(np.stack(
            [nw[h0 * 128:(h0 + 1) * 128], nw[h1 * 128:(h1 + 1) * 128]],
            axis=1).astype(np.float32))
        xres_s = np.ascontiguousarray(xT_s[:, :, c * Tc:(c + 1) * Tc])

        per_core.append({
            "xT": xT_s, "wf": wf_s, "wo": wo_s, "wg2": wg2_s,
            "lnw_c": lnw_cs, "lnb_c": lnb_cs, "nw_c": nw_cs,
            "gwt": gwt_s, "uwt": uwt_s, "dwt": dwt_s,
            "gsc": gsc_s, "usc": usc_s, "dsc": dsc_s,
            "idn": np.eye(128, dtype=BF),
            "mkKL": mkKL_s, "mkKH": mkKH_s, "mkGL": mkGL_s, "mkGH": mkGH_s,
            "onesc": np.ones((128, 1), np.float32),
            "onesf": np.ones((128, 128), np.float32),
            "onesrow": np.ones((1, 128), np.float32),
            "xres": xres_s,
        })
    return per_core


def kernel(**inputs):
    T = inputs["L0_attn_x"].shape[1]
    nc = build_program(T)
    in_maps = stage_inputs(inputs, T)
    res = run_bass_kernel_spmd(nc, in_maps, core_ids=list(range(NCORE)))
    outs = [np.asarray(res.results[c]["out"]) for c in range(NCORE)]
    return np.concatenate(outs, axis=0)[None].astype(np.float32)
